# version 10
# baseline (speedup 1.0000x reference)
import numpy as np
import ml_dtypes
import concourse.bass as bass
import concourse.mybir as mybir
from concourse.bass_utils import run_bass_kernel_spmd
from concourse.alu_op_type import AluOpType
from contextlib import ExitStack



PS = mybir.dt.float32
TT = 512


def kernel(**inputs):
    in_maps = stage_inputs(inputs)
    kb = build(nphase=99)
    res = run_bass_kernel_spmd(kb.nc, in_maps, core_ids=list(range(8)))
    return gather_output([r for r in res.results])




F32 = mybir.dt.float32
BF16 = mybir.dt.bfloat16
AF = mybir.ActivationFunctionType
AX = mybir.AxisListType.X

B = 2
T = 2048
D = 1024
NCH = 8
H = 16
DK = 64
CC = 128
NCK = T // CC
NIT = 15
TT = 512
NTT = T // TT
WIN = 256
NW = T // WIN
NL = 6
DH1 = 2816
NHC = 22

NEG = -3.0e38


def host_constants():
    c = {}
    c["identf"] = np.eye(128, dtype=np.float32)
    c["identb"] = np.eye(128, dtype=ml_dtypes.bfloat16)
    sig = np.arange(128)
    sig = (sig // 64) * 64 + ((sig % 64) + 32) % 64
    psw = np.zeros((128, 128), np.float32)
    psw[sig, np.arange(128)] = 1.0
    c["pswap"] = psw.astype(ml_dtypes.bfloat16)
    s = np.arange(CC)
    c["cgmask"] = np.where(s[:, None] <= s[None, :], 0.0, NEG).astype(np.float32)
    sel8 = np.zeros((NCH, 128, 16), np.float32)
    for ch in range(NCH):
        sel8[ch, 0:64, 2 * ch] = 1.0
        sel8[ch, 64:128, 2 * ch + 1] = 1.0
    c["sel8"] = sel8
    c["selT8"] = sel8.transpose(0, 2, 1).copy()
    bo2 = np.zeros((2, 128), np.float32)
    bo2[0, 0:64] = 1.0
    bo2[1, 64:128] = 1.0
    c["bo2p"] = bo2
    c["bo2n"] = -bo2
    c["blk2"] = bo2.T.copy()
    c["onescol"] = np.ones((128, 128), np.float32)
    zm = np.zeros((128, 2), np.float32)
    zm[0:64, 0] = 1.0
    zm[64:128, 1] = 1.0
    c["zm2"] = zm
    half = np.arange(0, DK, 2, dtype=np.float32)
    inv_freq = 1.0 / 10000.0 ** (half / DK)
    t = np.arange(T, dtype=np.float32)
    freqs = t[:, None] * inv_freq[None, :]
    freqs = np.concatenate([freqs, freqs], -1)
    base_scale = (half + 0.4 * DK) / (1.4 * DK)
    power = (t - T // 2) / (WIN // 2)
    scale = base_scale[None, :] ** power[:, None]
    scale = np.concatenate([scale, scale], -1)
    cos, sin = np.cos(freqs), np.sin(freqs)
    sgn = np.where(np.arange(DK) < 32, -1.0, 1.0)[None, :]
    QKS = 8.0
    csq = (QKS * cos * scale).T
    ssq = (QKS * sin * scale * sgn).T
    cik = (cos / scale).T
    sik = (sin / scale * sgn).T
    dup = lambda a: np.concatenate([a, a], 0).astype(np.float32)
    c["csq"], c["ssq"] = dup(csq), dup(ssq)
    c["cik"], c["sik"] = dup(cik), dup(sik)
    i = np.arange(128)
    j = np.arange(512)
    m = np.zeros((2, 128, 512), np.float32)
    for qt in range(2):
        lo = qt * 128 + i
        valid = (j[None, :] >= lo[:, None]) & (j[None, :] <= lo[:, None] + 256)
        m[qt] = np.where(valid, 0.0, NEG)
    c["wmask"] = m
    j2 = np.arange(256)
    m0 = np.zeros((2, 128, 256), np.float32)
    for qt in range(2):
        ig = qt * 128 + i
        valid = (j2[None, :] <= ig[:, None]) & (j2[None, :] >= ig[:, None] - 256)
        m0[qt] = np.where(valid, 0.0, NEG)
    c["wmask0"] = m0
    return c


def stage_inputs(full):
    c = host_constants()
    sig = np.arange(64)
    sig = (sig + 32) % 64
    w2full = full["L0_ffn_W2"]
    w2p = np.zeros((NL, NHC * 128, D), np.float32)
    w2p[:, :2730] = w2full
    w1full = full["L0_ffn_W1"]
    w1p = np.zeros((NL, D, 2 * DH1), np.float32)
    w1p[:, :, 0:2730] = w1full[:, :, 0:2730]
    w1p[:, :, DH1:DH1 + 2730] = w1full[:, :, 2730:5460]
    b1full = full["L0_ffn_b1"]
    b1p = np.zeros((NL, 2 * DH1), np.float32)
    b1p[:, 0:2730] = b1full[:, 0:2730]
    b1p[:, DH1:DH1 + 2730] = b1full[:, 2730:5460]
    b1c = b1p.reshape(NL, 2 * NHC, 128).transpose(0, 2, 1).copy()

    shared = dict(
        wq=full["L0_attn_Wq"].reshape(NCH, 128, D),
        wk=full["L0_attn_Wk"].reshape(NCH, 128, D),
        wv=full["L0_attn_Wv"].reshape(NCH, 128, D),
        wom=full["L0_attn_Wo"].reshape(NCH, 128, D),
        wa=full["L0_attn_Wa"].reshape(NCH, 128, H),
        wb=full["L0_attn_Wb"].reshape(NCH, 128, H),
        sm16=np.stack([full["L0_attn_bb"], full["L0_attn_A_log"],
                       full["L0_attn_dt_bias"]], 1).astype(np.float32),
        lamb_c=full["L0_attn_lambda_params"].reshape(NCH, 128).T.copy(),
        wcq=full["L0_attn_Wconv_q"].reshape(NCH, 128, 4),
        wck=full["L0_attn_Wconv_k"].reshape(NCH, 128, 4),
        normw2=np.tile(full["L0_attn_norm_w"], 2)[:, None],
        wqkv_t=full["L0_ffn_Wqkv"].reshape(NL, NCH, 128, 3 * D),
        wo_t=full["L0_ffn_Wo"].reshape(NL, NCH, 128, D),
        w1p=w1p.reshape(NL, NCH, 128, 2 * DH1),
        b1c=b1c,
        w2p=w2p.reshape(NL, NHC, 128, D),
        b2c=full["L0_ffn_b2"].reshape(NL, 8, 128).transpose(0, 2, 1).copy(),
        lng=full["L0_ffn_ln_g"].reshape(NL, 8, 128).transpose(0, 2, 1).copy(),
        lnbv=full["L0_ffn_ln_b"].reshape(NL, 8, 128).transpose(0, 2, 1).copy(),
        qs2=np.tile(full["L0_ffn_q_scale"], (1, 2))[:, :, None],
        ks2=np.tile(full["L0_ffn_k_scale"], (1, 2))[:, :, None],
        qs2s=np.tile(full["L0_ffn_q_scale"][:, sig], (1, 2))[:, :, None],
        ks2s=np.tile(full["L0_ffn_k_scale"][:, sig], (1, 2))[:, :, None],
        **c,
    )
    shared = {k: np.ascontiguousarray(v, dtype=v.dtype) for k, v in shared.items()}

    x = full["L0_attn_hidden_states"]
    in_maps = []
    for core in range(8):
        xs = x[2 * core:2 * core + 2]
        xfm = np.ascontiguousarray(
            xs.transpose(0, 2, 1).reshape(B, NCH, 128, T), dtype=np.float32)
        m = dict(shared)
        m["x_in"] = xfm
        in_maps.append(m)
    return in_maps


def gather_output(results):
    outs = []
    for r in results:
        o = r["out_f"]
        outs.append(o.reshape(B, D, T).transpose(0, 2, 1))
    return np.ascontiguousarray(np.concatenate(outs, 0), dtype=np.float32)


class SemScope:

    def __init__(self, kb, n, tag):
        self.kb = kb
        self.n = n
        self.tag = tag

    def __enter__(self):
        self.handles = [self.kb.nc.semaphore(f"{self.tag}_{i}")
                        for i in range(self.n)]
        self.sems = [h.__enter__() for h in self.handles]
        return self.sems

    def __exit__(self, *a):
        nc = self.kb.nc
        with nc.Block() as blk:
            @blk.gpsimd
            def _(g):
                for s in self.sems:
                    g.sem_clear(s)
        for h in reversed(self.handles):
            h.__exit__(None, None, None)
        return False


class KB:

    def __init__(self, nphase=99, debug_outs=()):
        self.nphase = nphase
        self.debug_outs = set(debug_outs)
        self.nc = bass.Bass("TRN2", target_bir_lowering=False)
        self.dram = {}
        self.ctx = []

    def dt(self, name, shape, dtype=F32, kind=None):
        if kind is None and name in self.debug_outs:
            kind = "ExternalOutput"
        kw = {} if kind is None else {"kind": kind}
        h = self.nc.dram_tensor(name, list(shape), dtype, **kw)
        self.dram[name] = h.ap()
        return self.dram[name]

    def sb(self, name, shape, dtype):
        h = self.nc.sbuf_tensor(name, list(shape), dtype)
        v = h.__enter__()
        self.ctx.append(h)
        return v

    def ps(self, name, shape, dtype=F32):
        h = self.nc.psum_tensor(name, list(shape), dtype)
        v = h.__enter__()
        self.ctx.append(h)
        return v

    def sem(self, name):
        h = self.nc.semaphore(name)
        v = h.__enter__()
        self.ctx.append(h)
        return v

    def init_dma_sems(self):
        self.streams = {}

    def stream(self, tag, depth=4):
        tag = {"S": "S", "G": "G"}.get(tag, tag)
        if tag not in self.streams:
            self.streams[tag] = DmaStream(self, tag, depth)
        return self.streams[tag]

    def close(self):
        for h in reversed(self.ctx):
            h.__exit__(None, None, None)


class DmaStream:

    def __init__(self, kb, tag, depth=4):
        self.depth = depth
        self.sems = [kb.sem(f"dma_{tag}_{j}") for j in range(depth)]
        self.i = 0

    def issue(self, eng, dst, src, slow=False):
        j = self.i % self.depth
        if self.i >= self.depth:
            eng.wait_ge(self.sems[j], (self.i // self.depth) * 16)
        eng.dma_start(dst, src,
                      allow_slow_non_contiguous=slow).then_inc(self.sems[j], 16)
        self.i += 1
        return self.i

    def wait(self, eng, ticket):
        j = (ticket - 1) % self.depth
        eng.wait_ge(self.sems[j], ((ticket - 1) // self.depth + 1) * 16)

    def wait_all(self, eng):
        for j in range(self.depth):
            cnt = (self.i - 1 - j) // self.depth + 1 if self.i > j else 0
            if cnt:
                eng.wait_ge(self.sems[j], cnt * 16)


def build(nphase=99, debug_outs=()):
    kb = KB(nphase, debug_outs)
    nc = kb.nc
    dt = kb.dt

    x_in = dt("x_in", [B, NCH, 128, T], F32, "ExternalInput")
    for nm, sh in [("wq", [NCH, 128, D]), ("wk", [NCH, 128, D]),
                   ("wv", [NCH, 128, D]), ("wom", [NCH, 128, D]),
                   ("wa", [NCH, 128, H]), ("wb", [NCH, 128, H]),
                   ("sm16", [H, 3]), ("lamb_c", [128, NCH]),
                   ("wcq", [NCH, 128, 4]), ("wck", [NCH, 128, 4]),
                   ("normw2", [128, 1]),
                   ("wqkv_t", [NL, NCH, 128, 3 * D]),
                   ("wo_t", [NL, NCH, 128, D]),
                   ("w1p", [NL, NCH, 128, 2 * DH1]),
                   ("b1c", [NL, 128, 2 * NHC]),
                   ("w2p", [NL, NHC, 128, D]),
                   ("b2c", [NL, 128, NCH]), ("lng", [NL, 128, NCH]),
                   ("lnbv", [NL, 128, NCH]),
                   ("qs2", [NL, 128, 1]), ("ks2", [NL, 128, 1]),
                   ("qs2s", [NL, 128, 1]), ("ks2s", [NL, 128, 1]),
                   ("identf", [128, 128]), ("cgmask", [128, 128]),
                   ("sel8", [NCH, 128, 16]), ("selT8", [NCH, 16, 128]),
                   ("bo2p", [2, 128]), ("bo2n", [2, 128]), ("blk2", [128, 2]),
                   ("zm2", [128, 2]),
                   ("onescol", [128, 128]),
                   ("csq", [128, T]), ("ssq", [128, T]),
                   ("cik", [128, T]), ("sik", [128, T]),
                   ("wmask", [2, 128, 512]), ("wmask0", [2, 128, 256])]:
        dt(nm, sh, F32, "ExternalInput")
    for nm, sh in [("identb", [128, 128]), ("pswap", [128, 128])]:
        dt(nm, sh, BF16, "ExternalInput")

    out_f = dt("out_f", [B, NCH, 128, T], F32, "ExternalOutput")

    qh = dt("qh", [B, NCH, 128, T], BF16)
    kh = dt("kh", [B, NCH, 128, T], BF16)
    kn_s = dt("kn_s", [B, H, NCK, 128, DK], BF16)
    vn_s = dt("vn_s", [B, H, NCK, 128, DK], BF16)
    mo = dt("mo", [B, NCH, 128, T], BF16)
    strA = dt("strA", [B, NCH, 128, T + 1], F32)
    strB = dt("strB", [B, NCH, 128, T + 1], F32)
    x1s = dt("x1s", [B, NCH, 128, T], F32)
    dt("dbg_rstd", [16, T], BF16)
    dt("dbg_ssq", [16, T], F32)
    dt("dbg_qsil", [128, NCH * T], BF16)
    dt("dbg_raw", [128, NCH * (T + 4)], BF16)

    P = {}
    kb.init_dma_sems()
    P["identf"] = kb.sb("c_identf", [128, 128], F32)
    P["identb"] = kb.sb("c_identb", [128, 128], BF16)
    P["pswap"] = kb.sb("c_pswap", [128, 128], BF16)
    P["cgmask"] = kb.sb("c_cgmask", [128, 128], F32)
    P["sel8b"] = kb.sb("c_sel8b", [128, NCH * 16], BF16)
    P["selT8b"] = kb.sb("c_selT8b", [16, NCH * 128], BF16)
    P["bo2p"] = kb.sb("c_bo2p", [2, 128], F32)
    P["bo2n"] = kb.sb("c_bo2n", [2, 128], F32)
    P["bo2pb"] = kb.sb("c_bo2pb", [2, 128], BF16)
    P["bo2nb"] = kb.sb("c_bo2nb", [2, 128], BF16)
    P["blk2b"] = kb.sb("c_blk2b", [128, 2], BF16)
    P["onesf"] = kb.sb("c_onesf", [128, 128], F32)
    P["onesb"] = kb.sb("c_onesb", [128, 128], BF16)
    P["sm16"] = kb.sb("c_sm16", [16, 3], F32)
    P["negexpA"] = kb.sb("c_negexpA", [16, 1], F32)
    P["lamsp"] = kb.sb("c_lamsp", [128, NCH], F32)
    P["normw2"] = kb.sb("c_normw2", [128, 1], F32)
    P["epsc"] = kb.sb("c_epsc", [128, 4], F32)
    P["zm2"] = kb.sb("c_zm2", [128, 2], F32)
    P["d_hm"] = kb.sb("p_dhm", [16, B * T], F32)
    P["gamT"] = kb.sb("p_gamT", [128, B * NCK * 16], F32)
    P["lnbmgT"] = kb.sb("p_lnbmgT", [128, B * NCK * 16], F32)
    P["ceT"] = kb.sb("p_ceT", [128, B * NCK * 16], F32)
    kb.P = P

    m1_emit_consts(kb)
    if kb.nphase >= 1:
        m1_emit(kb)
    if kb.nphase >= 2:
        cg_emit(kb)
    if kb.nphase >= 3:
        mo_emit(kb)
    if kb.nphase >= 4:
        tr_emit(kb)
    if kb.nphase >= 5:
        fin_emit(kb)

    kb.close()
    return kb





def m1_emit_consts(kb):
    nc = kb.nc
    P = kb.P
    d = kb.dram
    scl = kb.stream("S")
    gcl = kb.stream("G", depth=8)
    with nc.Block() as blk:
        @blk.sync
        def _(sy):
            for dst, src in [(P["identf"], d["identf"]),
                             (P["identb"], d["identb"]),
                             (P["pswap"], d["pswap"]),
                             (P["cgmask"], d["cgmask"]),
                             (P["bo2p"], d["bo2p"]),
                             (P["bo2n"], d["bo2n"]),
                             (P["sm16"], d["sm16"]),
                             (P["normw2"], d["normw2"]),
                             (P["zm2"], d["zm2"])]:
                scl.issue(sy, dst[:, :], src)

        @blk.gpsimd
        def _(g):
            for c in range(NCH):
                gcl.issue(g, P["sel8b"][:, c * 16:(c + 1) * 16], d["sel8"][c])
                gcl.issue(g, P["selT8b"][:, c * 128:(c + 1) * 128],
                          d["selT8"][c])
            gcl.issue(g, P["blk2b"][:, :], d["blk2"])
            gcl.issue(g, P["bo2pb"][:, :], d["bo2p"])
            gcl.issue(g, P["bo2nb"][:, :], d["bo2n"])
            g.memset(P["onesf"][:, :], 1.0)
            g.memset(P["onesb"][:, :], 1.0)
            g.memset(P["epsc"][:, 0:1], 1e-6)
            g.memset(P["epsc"][:, 1:2], 1e-5)
            g.memset(P["epsc"][:, 2:3], 1e-12)
            g.memset(P["epsc"][:, 3:4], 0.0)

    with nc.Block() as blk:
        @blk.sync
        def _(sy):
            scl.wait_all(sy)
            gcl.wait_all(sy)

    with (nc.sbuf_tensor("m1c_lraw", [128, NCH], F32) as lraw,
          nc.sbuf_tensor("m1c_t1", [128, NCH], F32) as t1,
          nc.sbuf_tensor("m1c_t2", [128, NCH], F32) as t2):
        with nc.Block() as blk:
            @blk.sync
            def _(sy):
                tk = scl.issue(sy, lraw[:, :], kb.dram["lamb_c"])
                scl.wait(sy, tk)
        with nc.Block() as blk:
            @blk.scalar
            def _(s):
                s.activation(t1[:, :], lraw[:, :], AF.Abs)
        with nc.Block() as blk:
            @blk.scalar
            def _(s):
                s.activation(t1[:, :], t1[:, :], AF.Exp, scale=-1.0)
        with nc.Block() as blk:
            @blk.scalar
            def _(s):
                s.activation(t1[:, :], t1[:, :], AF.Ln, bias=1.0)
                s.activation(P["negexpA"][:, :], P["sm16"][:, 1:2], AF.Exp)
        with nc.Block() as blk:
            @blk.vector
            def _(v):
                v.tensor_scalar(t2[:, :], lraw[:, :], 0.0, None, AluOpType.max)
                v.tensor_scalar(P["negexpA"][:, :], P["negexpA"][:, :], -1.0,
                                None, AluOpType.mult)
        with nc.Block() as blk:
            @blk.vector
            def _(v):
                v.scalar_tensor_tensor(P["lamsp"][:, :], t1[:, :], 0.25,
                                       t2[:, :], AluOpType.add, AluOpType.add)


def softplus_16(nc, zsb, t1, out, sign):
    with nc.Block() as blk:
        @blk.scalar
        def _(s):
            s.activation(t1[:, :], zsb[:, :], AF.Abs)
    with nc.Block() as blk:
        @blk.scalar
        def _(s):
            s.activation(t1[:, :], t1[:, :], AF.Exp, scale=-1.0)
    with nc.Block() as blk:
        @blk.scalar
        def _(s):
            s.activation(t1[:, :], t1[:, :], AF.Ln, bias=1.0)

        @blk.vector
        def _(v):
            if sign > 0:
                v.tensor_scalar(out[:, :], zsb[:, :], 0.0, None, AluOpType.max)
            else:
                v.tensor_scalar(out[:, :], zsb[:, :], 0.0, None, AluOpType.min)
    with nc.Block() as blk:
        @blk.vector
        def _(v):
            if sign > 0:
                v.tensor_add(out[:, :], out[:, :], t1[:, :])
            else:
                v.tensor_sub(out[:, :], out[:, :], t1[:, :])


def m1_emit(kb):
    nc = kb.nc
    d = kb.dram
    wl = kb.stream("G", depth=8)
    ws = kb.stream("S")

    with (nc.sbuf_tensor("m1_wab", [128, NCH * 2 * H], BF16) as wab_sb,
          nc.sbuf_tensor("m1_wc", [128, NCH * 8], F32) as wc_sb,
          nc.sbuf_tensor("m1_xb", [128, NCH * T], BF16) as xb):

        with nc.Block() as blk:
            @blk.gpsimd
            def _(g):
                for c in range(NCH):
                    wl.issue(g, wab_sb[:, c * 2 * H:c * 2 * H + H], d["wa"][c])
                    wl.issue(g, wab_sb[:, c * 2 * H + H:(c + 1) * 2 * H],
                             d["wb"][c])
                wl.wait_all(g)

            @blk.sync
            def _(sy):
                for c in range(NCH):
                    ws.issue(sy, wc_sb[:, c * 8:c * 8 + 4], d["wcq"][c])
                    ws.issue(sy, wc_sb[:, c * 8 + 4:c * 8 + 8], d["wck"][c])
                ws.wait_all(sy)

        for b in range(B):
            with nc.Block() as blk:
                @blk.gpsimd
                def _(g):
                    for c in range(NCH):
                        wl.issue(g, xb[:, c * T:(c + 1) * T], d["x_in"][b, c])
                    wl.wait_all(g)
            with (nc.sbuf_tensor(f"m1_w{b}", [128, NCH * D], BF16) as w_sb,
                  nc.sbuf_tensor(f"m1_raw{b}", [128, NCH * (T + 4)], BF16)
                  as rawall,
                  nc.sbuf_tensor(f"m1_qsil{b}", [128, NCH * T], BF16) as qsil,
                  nc.sbuf_tensor(f"m1_convf{b}", [128, 2 * T], F32) as convf,
                  nc.sbuf_tensor(f"m1_sq{b}", [128, 2 * T], BF16) as sqb,
                  nc.sbuf_tensor(f"m1_ssq{b}", [16, T], F32) as ssq_sb,
                  nc.sbuf_tensor(f"m1_rstdb{b}", [16, T], BF16) as rstd_b,
                  nc.sbuf_tensor(f"m1_hat{b}", [128, 2 * T], BF16) as hatb,
                  nc.sbuf_tensor(f"m1_trev{b}", [128, 4 * 128], BF16) as trev):
                m1_emit_b(kb, b, w_sb, wab_sb, wc_sb, xb, rawall,
                       qsil, convf, sqb, ssq_sb, rstd_b, hatb, trev)
            m1_emit_gate(kb, b, wab_sb, xb)


def m1_emit_b(kb, b, w_sb, wab_sb, wc_sb, xb, rawall, qsil,
           convf, sqb, ssq_sb, rstd_b, hatb, trev):
    nc = kb.nc
    P = kb.P
    d = kb.dram
    wl = kb.stream("G", depth=8)
    hd = kb.stream("S")

    for which, wname, hdram in ((0, "wq", "qh"), (1, "wk", "kh")):
        wcoff = which * 4
        with nc.Block() as blk:
            @blk.gpsimd
            def _(g):
                for c in range(NCH):
                    wl.issue(g, w_sb[:, c * D:(c + 1) * D], d[wname][c])
                wl.wait_all(g)

        with nc.psum_tensor(f"m1_ps{b}{which}", [128, 2048],
                            mybir.dt.float32) as mps:
            with SemScope(kb, 2, "m1A") as (smm, sev):
                with nc.Block() as blk:
                    @blk.tensor
                    def _(t):
                        for c in range(NCH):
                            for tt in range(4):
                                slot = c * 4 + tt
                                if slot >= 4:
                                    t.wait_ge(sev, slot - 3)
                                for kc in range(NCH):
                                    ins = t.matmul(
                                        mps[:, tt * 512:(tt + 1) * 512],
                                        w_sb[:, kc * D + c * 128:
                                             kc * D + (c + 1) * 128],
                                        xb[:, kc * T + tt * 512:
                                           kc * T + (tt + 1) * 512],
                                        start=(kc == 0), stop=(kc == NCH - 1))
                                ins.then_inc(smm, 1)

                    @blk.scalar
                    def _(s):
                        for c in range(NCH):
                            s.memzero(rawall[:, c * (T + 4):c * (T + 4) + 4])
                            for tt in range(4):
                                slot = c * 4 + tt
                                s.wait_ge(smm, slot + 1)
                                s.copy(rawall[:, c * (T + 4) + 4 + tt * 512:
                                              c * (T + 4) + 4 + (tt + 1) * 512],
                                       mps[:, tt * 512:(tt + 1) * 512]
                                       ).then_inc(sev, 1)

            with (nc.psum_tensor(f"m1_ss{b}{which}", [16, 2048],
                                 mybir.dt.float32) as ssps,
                  SemScope(kb, 5, "m1B") as (scv, ssl, ssp, smq, sch)):
                nch_chain = [0]
                with nc.Block() as blk:
                    @blk.vector
                    def _(v):
                        for c in range(NCH):
                            if c >= 2:
                                v.wait_ge(ssl, c - 1)
                            buf = convf[:, (c % 2) * T:(c % 2 + 1) * T]
                            ro = c * (T + 4) + 1
                            wco = c * 8 + wcoff
                            v.tensor_scalar(buf, rawall[:, ro:ro + T],
                                            wc_sb[:, wco:wco + 1], None,
                                            AluOpType.mult).then_inc(sch, 1)
                            nch_chain[0] += 1
                            for i in range(1, 4):
                                v.wait_ge(sch, nch_chain[0])
                                ins = v.scalar_tensor_tensor(
                                    buf, rawall[:, ro + i:ro + i + T],
                                    wc_sb[:, wco + i:wco + i + 1],
                                    buf, AluOpType.mult, AluOpType.add)
                                if i < 3:
                                    ins.then_inc(sch, 1)
                                    nch_chain[0] += 1
                            ins.then_inc(scv, 1)

                    @blk.scalar
                    def _(s):
                        for c in range(NCH):
                            s.wait_ge(scv, c + 1)
                            s.activation(qsil[:, c * T:(c + 1) * T],
                                         convf[:, (c % 2) * T:(c % 2 + 1) * T],
                                         AF.Silu).then_inc(ssl, 1)

                    @blk.gpsimd
                    def _(g):
                        for c in range(NCH):
                            g.wait_ge(ssl, c + 1)
                            if c >= 2:
                                g.wait_ge(smq, c - 1)
                            g.tensor_mul(sqb[:, (c % 2) * T:(c % 2 + 1) * T],
                                         qsil[:, c * T:(c + 1) * T],
                                         qsil[:, c * T:(c + 1) * T]
                                         ).then_inc(ssp, 1)

                    @blk.tensor
                    def _(t):
                        for c in range(NCH):
                            t.wait_ge(ssp, c + 1)
                            for tt in range(4):
                                ins = t.matmul(
                                    ssps[:, tt * 512:(tt + 1) * 512],
                                    P["sel8b"][:, c * 16:(c + 1) * 16],
                                    sqb[:, (c % 2) * T + tt * 512:
                                        (c % 2) * T + (tt + 1) * 512],
                                    start=(c == 0), stop=(c == NCH - 1),
                                    skip_group_check=True)
                            ins.then_inc(smq, 1)

                with nc.Block() as blk:
                    @blk.vector
                    def _(v):
                        for tt in range(4):
                            v.tensor_copy(ssq_sb[:, tt * 512:(tt + 1) * 512],
                                          ssps[:, tt * 512:(tt + 1) * 512])
                with nc.Block() as blk:
                    @blk.scalar
                    def _(s):
                        s.activation(ssq_sb[:, :], ssq_sb[:, :], AF.Ln,
                                     bias=P["epsc"][0:16, 0:1])
                with nc.Block() as blk:
                    @blk.scalar
                    def _(s):
                        s.activation(rstd_b[:, :], ssq_sb[:, :], AF.Exp,
                                     scale=-0.5)
                if b == 0 and which == 0 and "dbg_rstd" in kb.dram:
                    with nc.Block() as blk:
                        @blk.sync
                        def _(sy):
                            st = kb.stream("S")
                            tks = [st.issue(sy, kb.dram["dbg_rstd"], rstd_b[:, :]),
                                   st.issue(sy, kb.dram["dbg_ssq"], ssq_sb[:, :]),
                                   st.issue(sy, kb.dram["dbg_qsil"], qsil[:, :]),
                                   st.issue(sy, kb.dram["dbg_raw"], rawall[:, :])]
                            for tk in tks:
                                st.wait(sy, tk)

            with SemScope(kb, 2, "m1D") as (sbc, sht):
                hd_base = hd.i
                with nc.Block() as blk:
                    @blk.tensor
                    def _(t):
                        for c in range(NCH):
                            for tt in range(4):
                                slot = c * 4 + tt
                                if slot >= 4:
                                    t.wait_ge(sht, slot - 3)
                                t.matmul(mps[:, tt * 512:(tt + 1) * 512],
                                         P["selT8b"][:, c * 128:(c + 1) * 128],
                                         rstd_b[:, tt * 512:(tt + 1) * 512],
                                         start=True, stop=True,
                                         skip_group_check=True
                                         ).then_inc(sbc, 1)

                    @blk.vector
                    def _(v):
                        for c in range(NCH):
                            if c >= 2:
                                hd.wait(v, hd_base + c - 1)
                            for tt in range(4):
                                slot = c * 4 + tt
                                v.wait_ge(sbc, slot + 1)
                                v.tensor_mul(
                                    hatb[:, (c % 2) * T + tt * 512:
                                         (c % 2) * T + (tt + 1) * 512],
                                    qsil[:, c * T + tt * 512:
                                         c * T + (tt + 1) * 512],
                                    mps[:, tt * 512:(tt + 1) * 512]
                                    ).then_inc(sht, 1)

                    @blk.sync
                    def _(sy):
                        for c in range(NCH):
                            sy.wait_ge(sht, c * 4 + 4)
                            hd.issue(sy, d[hdram][b, c],
                                     hatb[:, (c % 2) * T:(c % 2 + 1) * T])
                        hd.wait_all(sy)

        if which == 1:
            m1_emit_tr_tokmajor(kb, b, trev, "kn_s")

    with nc.Block() as blk:
        @blk.gpsimd
        def _(g):
            for c in range(NCH):
                wl.issue(g, w_sb[:, c * D:(c + 1) * D], d["wv"][c])
            wl.wait_all(g)

    with (nc.psum_tensor(f"m1_pv{b}", [128, 2048], mybir.dt.float32) as mps,
          SemScope(kb, 2, "m1V") as (smm, sev)):
        hd_base = hd.i
        with nc.Block() as blk:
            @blk.tensor
            def _(t):
                for c in range(NCH):
                    for tt in range(4):
                        slot = c * 4 + tt
                        if slot >= 4:
                            t.wait_ge(sev, slot - 3)
                        for kc in range(NCH):
                            ins = t.matmul(
                                mps[:, tt * 512:(tt + 1) * 512],
                                w_sb[:, kc * D + c * 128:kc * D + (c + 1) * 128],
                                xb[:, kc * T + tt * 512:kc * T + (tt + 1) * 512],
                                start=(kc == 0), stop=(kc == NCH - 1))
                        ins.then_inc(smm, 1)

            @blk.scalar
            def _(s):
                for c in range(NCH):
                    if c >= 2:
                        hd.wait(s, hd_base + c - 1)
                    for tt in range(4):
                        slot = c * 4 + tt
                        s.wait_ge(smm, slot + 1)
                        s.copy(hatb[:, (c % 2) * T + tt * 512:
                                    (c % 2) * T + (tt + 1) * 512],
                               mps[:, tt * 512:(tt + 1) * 512]).then_inc(sev, 1)

            @blk.sync
            def _(sy):
                for c in range(NCH):
                    sy.wait_ge(sev, c * 4 + 4)
                    hd.issue(sy, d["mo"][b, c],
                             hatb[:, (c % 2) * T:(c % 2 + 1) * T])
                hd.wait_all(sy)

    m1_emit_tr_tokmajor(kb, b, trev, "vn_s")


def m1_emit_tr_tokmajor(kb, b, trev, dram_name):
    nc = kb.nc
    P = kb.P
    d = kb.dram
    src = {"kn_s": "kh", "vn_s": "mo"}[dram_name]
    tl = kb.stream("S")
    tg = kb.stream("G", depth=8)
    with (nc.sbuf_tensor(f"m1t_in{b}{dram_name}", [128, 2 * T], BF16) as tin,
          nc.psum_tensor(f"m1t_ps{b}{dram_name}", [128, 2048],
                         mybir.dt.bfloat16) as tps,
          SemScope(kb, 2, "m1T") as (str_, sev)):
        tl_base = tl.i
        tg_base = tg.i
        with nc.Block() as blk:
            @blk.sync
            def _(sy):
                for c in range(NCH):
                    if c >= 2:
                        sy.wait_ge(str_, (c - 1) * 16)
                    tl.issue(sy, tin[:, (c % 2) * T:(c % 2 + 1) * T],
                             d[src][b, c])

            @blk.tensor
            def _(t):
                for c in range(NCH):
                    tl.wait(t, tl_base + c + 1)
                    for ts in range(16):
                        slot = c * 16 + ts
                        if slot >= 2:
                            t.wait_ge(sev, slot - 1)
                        t.transpose(tps[:, (slot % 2) * 1024:
                                        (slot % 2) * 1024 + 128],
                                    tin[:, (c % 2) * T + ts * 128:
                                        (c % 2) * T + (ts + 1) * 128],
                                    P["identb"][:, :]).then_inc(str_, 1)

            @blk.vector
            def _(v):
                for c in range(NCH):
                    for ts in range(16):
                        slot = c * 16 + ts
                        v.wait_ge(str_, slot + 1)
                        if slot >= 4:
                            tg.wait(v, tg_base + 2 * (slot - 4) + 1)
                            tg.wait(v, tg_base + 2 * (slot - 4) + 2)
                        v.tensor_copy(trev[:, (slot % 4) * 128:
                                           (slot % 4 + 1) * 128],
                                      tps[:, (slot % 2) * 1024:
                                          (slot % 2) * 1024 + 128]
                                      ).then_inc(sev, 1)

            @blk.gpsimd
            def _(g):
                for c in range(NCH):
                    for ts in range(16):
                        slot = c * 16 + ts
                        g.wait_ge(sev, slot + 1)
                        tg.issue(g, d[dram_name][b, 2 * c, ts],
                                 trev[:, (slot % 4) * 128:(slot % 4) * 128 + 64])
                        tg.issue(g, d[dram_name][b, 2 * c + 1, ts],
                                 trev[:, (slot % 4) * 128 + 64:
                                      (slot % 4 + 1) * 128])
                tg.wait_all(g)


def m1_emit_gate(kb, b, wab_sb, xb):
    nc = kb.nc
    P = kb.P
    with (nc.sbuf_tensor(f"m1g_za{b}", [16, T], F32) as za,
          nc.sbuf_tensor(f"m1g_zb{b}", [16, T], F32) as zb,
          nc.sbuf_tensor(f"m1g_t1{b}", [16, T], F32) as gt1,
          nc.sbuf_tensor(f"m1g_sp{b}", [16, T], F32) as sp,
          nc.sbuf_tensor(f"m1g_lnb{b}", [16, T], F32) as lnb_s,
          nc.sbuf_tensor(f"m1g_gam{b}", [16, T], F32) as gam_s,
          nc.sbuf_tensor(f"m1g_ce{b}", [16, T], F32) as ce_s):
        with (nc.psum_tensor(f"m1g_pa{b}", [16, 2048], mybir.dt.float32) as pa,
              nc.psum_tensor(f"m1g_pb{b}", [16, 2048], mybir.dt.float32) as pb):
            with nc.Block() as blk:
                @blk.tensor
                def _(t):
                    for tt in range(4):
                        for kc in range(NCH):
                            t.matmul(pa[:, tt * 512:(tt + 1) * 512],
                                     wab_sb[:, kc * 2 * H:kc * 2 * H + H],
                                     xb[:, kc * T + tt * 512:
                                        kc * T + (tt + 1) * 512],
                                     start=(kc == 0), stop=(kc == NCH - 1))
                        for kc in range(NCH):
                            t.matmul(pb[:, tt * 512:(tt + 1) * 512],
                                     wab_sb[:, kc * 2 * H + H:(kc + 1) * 2 * H],
                                     xb[:, kc * T + tt * 512:
                                        kc * T + (tt + 1) * 512],
                                     start=(kc == 0), stop=(kc == NCH - 1))
            with nc.Block() as blk:
                @blk.vector
                def _(v):
                    for tt in range(4):
                        sl = slice(tt * 512, (tt + 1) * 512)
                        v.tensor_scalar(za[:, sl], pa[:, sl],
                                        P["sm16"][:, 2:3], None, AluOpType.add)
                        v.tensor_scalar(zb[:, sl], pb[:, sl],
                                        P["sm16"][:, 0:1], None, AluOpType.add)

        softplus_16(nc, za, gt1, sp, +1)
        with nc.Block() as blk:
            @blk.vector
            def _(v):
                v.tensor_scalar(za[:, :], sp[:, :], P["negexpA"][:, 0:1],
                                None, AluOpType.mult)
        softplus_16(nc, zb, gt1, lnb_s, -1)
        with nc.Block() as blk:
            @blk.vector
            def _(v):
                v.tensor_tensor_scan(sp[:, :], za[:, :], za[:, :], 0.0,
                                     AluOpType.add, AluOpType.bypass)
        with nc.Block() as blk:
            @blk.vector
            def _(v):
                v.tensor_copy(gam_s[:, 0:CC], sp[:, 0:CC])
                for c in range(1, NCK):
                    v.tensor_scalar(gam_s[:, c * CC:(c + 1) * CC],
                                    sp[:, c * CC:(c + 1) * CC],
                                    sp[:, c * CC - 1:c * CC], None,
                                    AluOpType.subtract)
        with nc.Block() as blk:
            @blk.scalar
            def _(s):
                s.activation(P["d_hm"][:, b * T:(b + 1) * T], gam_s[:, :],
                             AF.Exp)
        with nc.Block() as blk:
            @blk.vector
            def _(v):
                v.tensor_sub(sp[:, :], lnb_s[:, :], gam_s[:, :])
                for c in range(NCK):
                    sl = slice(c * CC, (c + 1) * CC)
                    v.scalar_tensor_tensor(
                        ce_s[:, sl], gam_s[:, sl],
                        gam_s[:, (c + 1) * CC - 1:(c + 1) * CC], lnb_s[:, sl],
                        AluOpType.subtract, AluOpType.subtract)
        with nc.Block() as blk:
            @blk.scalar
            def _(s):
                s.activation(ce_s[:, :], ce_s[:, :], AF.Exp, scale=-1.0)

        for srcbuf, dstname in ((gam_s, "gamT"), (sp, "lnbmgT"), (ce_s, "ceT")):
            dst = kb.P[dstname]
            with (nc.psum_tensor(f"m1g_tr{b}{dstname}", [128, 1024],
                                 mybir.dt.float32) as trf,
                  SemScope(kb, 2, "m1Gt") as (st1, st2)):
                with nc.Block() as blk:
                    @blk.tensor
                    def _(t):
                        for c in range(NCK):
                            if c >= 2:
                                t.wait_ge(st2, c - 1)
                            t.transpose(trf[:, (c % 2) * 512:(c % 2) * 512 + 16],
                                        srcbuf[:, c * CC:(c + 1) * CC],
                                        P["identf"][0:16, 0:16]
                                        ).then_inc(st1, 1)

                    @blk.vector
                    def _(v):
                        for c in range(NCK):
                            v.wait_ge(st1, c + 1)
                            col = (b * NCK + c) * 16
                            v.tensor_copy(dst[:, col:col + 16],
                                          trf[:, (c % 2) * 512:
                                              (c % 2) * 512 + 16]
                                          ).then_inc(st2, 1)





CGLIM = 9


def cg_emit(kb):
    nc = kb.nc
    with ExitStack() as stk:
        def mk(name, shape, dt):
            return stk.enter_context(nc.sbuf_tensor(name, shape, dt))
        bufs = dict(
            ktm=mk("cg_ktm", [128, 4 * T], BF16),
            q2=mk("cg_q", [128, 2 * T], BF16),
            kn4=mk("cg_kn", [128, 4 * NCK * DK], BF16),
            vn4=mk("cg_vn", [128, 4 * NCK * DK], BF16),
            A4=mk("cg_A", [128, 4 * NCK * CC], BF16),
            D2=mk("cg_D", [128, 2 * T], BF16),
            X2=mk("cg_X", [128, 2 * T], BF16),
            R2=mk("cg_R", [128, 2 * T], BF16),
            P2=mk("cg_P", [128, 2 * T], BF16),
            Pt2=mk("cg_Pt", [128, 2 * T], BF16),
            Skkb=mk("cg_Skk", [128, 2 * NCK * CC], BF16),
            Skvb=mk("cg_Skv", [128, 2 * NCK * CC], BF16),
            Lb2=mk("cg_Lb", [128, 2 * 128], BF16),
            Ktl=mk("cg_Ktl", [128, 4 * 64], BF16),
            eIb=mk("cg_eI", [128, 2 * 128], BF16),
            W4=mk("cg_W", [128, 4 * 512], BF16),
            tmb=mk("cg_tm", [128, 2 * 512], BF16),
            Apb=mk("cg_Apb", [128, 2 * 512], BF16),
            mub=mk("cg_mu", [128, 6 * 512], BF16),
            r2b=mk("cg_r2", [128, 2 * 512], BF16),
            rsA=mk("cg_rsA", [2, 2 * T], BF16),
            rsB=mk("cg_rsB", [2, 2 * T], BF16),
            al2=mk("cg_al", [2, 2 * 512], BF16),
            be2=mk("cg_be", [2, 2 * 512], BF16),
            sc1=mk("cg_sc1", [2, 4 * 512], F32),
            sc2=mk("cg_sc2", [2, 4 * 512], BF16),
            dhb=mk("cg_dh", [16, T], BF16),
            crb=mk("cg_cr", [128, 2 * 128], F32),
            dlb=mk("cg_dl", [128, 2 * 128], F32),
            dEnd=mk("cg_dEnd", [128, 32], F32),
        )
        bufs["ob2"] = bufs["q2"]
        ngroups = getattr(kb, "cg_ngroups", 8)
        for g in range(ngroups):
            cg_emit_group(kb, g, g // 4, 4 * (g % 4), bufs)


def cg_emit_group(kb, g, b, hbase, bf):
    nc = kb.nc
    P = kb.P
    d = kb.dram
    ld = kb.stream("S")

    q2, kn4, vn4, A4 = bf["q2"], bf["kn4"], bf["vn4"], bf["A4"]
    D2, X2, R2, P2, Pt2 = bf["D2"], bf["X2"], bf["R2"], bf["P2"], bf["Pt2"]
    Skkb, Skvb, Lb2 = bf["Skkb"], bf["Skvb"], bf["Lb2"]
    Ktl, eIb, W4, tmb, Apb = (bf["Ktl"], bf["eIb"], bf["W4"], bf["tmb"],
                              bf["Apb"])
    mub, r2b, rsA, rsB = bf["mub"], bf["r2b"], bf["rsA"], bf["rsB"]
    al2, be2, sc1, sc2 = bf["al2"], bf["be2"], bf["sc1"], bf["sc2"]
    dhb, crb, dlb, ob2 = bf["dhb"], bf["crb"], bf["dlb"], bf["ob2"]
    dEnd = bf["dEnd"]

    ktm = bf["ktm"]
    with nc.Block() as blk:
        @blk.sync
        def _(sy):
            for j in range(2):
                hp = hbase // 2 + j
                ld.issue(sy, ktm[:, (2 * j) * T:(2 * j + 1) * T],
                         d["kh"][b, hp])
                ld.issue(sy, ktm[:, (2 * j + 1) * T:(2 * j + 2) * T],
                         d["kh"][b, hp])
                ld.issue(sy, q2[:, j * T:(j + 1) * T], d["qh"][b, hp])
            for hh in range(4):
                h = hbase + hh
                for c in range(NCK):
                    ld.issue(sy, kn4[:, (hh * NCK + c) * DK:
                                     (hh * NCK + c + 1) * DK],
                             d["kn_s"][b, h, c])
                    ld.issue(sy, vn4[:, (hh * NCK + c) * DK:
                                     (hh * NCK + c + 1) * DK],
                             d["vn_s"][b, h, c])
            ld.wait_all(sy)

    with nc.Block() as blk:
        @blk.vector
        def _(v):
            for hh in range(4):
                pj = hh % 2
                v.tensor_scalar(ktm[:, hh * T:(hh + 1) * T],
                                ktm[:, hh * T:(hh + 1) * T],
                                P["zm2"][:, pj:pj + 1], None,
                                AluOpType.mult)

    if CGLIM < 1:
        return
    with (nc.psum_tensor(f"cg_psD{g}", [128, 2048], PS) as psD,
          SemScope(kb, 2, "cgD") as (s1, s2)):
        with nc.Block() as blk:
            @blk.vector
            def _(v):
                v.tensor_copy(dhb[:, :], P["d_hm"][0:16, b * T:(b + 1) * T])
        with nc.Block() as blk:
            @blk.tensor
            def _(t):
                for j in range(2):
                    hp = hbase // 2 + j
                    for tt in range(4):
                        if j == 1:
                            t.wait_ge(s2, tt + 1)
                        t.matmul(psD[:, tt * 512:(tt + 1) * 512],
                                 P["selT8b"][:, hp * 128:(hp + 1) * 128],
                                 dhb[0:16, tt * 512:(tt + 1) * 512],
                                 start=True, stop=True,
                                 skip_group_check=True).then_inc(s1, 1)

            @blk.vector
            def _(v):
                for j in range(2):
                    for tt in range(4):
                        v.wait_ge(s1, j * 4 + tt + 1)
                        v.tensor_copy(D2[:, j * T + tt * 512:
                                         j * T + (tt + 1) * 512],
                                      psD[:, tt * 512:(tt + 1) * 512])
                        for dc in range(4):
                            c = tt * 4 + dc
                            ins = v.tensor_copy(
                                dEnd[:, j * 16 + c:j * 16 + c + 1],
                                psD[:, tt * 512 + dc * 128 + 127:
                                    tt * 512 + dc * 128 + 128])
                        ins.then_inc(s2, 1)

            @blk.gpsimd
            def _(gp):
                for j in range(2):
                    hp = hbase // 2 + j
                    gp.tensor_scalar(Lb2[:, j * 128:(j + 1) * 128],
                                     P["identb"][:, :],
                                     P["lamsp"][:, hp:hp + 1], None,
                                     AluOpType.mult)

    if CGLIM < 2:
        return
    with (nc.psum_tensor(f"cg_psA{g}", [128, 1024], PS) as psA,
          SemScope(kb, 4, "cgA") as (s1, s2, s3, s4)):
        with nc.Block() as blk:
            @blk.vector
            def _(v):
                for i in range(66):
                    if i < 64:
                        hh, c = i // NCK, i % NCK
                        col = (b * NCK + c) * 16 + hbase + hh
                        if i >= 2:
                            v.wait_ge(s2, i - 1)
                        v.tensor_scalar(crb[:, (i % 2) * 128:(i % 2 + 1) * 128],
                                        P["onesf"][:, :],
                                        P["gamT"][:, col:col + 1], None,
                                        AluOpType.mult).then_inc(s1, 1)
                    if i >= 2:
                        k = i - 2
                        hh, c = k // NCK, k % NCK
                        col = (b * NCK + c) * 16 + hbase + hh
                        v.wait_ge(s2, k + 1)
                        if k >= 2:
                            v.wait_ge(s4, k - 1)
                        v.scalar_tensor_tensor(
                            dlb[:, (k % 2) * 128:(k % 2 + 1) * 128],
                            psA[:, (k % 2) * 512:(k % 2) * 512 + 128],
                            P["lnbmgT"][:, col:col + 1], P["cgmask"][:, :],
                            AluOpType.add, AluOpType.add).then_inc(s3, 1)

            @blk.tensor
            def _(t):
                for i in range(64):
                    t.wait_ge(s1, i + 1)
                    if i >= 2:
                        t.wait_ge(s3, i - 1)
                    t.transpose(psA[:, (i % 2) * 512:(i % 2) * 512 + 128],
                                crb[:, (i % 2) * 128:(i % 2 + 1) * 128],
                                P["identf"][:, :]).then_inc(s2, 1)

            @blk.scalar
            def _(s):
                for i in range(64):
                    hh, c = i // NCK, i % NCK
                    s.wait_ge(s3, i + 1)
                    s.activation(A4[:, (hh * NCK + c) * CC:
                                    (hh * NCK + c + 1) * CC],
                                 dlb[:, (i % 2) * 128:(i % 2 + 1) * 128],
                                 AF.Exp).then_inc(s4, 1)

    if CGLIM < 3:
        return
    with (nc.psum_tensor(f"cg_psS{g}", [128, 512], PS) as psS,
          SemScope(kb, 3, "cgS") as (s1, s2, s3)):
        with nc.Block() as blk:
            @blk.vector
            def _(v):
                for j in range(2):
                    v.memset(Skkb[:, j * NCK * CC:j * NCK * CC + CC], 0.0)
                    v.memset(Skvb[:, j * NCK * CC:j * NCK * CC + CC], 0.0)
        with nc.Block() as blk:
            @blk.gpsimd
            def _(gp):
                for c in range(NCK - 1):
                    if c >= 1:
                        gp.wait_ge(s2, c)
                    for j in range(2):
                        for pj in range(2):
                            hh = 2 * j + pj
                            col = (b * NCK + c) * 16 + hbase + hh
                            gp.tensor_scalar(
                                Ktl[:, hh * 64:(hh + 1) * 64],
                                kn4[:, (hh * NCK + c) * DK:
                                    (hh * NCK + c + 1) * DK],
                                P["ceT"][:, col:col + 1], None,
                                AluOpType.mult)
                        ins = gp.tensor_scalar(
                            eIb[:, j * 128:(j + 1) * 128], P["identb"][:, :],
                            dEnd[:, j * 16 + c:j * 16 + c + 1],
                            None, AluOpType.mult)
                    ins.then_inc(s1, 1)

            @blk.tensor
            def _(t):
                for c in range(NCK - 1):
                    t.wait_ge(s1, c + 1)
                    if c >= 1:
                        t.wait_ge(s3, c)
                    for j in range(2):
                        for kind, src, dstb in ((0, kn4, Skkb), (1, vn4, Skvb)):
                            pso = (2 * j + kind) * 128
                            t.matmul(psS[:, pso:pso + 128],
                                     eIb[:, j * 128:(j + 1) * 128],
                                     dstb[:, j * NCK * CC + c * CC:
                                          j * NCK * CC + (c + 1) * CC],
                                     start=True, stop=False,
                                     skip_group_check=True)
                            for pj in range(2):
                                hh = 2 * j + pj
                                ins = t.matmul(
                                    psS[pj * 64:(pj + 1) * 64,
                                        pso + pj * 64:pso + (pj + 1) * 64],
                                    Ktl[:, hh * 64:(hh + 1) * 64],
                                    src[:, (hh * NCK + c) * DK:
                                        (hh * NCK + c + 1) * DK],
                                    start=False, stop=(pj == 1),
                                    skip_group_check=True)
                    ins.then_inc(s2, 1)

            @blk.vector
            def _(v):
                for c in range(NCK - 1):
                    v.wait_ge(s2, c + 1)
                    for j in range(2):
                        v.tensor_copy(
                            Skkb[:, j * NCK * CC + (c + 1) * CC:
                                 j * NCK * CC + (c + 2) * CC],
                            psS[:, (2 * j) * 128:(2 * j + 1) * 128])
                        ins = v.tensor_copy(
                            Skvb[:, j * NCK * CC + (c + 1) * CC:
                                 j * NCK * CC + (c + 2) * CC],
                            psS[:, (2 * j + 1) * 128:(2 * j + 2) * 128])
                    ins.then_inc(s3, 1)

    if CGLIM < 4:
        return
    with (nc.psum_tensor(f"cg_psI{g}", [2, 1024], PS) as psI,
          SemScope(kb, 3, "cgI") as (s1, s2, s3)):
        with nc.Block() as blk:
            @blk.vector
            def _(v):
                for j in range(2):
                    v.memset(X2[:, j * T:(j + 1) * T], 0.0)
                    v.tensor_copy(R2[:, j * T:(j + 1) * T],
                                  q2[:, j * T:(j + 1) * T])
                    v.tensor_copy(P2[:, j * T:(j + 1) * T],
                                  q2[:, j * T:(j + 1) * T])
        with nc.Block() as blk:
            @blk.gpsimd
            def _(gp):
                for w in range(4):
                    for j in range(2):
                        if 2 * w + j >= 2:
                            gp.wait_ge(s2, 2 * w + j - 1)
                        gp.tensor_mul(r2b[:, j * 512:(j + 1) * 512],
                                      q2[:, j * T + w * 512:
                                         j * T + (w + 1) * 512],
                                      q2[:, j * T + w * 512:
                                         j * T + (w + 1) * 512]
                                      ).then_inc(s1, 1)

            @blk.tensor
            def _(t):
                for w in range(4):
                    for j in range(2):
                        t.wait_ge(s1, 2 * w + j + 1)
                        if 2 * w + j >= 2:
                            t.wait_ge(s3, 2 * w + j - 1)
                        t.matmul(psI[:, j * 512:(j + 1) * 512],
                                 P["blk2b"][:, :],
                                 r2b[:, j * 512:(j + 1) * 512],
                                 start=True, stop=True,
                                 skip_group_check=True).then_inc(s2, 1)

            @blk.scalar
            def _(s):
                for w in range(4):
                    for j in range(2):
                        s.wait_ge(s2, 2 * w + j + 1)
                        s.copy(rsA[:, j * T + w * 512:j * T + (w + 1) * 512],
                               psI[:, j * 512:(j + 1) * 512]).then_inc(s3, 1)

    if CGLIM >= 50:
        kb_cgprobe.emit_iter_probe(kb, g, b, hbase, bf, CGLIM - 50)
        return
    if CGLIM < 5:
        return
    emit_iters(kb, g, b, hbase, bf)

    if CGLIM < 6:
        return
    cg_emit_out(kb, g, b, hbase, bf)


def emit_iters(kb, g, b, hbase, bf):
    nc = kb.nc
    P = kb.P
    kn4, A4 = bf["kn4"], bf["A4"]
    D2, X2, R2, P2, Pt2 = bf["D2"], bf["X2"], bf["R2"], bf["P2"], bf["Pt2"]
    Skkb, Lb2 = bf["Skkb"], bf["Lb2"]
    W4, tmb, Apb, mub, r2b = (bf["W4"], bf["tmb"], bf["Apb"], bf["mub"],
                              bf["r2b"])
    al2, be2, sc1, sc2 = bf["al2"], bf["be2"], bf["sc1"], bf["sc2"]
    rsAB = (bf["rsA"], bf["rsB"])

    with (nc.psum_tensor(f"cg_psU{g}", [128, 2048], PS) as psU,
          nc.psum_tensor(f"cg_psa{g}", [128, 1024], PS) as psa,
          nc.psum_tensor(f"cg_pss{g}", [2, 1024], PS) as pss,
          SemScope(kb, 17, "cgit") as sems):
        (s_u, s_wd, s_ap, s_tm, s_ae, s_pp, s_lnA, s_expA, s_al, s_bc,
         s_mu, s_pt, s_r2, s_rs, s_rc, s_be, s_ga) = sems

        with nc.Block() as blk:
            @blk.tensor
            def _(t):
                for it in range(NIT):
                    for w in range(4):
                        W = 4 * it + w
                        for hh in range(4):
                            j, base = hh // 2, (hh % 2) * 64
                            if W >= 1:
                                t.wait_ge(s_wd, 4 * (W - 1) + hh + 1)
                                t.wait_ge(s_mu, 6 * (W - 1) + 6)
                            if W >= 4:
                                t.wait_ge(s_ga, 6 * (W - 4) + 6)
                            for dc in range(4):
                                c = 4 * w + dc
                                t.matmul(
                                    psU[:, hh * 512 + dc * 128:
                                        hh * 512 + (dc + 1) * 128],
                                    bf["ktm"][:, hh * T + c * CC:
                                              hh * T + (c + 1) * CC],
                                    P2[:, j * T + c * CC:
                                       j * T + (c + 1) * CC],
                                    start=True, stop=True,
                                    skip_group_check=True).then_inc(s_u, 1)
                        t.wait_ge(s_pt, 8 * it + 2 * w + 2)
                        for j in range(2):
                            t.wait_ge(s_wd, 4 * W + 2 * j + 2)
                            if W >= 1:
                                t.wait_ge(s_tm, 2 * (W - 1) + j + 1)
                                t.wait_ge(s_ae, 2 * (W - 1) + j + 1)
                                t.wait_ge(s_mu, 6 * (W - 1) + 4 + j + 1)
                            for dc in range(4):
                                c = 4 * w + dc
                                po = j * 512 + dc * 128
                                t.matmul(psa[:, po:po + 128],
                                         Skkb[:, j * NCK * CC + c * CC:
                                              j * NCK * CC + (c + 1) * CC],
                                         Pt2[:, j * T + c * CC:
                                             j * T + (c + 1) * CC],
                                         start=True, stop=False,
                                         skip_group_check=True)
                                t.matmul(psa[:, po:po + 128],
                                         Lb2[:, j * 128:(j + 1) * 128],
                                         P2[:, j * T + c * CC:
                                            j * T + (c + 1) * CC],
                                         start=False, stop=False,
                                         skip_group_check=True)
                                t.matmul(psa[0:64, po:po + 128],
                                         kn4[:, (2 * j * NCK + c) * DK:
                                             (2 * j * NCK + c + 1) * DK],
                                         W4[:, 2 * j * 512 + dc * 128:
                                            2 * j * 512 + (dc + 1) * 128],
                                         start=False, stop=False,
                                         skip_group_check=True)
                                t.matmul(psa[64:128, po:po + 128],
                                         kn4[:, ((2 * j + 1) * NCK + c) * DK:
                                             ((2 * j + 1) * NCK + c + 1) * DK],
                                         W4[:, (2 * j + 1) * 512 + dc * 128:
                                            (2 * j + 1) * 512 + (dc + 1) * 128],
                                         start=False, stop=True,
                                         skip_group_check=True
                                         ).then_inc(s_ap, 1)
                        for j in range(2):
                            t.wait_ge(s_tm, 2 * W + j + 1)
                            if W >= 1:
                                t.wait_ge(s_rc, 2 * (W - 1) + j + 1)
                            t.matmul(pss[:, j * 512:(j + 1) * 512],
                                     P["blk2b"][:, :],
                                     tmb[:, j * 512:(j + 1) * 512],
                                     start=True, stop=True,
                                     skip_group_check=True).then_inc(s_pp, 1)
                        for j in range(2):
                            t.wait_ge(s_al, 2 * W + j + 1)
                            t.wait_ge(s_wd, 4 * W + j + 1)
                            t.matmul(psU[:, j * 512:(j + 1) * 512],
                                     P["bo2pb"][:, :],
                                     al2[:, j * 512:(j + 1) * 512],
                                     start=True, stop=True,
                                     skip_group_check=True).then_inc(s_bc, 1)
                        for j in range(2):
                            t.wait_ge(s_wd, 4 * W + 2 + j + 1)
                            t.matmul(psU[:, (2 + j) * 512:(3 + j) * 512],
                                     P["bo2nb"][:, :],
                                     al2[:, j * 512:(j + 1) * 512],
                                     start=True, stop=True,
                                     skip_group_check=True).then_inc(s_bc, 1)
                        for j in range(2):
                            t.wait_ge(s_r2, 2 * W + j + 1)
                            t.wait_ge(s_lnA, 4 * W + j + 1)
                            t.matmul(pss[:, j * 512:(j + 1) * 512],
                                     P["blk2b"][:, :],
                                     r2b[:, j * 512:(j + 1) * 512],
                                     start=True, stop=True,
                                     skip_group_check=True).then_inc(s_rs, 1)
                        for j in range(2):
                            t.wait_ge(s_be, 2 * W + j + 1)
                            t.wait_ge(s_tm, 2 * W + j + 1)
                            t.wait_ge(s_ae, 2 * W + j + 1)
                            t.matmul(psa[:, j * 512:(j + 1) * 512],
                                     P["bo2pb"][:, :],
                                     be2[:, j * 512:(j + 1) * 512],
                                     start=True, stop=True,
                                     skip_group_check=True).then_inc(s_bc, 1)

            @blk.vector
            def _(v):
                for it in range(NIT):
                    rsO = rsAB[it % 2]
                    rsN = rsAB[(it + 1) % 2]
                    for w in range(4):
                        W = 4 * it + w
                        for hh in range(4):
                            v.wait_ge(s_u, 16 * W + 4 * (hh + 1))
                            if W >= 1:
                                v.wait_ge(s_ap, 8 * (W - 1) + 8)
                            v.tensor_mul(W4[:, hh * 512:(hh + 1) * 512],
                                         A4[:, hh * NCK * CC + 4 * w * CC:
                                            hh * NCK * CC + 4 * (w + 1) * CC],
                                         psU[:, hh * 512:(hh + 1) * 512]
                                         ).then_inc(s_wd, 1)
                        for j in range(2):
                            v.wait_ge(s_ap, 8 * W + 4 * (j + 1))
                            if W >= 1:
                                v.wait_ge(s_pp, 2 * (W - 1) + j + 1)
                            v.tensor_mul(tmb[:, j * 512:(j + 1) * 512],
                                         P2[:, j * T + w * 512:
                                            j * T + (w + 1) * 512],
                                         psa[:, j * 512:(j + 1) * 512]
                                         ).then_inc(s_tm, 1)
                        for j in range(2):
                            v.wait_ge(s_expA, 4 * W + j + 1)
                            if W >= 1:
                                v.wait_ge(s_bc, 6 * (W - 1) + 4)
                            if W >= 4:
                                v.wait_ge(s_rc, 2 * (W - 4) + 2)
                            v.tensor_mul(al2[:, j * 512:(j + 1) * 512],
                                         rsO[0:2, j * T + w * 512:
                                             j * T + (w + 1) * 512],
                                         sc2[:, j * 512:(j + 1) * 512]
                                         ).then_inc(s_al, 1)
                        for j in range(2):
                            v.wait_ge(s_bc, 6 * W + j + 1)
                            if W >= 1:
                                v.wait_ge(s_ga, 6 * (W - 1) + j + 1)
                            v.tensor_mul(mub[:, (3 * j) * 512:
                                             (3 * j + 1) * 512],
                                         P2[:, j * T + w * 512:
                                            j * T + (w + 1) * 512],
                                         psU[:, j * 512:(j + 1) * 512]
                                         ).then_inc(s_mu, 1)
                        for j in range(2):
                            v.wait_ge(s_bc, 6 * W + 2 + j + 1)
                            v.wait_ge(s_ae, 2 * W + j + 1)
                            if W >= 1:
                                v.wait_ge(s_ga, 6 * (W - 1) + 2 + j + 1)
                            v.tensor_mul(mub[:, (3 * j + 1) * 512:
                                             (3 * j + 2) * 512],
                                         Apb[:, j * 512:(j + 1) * 512],
                                         psU[:, (2 + j) * 512:(3 + j) * 512]
                                         ).then_inc(s_mu, 1)
                        for j in range(2):
                            v.wait_ge(s_rc, 2 * W + j + 1)
                            v.wait_ge(s_expA, 4 * W + 2 + j + 1)
                            if W >= 1:
                                v.wait_ge(s_bc, 6 * (W - 1) + 6)
                            v.tensor_mul(be2[:, j * 512:(j + 1) * 512],
                                         rsN[0:2, j * T + w * 512:
                                             j * T + (w + 1) * 512],
                                         sc2[:, (2 + j) * 512:(3 + j) * 512]
                                         ).then_inc(s_be, 1)
                        for j in range(2):
                            v.wait_ge(s_bc, 6 * W + 4 + j + 1)
                            if W >= 1:
                                v.wait_ge(s_ga, 6 * (W - 1) + 4 + j + 1)
                            v.tensor_mul(mub[:, (3 * j + 2) * 512:
                                             (3 * j + 3) * 512],
                                         P2[:, j * T + w * 512:
                                            j * T + (w + 1) * 512],
                                         psa[:, j * 512:(j + 1) * 512]
                                         ).then_inc(s_mu, 1)

            @blk.scalar
            def _(s):
                for it in range(NIT):
                    rsO = rsAB[it % 2]
                    rsN = rsAB[(it + 1) % 2]
                    for w in range(4):
                        W = 4 * it + w
                        for j in range(2):
                            s.wait_ge(s_ap, 8 * W + 4 * (j + 1))
                            s.wait_ge(s_tm, 2 * W + j + 1)
                            if W >= 1:
                                s.wait_ge(s_mu, 6 * (W - 1) + 2 + j + 1)
                            s.copy(Apb[:, j * 512:(j + 1) * 512],
                                   psa[:, j * 512:(j + 1) * 512]
                                   ).then_inc(s_ae, 1)
                        for j in range(2):
                            s.wait_ge(s_pp, 2 * W + j + 1)
                            s.activation(sc1[:, j * 512:(j + 1) * 512],
                                         pss[:, j * 512:(j + 1) * 512], AF.Ln,
                                         bias=P["epsc"][0:2, 2:3]
                                         ).then_inc(s_lnA, 1)
                        for j in range(2):
                            s.wait_ge(s_lnA, 4 * W + j + 1)
                            if W >= 1:
                                s.wait_ge(s_al, 2 * (W - 1) + j + 1)
                            s.activation(sc2[:, j * 512:(j + 1) * 512],
                                         sc1[:, j * 512:(j + 1) * 512],
                                         AF.Exp,
                                         scale=-1.0).then_inc(s_expA, 1)
                        for j in range(2):
                            s.activation(sc1[:, (2 + j) * 512:(3 + j) * 512],
                                         rsO[0:2, j * T + w * 512:
                                             j * T + (w + 1) * 512], AF.Ln,
                                         bias=P["epsc"][0:2, 2:3]
                                         ).then_inc(s_lnA, 1)
                        for j in range(2):
                            s.wait_ge(s_lnA, 4 * W + 2 + j + 1)
                            if W >= 1:
                                s.wait_ge(s_be, 2 * (W - 1) + j + 1)
                            s.activation(sc2[:, (2 + j) * 512:(3 + j) * 512],
                                         sc1[:, (2 + j) * 512:(3 + j) * 512],
                                         AF.Exp, scale=-1.0
                                         ).then_inc(s_expA, 1)
                        for j in range(2):
                            s.wait_ge(s_rs, 2 * W + j + 1)
                            if W >= 4:
                                s.wait_ge(s_al, 2 * (W - 4) + 2)
                            s.copy(rsN[0:2, j * T + w * 512:
                                       j * T + (w + 1) * 512],
                                   pss[:, j * 512:(j + 1) * 512]
                                   ).then_inc(s_rc, 1)

            @blk.gpsimd
            def _(gp):
                for w in range(4):
                    for j in range(2):
                        gp.tensor_mul(Pt2[:, j * T + w * 512:
                                          j * T + (w + 1) * 512],
                                      P2[:, j * T + w * 512:
                                         j * T + (w + 1) * 512],
                                      D2[:, j * T + w * 512:
                                         j * T + (w + 1) * 512]
                                      ).then_inc(s_pt, 1)
                for it in range(NIT):
                    for w in range(4):
                        W = 4 * it + w
                        for j in range(2):
                            gp.wait_ge(s_mu, 6 * W + j + 1)
                            gp.tensor_add(X2[:, j * T + w * 512:
                                             j * T + (w + 1) * 512],
                                          X2[:, j * T + w * 512:
                                             j * T + (w + 1) * 512],
                                          mub[:, (3 * j) * 512:
                                              (3 * j + 1) * 512]
                                          ).then_inc(s_ga, 1)
                        for j in range(2):
                            gp.wait_ge(s_mu, 6 * W + 2 + j + 1)
                            gp.tensor_add(R2[:, j * T + w * 512:
                                             j * T + (w + 1) * 512],
                                          R2[:, j * T + w * 512:
                                             j * T + (w + 1) * 512],
                                          mub[:, (3 * j + 1) * 512:
                                              (3 * j + 2) * 512]
                                          ).then_inc(s_ga, 1)
                        for j in range(2):
                            if W >= 1:
                                gp.wait_ge(s_rs, 2 * (W - 1) + j + 1)
                            gp.wait_ge(s_ga, 6 * W + 2 + j + 1)
                            gp.tensor_mul(r2b[:, j * 512:(j + 1) * 512],
                                          R2[:, j * T + w * 512:
                                             j * T + (w + 1) * 512],
                                          R2[:, j * T + w * 512:
                                             j * T + (w + 1) * 512]
                                          ).then_inc(s_r2, 1)
                        for j in range(2):
                            gp.wait_ge(s_mu, 6 * W + 4 + j + 1)
                            gp.wait_ge(s_ga, 6 * W + 2 + j + 1)
                            gp.tensor_add(P2[:, j * T + w * 512:
                                             j * T + (w + 1) * 512],
                                          R2[:, j * T + w * 512:
                                             j * T + (w + 1) * 512],
                                          mub[:, (3 * j + 2) * 512:
                                              (3 * j + 3) * 512]
                                          ).then_inc(s_ga, 1)
                        for j in range(2):
                            gp.wait_ge(s_ga, 6 * W + 4 + j + 1)
                            gp.tensor_mul(Pt2[:, j * T + w * 512:
                                              j * T + (w + 1) * 512],
                                          P2[:, j * T + w * 512:
                                             j * T + (w + 1) * 512],
                                          D2[:, j * T + w * 512:
                                             j * T + (w + 1) * 512]
                                          ).then_inc(s_pt, 1)


def cg_emit_iter(kb, g, b, hbase, bf, it, rsO, rsN):
    nc = kb.nc
    P = kb.P
    kn4, A4 = bf["kn4"], bf["A4"]
    D2, X2, R2, P2, Pt2 = bf["D2"], bf["X2"], bf["R2"], bf["P2"], bf["Pt2"]
    Skkb, Lb2 = bf["Skkb"], bf["Lb2"]
    W4, tmb, Apb, mub, r2b = (bf["W4"], bf["tmb"], bf["Apb"], bf["mub"],
                              bf["r2b"])
    al2, be2, sc1, sc2 = bf["al2"], bf["be2"], bf["sc1"], bf["sc2"]

    with (nc.psum_tensor(f"cg_psU{g}_{it}", [128, 2048], PS) as psU,
          nc.psum_tensor(f"cg_psa{g}_{it}", [128, 1024], PS) as psa,
          nc.psum_tensor(f"cg_pss{g}_{it}", [2, 1024], PS) as pss,
          SemScope(kb, 17, "cgit") as sems):
        (s_u, s_wd, s_ap, s_tm, s_ae, s_pp, s_lnA, s_expA, s_al, s_bc,
         s_mu, s_pt, s_r2, s_rs, s_rc, s_be, s_ga) = sems

        with nc.Block() as blk:
            @blk.tensor
            def _(t):
                for w in range(4):
                    for hh in range(4):
                        j, base = hh // 2, (hh % 2) * 64
                        if w >= 1:
                            t.wait_ge(s_wd, 4 * (w - 1) + hh + 1)
                            t.wait_ge(s_mu, 6 * (w - 1) + 6)
                        for dc in range(4):
                            c = 4 * w + dc
                            t.matmul(
                                psU[:, hh * 512 + dc * 128:
                                    hh * 512 + (dc + 1) * 128],
                                bf["ktm"][:, hh * T + c * CC:
                                          hh * T + (c + 1) * CC],
                                P2[:, j * T + c * CC:j * T + (c + 1) * CC],
                                start=True, stop=True,
                                skip_group_check=True).then_inc(s_u, 1)
                    t.wait_ge(s_pt, 2)
                    for j in range(2):
                        t.wait_ge(s_wd, 4 * w + 2 * j + 2)
                        if w >= 1:
                            t.wait_ge(s_tm, 2 * (w - 1) + j + 1)
                            t.wait_ge(s_ae, 2 * (w - 1) + j + 1)
                            t.wait_ge(s_mu, 6 * (w - 1) + 4 + j + 1)
                        for dc in range(4):
                            c = 4 * w + dc
                            po = j * 512 + dc * 128
                            t.matmul(psa[:, po:po + 128],
                                     Skkb[:, j * NCK * CC + c * CC:
                                          j * NCK * CC + (c + 1) * CC],
                                     Pt2[:, j * T + c * CC:
                                         j * T + (c + 1) * CC],
                                     start=True, stop=False,
                                     skip_group_check=True)
                            t.matmul(psa[:, po:po + 128],
                                     Lb2[:, j * 128:(j + 1) * 128],
                                     P2[:, j * T + c * CC:
                                        j * T + (c + 1) * CC],
                                     start=False, stop=False,
                                     skip_group_check=True)
                            t.matmul(psa[0:64, po:po + 128],
                                     kn4[:, (2 * j * NCK + c) * DK:
                                         (2 * j * NCK + c + 1) * DK],
                                     W4[:, 2 * j * 512 + dc * 128:
                                        2 * j * 512 + (dc + 1) * 128],
                                     start=False, stop=False,
                                     skip_group_check=True)
                            t.matmul(psa[64:128, po:po + 128],
                                     kn4[:, ((2 * j + 1) * NCK + c) * DK:
                                         ((2 * j + 1) * NCK + c + 1) * DK],
                                     W4[:, (2 * j + 1) * 512 + dc * 128:
                                        (2 * j + 1) * 512 + (dc + 1) * 128],
                                     start=False, stop=True,
                                     skip_group_check=True
                                     ).then_inc(s_ap, 1)
                    for j in range(2):
                        t.wait_ge(s_tm, 2 * w + j + 1)
                        if w >= 1:
                            t.wait_ge(s_rc, 2 * (w - 1) + j + 1)
                        t.matmul(pss[:, j * 512:(j + 1) * 512],
                                 P["blk2b"][:, :],
                                 tmb[:, j * 512:(j + 1) * 512],
                                 start=True, stop=True,
                                 skip_group_check=True).then_inc(s_pp, 1)
                    for j in range(2):
                        t.wait_ge(s_al, 2 * w + j + 1)
                        t.wait_ge(s_wd, 4 * w + j + 1)
                        t.matmul(psU[:, j * 512:(j + 1) * 512],
                                 P["bo2pb"][:, :],
                                 al2[:, j * 512:(j + 1) * 512],
                                 start=True, stop=True,
                                 skip_group_check=True).then_inc(s_bc, 1)
                    for j in range(2):
                        t.wait_ge(s_wd, 4 * w + 2 + j + 1)
                        t.matmul(psU[:, (2 + j) * 512:(3 + j) * 512],
                                 P["bo2nb"][:, :],
                                 al2[:, j * 512:(j + 1) * 512],
                                 start=True, stop=True,
                                 skip_group_check=True).then_inc(s_bc, 1)
                    for j in range(2):
                        t.wait_ge(s_r2, 2 * w + j + 1)
                        t.wait_ge(s_lnA, 4 * w + j + 1)
                        t.matmul(pss[:, j * 512:(j + 1) * 512],
                                 P["blk2b"][:, :],
                                 r2b[:, j * 512:(j + 1) * 512],
                                 start=True, stop=True,
                                 skip_group_check=True).then_inc(s_rs, 1)
                    for j in range(2):
                        t.wait_ge(s_be, 2 * w + j + 1)
                        t.wait_ge(s_tm, 2 * w + j + 1)
                        t.wait_ge(s_ae, 2 * w + j + 1)
                        t.matmul(psa[:, j * 512:(j + 1) * 512],
                                 P["bo2pb"][:, :],
                                 be2[:, j * 512:(j + 1) * 512],
                                 start=True, stop=True,
                                 skip_group_check=True).then_inc(s_bc, 1)

            @blk.vector
            def _(v):
                for w in range(4):
                    for hh in range(4):
                        v.wait_ge(s_u, 16 * w + 4 * (hh + 1))
                        if w >= 1:
                            v.wait_ge(s_ap, 8 * (w - 1) + 8)
                        v.tensor_mul(W4[:, hh * 512:(hh + 1) * 512],
                                     A4[:, hh * NCK * CC + 4 * w * CC:
                                        hh * NCK * CC + 4 * (w + 1) * CC],
                                     psU[:, hh * 512:(hh + 1) * 512]
                                     ).then_inc(s_wd, 1)
                    for j in range(2):
                        v.wait_ge(s_ap, 8 * w + 4 * (j + 1))
                        if w >= 1:
                            v.wait_ge(s_pp, 2 * (w - 1) + j + 1)
                        v.tensor_mul(tmb[:, j * 512:(j + 1) * 512],
                                     P2[:, j * T + w * 512:
                                        j * T + (w + 1) * 512],
                                     psa[:, j * 512:(j + 1) * 512]
                                     ).then_inc(s_tm, 1)
                    for j in range(2):
                        v.wait_ge(s_expA, 4 * w + j + 1)
                        if w >= 1:
                            v.wait_ge(s_bc, 6 * (w - 1) + 4)
                        v.tensor_mul(al2[:, j * 512:(j + 1) * 512],
                                     rsO[0:2, j * T + w * 512:
                                         j * T + (w + 1) * 512],
                                     sc2[:, j * 512:(j + 1) * 512]
                                     ).then_inc(s_al, 1)
                    for j in range(2):
                        v.wait_ge(s_bc, 6 * w + j + 1)
                        if w >= 1:
                            v.wait_ge(s_ga, 6 * (w - 1) + j + 1)
                        v.tensor_mul(mub[:, (3 * j) * 512:(3 * j + 1) * 512],
                                     P2[:, j * T + w * 512:
                                        j * T + (w + 1) * 512],
                                     psU[:, j * 512:(j + 1) * 512]
                                     ).then_inc(s_mu, 1)
                    for j in range(2):
                        v.wait_ge(s_bc, 6 * w + 2 + j + 1)
                        v.wait_ge(s_ae, 2 * w + j + 1)
                        if w >= 1:
                            v.wait_ge(s_ga, 6 * (w - 1) + 2 + j + 1)
                        v.tensor_mul(mub[:, (3 * j + 1) * 512:
                                         (3 * j + 2) * 512],
                                     Apb[:, j * 512:(j + 1) * 512],
                                     psU[:, (2 + j) * 512:(3 + j) * 512]
                                     ).then_inc(s_mu, 1)
                    for j in range(2):
                        v.wait_ge(s_rc, 2 * w + j + 1)
                        v.wait_ge(s_expA, 4 * w + 2 + j + 1)
                        if w >= 1:
                            v.wait_ge(s_bc, 6 * (w - 1) + 6)
                        v.tensor_mul(be2[:, j * 512:(j + 1) * 512],
                                     rsN[0:2, j * T + w * 512:
                                         j * T + (w + 1) * 512],
                                     sc2[:, (2 + j) * 512:(3 + j) * 512]
                                     ).then_inc(s_be, 1)
                    for j in range(2):
                        v.wait_ge(s_bc, 6 * w + 4 + j + 1)
                        if w >= 1:
                            v.wait_ge(s_ga, 6 * (w - 1) + 4 + j + 1)
                        v.tensor_mul(mub[:, (3 * j + 2) * 512:
                                         (3 * j + 3) * 512],
                                     P2[:, j * T + w * 512:
                                        j * T + (w + 1) * 512],
                                     psa[:, j * 512:(j + 1) * 512]
                                     ).then_inc(s_mu, 1)

            @blk.scalar
            def _(s):
                for w in range(4):
                    for j in range(2):
                        s.wait_ge(s_ap, 8 * w + 4 * (j + 1))
                        s.wait_ge(s_tm, 2 * w + j + 1)
                        if w >= 1:
                            s.wait_ge(s_mu, 6 * (w - 1) + 2 + j + 1)
                        s.copy(Apb[:, j * 512:(j + 1) * 512],
                               psa[:, j * 512:(j + 1) * 512]).then_inc(s_ae, 1)
                    for j in range(2):
                        s.wait_ge(s_pp, 2 * w + j + 1)
                        s.activation(sc1[:, j * 512:(j + 1) * 512],
                                     pss[:, j * 512:(j + 1) * 512], AF.Ln,
                                     bias=P["epsc"][0:2, 2:3]
                                     ).then_inc(s_lnA, 1)
                    for j in range(2):
                        s.wait_ge(s_lnA, 4 * w + j + 1)
                        if w >= 1:
                            s.wait_ge(s_al, 2 * (w - 1) + j + 1)
                        s.activation(sc2[:, j * 512:(j + 1) * 512],
                                     sc1[:, j * 512:(j + 1) * 512], AF.Exp,
                                     scale=-1.0).then_inc(s_expA, 1)
                    for j in range(2):
                        s.activation(sc1[:, (2 + j) * 512:(3 + j) * 512],
                                     rsO[0:2, j * T + w * 512:
                                         j * T + (w + 1) * 512], AF.Ln,
                                     bias=P["epsc"][0:2, 2:3]
                                     ).then_inc(s_lnA, 1)
                    for j in range(2):
                        s.wait_ge(s_lnA, 4 * w + 2 + j + 1)
                        if w >= 1:
                            s.wait_ge(s_be, 2 * (w - 1) + j + 1)
                        s.activation(sc2[:, (2 + j) * 512:(3 + j) * 512],
                                     sc1[:, (2 + j) * 512:(3 + j) * 512],
                                     AF.Exp, scale=-1.0).then_inc(s_expA, 1)
                    for j in range(2):
                        if w >= 1:
                            s.wait_ge(s_rs, 2 * (w - 1) + j + 1)
                        s.wait_ge(s_ga, 6 * w + 2 + j + 1)
                        s.activation(r2b[:, j * 512:(j + 1) * 512],
                                     bf["R2"][:, j * T + w * 512:
                                              j * T + (w + 1) * 512],
                                     AF.Square).then_inc(s_r2, 1)
                    for j in range(2):
                        s.wait_ge(s_rs, 2 * w + j + 1)
                        s.copy(rsN[0:2, j * T + w * 512:j * T + (w + 1) * 512],
                               pss[:, j * 512:(j + 1) * 512]).then_inc(s_rc, 1)

            @blk.gpsimd
            def _(gp):
                for j in range(2):
                    gp.tensor_mul(Pt2[:, j * T:(j + 1) * T],
                                  P2[:, j * T:(j + 1) * T],
                                  D2[:, j * T:(j + 1) * T]).then_inc(s_pt, 1)
                for w in range(4):
                    for j in range(2):
                        gp.wait_ge(s_mu, 6 * w + j + 1)
                        gp.tensor_add(X2[:, j * T + w * 512:
                                         j * T + (w + 1) * 512],
                                      X2[:, j * T + w * 512:
                                         j * T + (w + 1) * 512],
                                      mub[:, (3 * j) * 512:(3 * j + 1) * 512]
                                      ).then_inc(s_ga, 1)
                    for j in range(2):
                        gp.wait_ge(s_mu, 6 * w + 2 + j + 1)
                        gp.tensor_add(R2[:, j * T + w * 512:
                                         j * T + (w + 1) * 512],
                                      R2[:, j * T + w * 512:
                                         j * T + (w + 1) * 512],
                                      mub[:, (3 * j + 1) * 512:
                                          (3 * j + 2) * 512]).then_inc(s_ga, 1)
                    for j in range(2):
                        gp.wait_ge(s_mu, 6 * w + 4 + j + 1)
                        gp.wait_ge(s_ga, 6 * w + 2 + j + 1)
                        gp.tensor_add(P2[:, j * T + w * 512:
                                         j * T + (w + 1) * 512],
                                      R2[:, j * T + w * 512:
                                         j * T + (w + 1) * 512],
                                      mub[:, (3 * j + 2) * 512:
                                          (3 * j + 3) * 512]).then_inc(s_ga, 1)


def cg_emit_out(kb, g, b, hbase, bf):
    nc = kb.nc
    P = kb.P
    d = kb.dram
    kn4, vn4, A4 = bf["kn4"], bf["vn4"], bf["A4"]
    D2, X2, Pt2 = bf["D2"], bf["X2"], bf["Pt2"]
    Skvb, W4, ob2 = bf["Skvb"], bf["W4"], bf["ob2"]
    st = kb.stream("S")

    with (nc.psum_tensor(f"cg_psO{g}", [128, 2048], PS) as psU,
          nc.psum_tensor(f"cg_pso{g}", [128, 1024], PS) as psa,
          SemScope(kb, 5, "cgo") as (s_x, s_u, s_wd, s_ap, s_ev)):
        with nc.Block() as blk:
            @blk.gpsimd
            def _(gp):
                for j in range(2):
                    gp.tensor_mul(Pt2[:, j * T:(j + 1) * T],
                                  X2[:, j * T:(j + 1) * T],
                                  D2[:, j * T:(j + 1) * T]).then_inc(s_x, 1)

            @blk.tensor
            def _(t):
                t.wait_ge(s_x, 2)
                for w in range(4):
                    for hh in range(4):
                        j, base = hh // 2, (hh % 2) * 64
                        if w >= 1:
                            t.wait_ge(s_wd, 4 * (w - 1) + hh + 1)
                        for dc in range(4):
                            c = 4 * w + dc
                            t.matmul(
                                psU[:, hh * 512 + dc * 128:
                                    hh * 512 + (dc + 1) * 128],
                                bf["ktm"][:, hh * T + c * CC:
                                          hh * T + (c + 1) * CC],
                                X2[:, j * T + c * CC:j * T + (c + 1) * CC],
                                start=True, stop=True,
                                skip_group_check=True).then_inc(s_u, 1)
                    for j in range(2):
                        t.wait_ge(s_wd, 4 * w + 2 * j + 2)
                        if w >= 1:
                            t.wait_ge(s_ev, 2 * (w - 1) + j + 1)
                        for dc in range(4):
                            c = 4 * w + dc
                            po = j * 512 + dc * 128
                            t.matmul(psa[:, po:po + 128],
                                     Skvb[:, j * NCK * CC + c * CC:
                                          j * NCK * CC + (c + 1) * CC],
                                     Pt2[:, j * T + c * CC:
                                         j * T + (c + 1) * CC],
                                     start=True, stop=False,
                                     skip_group_check=True)
                            t.matmul(psa[0:64, po:po + 128],
                                     vn4[:, (2 * j * NCK + c) * DK:
                                         (2 * j * NCK + c + 1) * DK],
                                     W4[:, 2 * j * 512 + dc * 128:
                                        2 * j * 512 + (dc + 1) * 128],
                                     start=False, stop=False,
                                     skip_group_check=True)
                            ins = t.matmul(
                                     psa[64:128, po:po + 128],
                                     vn4[:, ((2 * j + 1) * NCK + c) * DK:
                                         ((2 * j + 1) * NCK + c + 1) * DK],
                                     W4[:, (2 * j + 1) * 512 + dc * 128:
                                        (2 * j + 1) * 512 + (dc + 1) * 128],
                                     start=False, stop=True,
                                     skip_group_check=True)
                        ins.then_inc(s_ap, 1)

            @blk.vector
            def _(v):
                for w in range(4):
                    for hh in range(4):
                        v.wait_ge(s_u, 16 * w + 4 * (hh + 1))
                        if w >= 1:
                            v.wait_ge(s_ap, 2 * (w - 1) + 2)
                        v.tensor_mul(W4[:, hh * 512:(hh + 1) * 512],
                                     A4[:, hh * NCK * CC + 4 * w * CC:
                                        hh * NCK * CC + 4 * (w + 1) * CC],
                                     psU[:, hh * 512:(hh + 1) * 512]
                                     ).then_inc(s_wd, 1)
                    for j in range(2):
                        v.wait_ge(s_ap, 2 * w + j + 1)
                        v.tensor_copy(ob2[:, j * T + w * 512:
                                          j * T + (w + 1) * 512],
                                      psa[:, j * 512:(j + 1) * 512]
                                      ).then_inc(s_ev, 1)

            @blk.sync
            def _(sy):
                sy.wait_ge(s_ev, 8)
                for j in range(2):
                    hp = hbase // 2 + j
                    st.issue(sy, d["mo"][b, hp], ob2[:, j * T:(j + 1) * T])
                st.wait_all(sy)







def mo_emit(kb):
    nc = kb.nc
    P = kb.P
    d = kb.dram
    ld = kb.stream("S")

    with ExitStack() as stk:
        def mk(name, shape, dt):
            return stk.enter_context(nc.sbuf_tensor(name, shape, dt))
        mo_sb = mk("mo_m", [128, NCH * T], BF16)
        wom_sb = mk("mo_w", [128, NCH * D], BF16)
        sqv = mk("mo_sq", [128, 2 * T], BF16)
        ssq = mk("mo_ssq", [16, T], F32)
        rstd = mk("mo_rstd", [16, T], BF16)
        ybuf = mk("mo_y", [128, NCH * T], F32)
        sqf = mk("mo_sqf", [128, 2 * T], F32)
        rowb = mk("mo_row", [1, 3 * T], F32)
        zcol = mk("mo_z", [128, 1], F32)

        gld = kb.stream("G", depth=8)
        with nc.Block() as blk:
            @blk.gpsimd
            def _(g):
                for c in range(NCH):
                    gld.issue(g, wom_sb[:, c * D:(c + 1) * D], d["wom"][c])
                g.memset(zcol[:, :], 0.0)
                gld.wait_all(g)

        for b in range(B):
            mo_emit_b(kb, b, mo_sb, wom_sb, sqv, ssq, rstd, ybuf, sqf, rowb,
                   zcol, ld)


def mo_emit_b(kb, b, mo_sb, wom_sb, sqv, ssq, rstd, ybuf, sqf, rowb, zcol, ld):
    nc = kb.nc
    P = kb.P
    d = kb.dram

    with nc.Block() as blk:
        @blk.sync
        def _(sy):
            for c in range(NCH):
                ld.issue(sy, mo_sb[:, c * T:(c + 1) * T], d["mo"][b, c])
            ld.wait_all(sy)

    with (nc.psum_tensor(f"mo_ps{b}", [16, 2048], PS) as ssps,
          SemScope(kb, 2, "moA") as (s1, s2)):
        with nc.Block() as blk:
            @blk.gpsimd
            def _(gp):
                for c in range(NCH):
                    if c >= 2:
                        gp.wait_ge(s2, c - 1)
                    gp.tensor_mul(sqv[:, (c % 2) * T:(c % 2 + 1) * T],
                                  mo_sb[:, c * T:(c + 1) * T],
                                  mo_sb[:, c * T:(c + 1) * T]).then_inc(s1, 1)

            @blk.tensor
            def _(t):
                for c in range(NCH):
                    t.wait_ge(s1, c + 1)
                    for tt in range(4):
                        ins = t.matmul(ssps[:, tt * 512:(tt + 1) * 512],
                                       P["sel8b"][:, c * 16:(c + 1) * 16],
                                       sqv[:, (c % 2) * T + tt * 512:
                                           (c % 2) * T + (tt + 1) * 512],
                                       start=(c == 0), stop=(c == NCH - 1),
                                       skip_group_check=True)
                    ins.then_inc(s2, 1)
        with nc.Block() as blk:
            @blk.vector
            def _(v):
                for tt in range(4):
                    v.tensor_copy(ssq[:, tt * 512:(tt + 1) * 512],
                                  ssps[:, tt * 512:(tt + 1) * 512])
        with nc.Block() as blk:
            @blk.scalar
            def _(s):
                s.activation(ssq[:, :], ssq[:, :], AF.Ln,
                             bias=P["epsc"][0:16, 1:2], scale=1.0 / 64)
        with nc.Block() as blk:
            @blk.scalar
            def _(s):
                s.activation(rstd[:, :], ssq[:, :], AF.Exp, scale=-0.5)

    with (nc.psum_tensor(f"mo_pb{b}", [128, 2048], PS) as bps,
          SemScope(kb, 3, "moB") as (s1, s2, s3)):
        with nc.Block() as blk:
            @blk.tensor
            def _(t):
                for c in range(NCH):
                    for tt in range(4):
                        slot = c * 4 + tt
                        if slot >= 4:
                            t.wait_ge(s2, slot - 3)
                        t.matmul(bps[:, tt * 512:(tt + 1) * 512],
                                 P["selT8b"][:, c * 128:(c + 1) * 128],
                                 rstd[:, tt * 512:(tt + 1) * 512],
                                 start=True, stop=True,
                                 skip_group_check=True).then_inc(s1, 1)

            @blk.vector
            def _(v):
                for c in range(NCH):
                    for tt in range(4):
                        slot = c * 4 + tt
                        v.wait_ge(s1, slot + 1)
                        v.tensor_mul(mo_sb[:, c * T + tt * 512:
                                           c * T + (tt + 1) * 512],
                                     mo_sb[:, c * T + tt * 512:
                                           c * T + (tt + 1) * 512],
                                     bps[:, tt * 512:(tt + 1) * 512]
                                     ).then_inc(s2, 1)

            @blk.gpsimd
            def _(gp):
                for c in range(NCH):
                    gp.wait_ge(s2, c * 4 + 4)
                    gp.tensor_scalar(mo_sb[:, c * T:(c + 1) * T],
                                     mo_sb[:, c * T:(c + 1) * T],
                                     P["normw2"][:, 0:1], None,
                                     AluOpType.mult).then_inc(s3, 1)

    with (nc.psum_tensor(f"mo_pw{b}", [128, 2048], PS) as wps,
          SemScope(kb, 3, "moC") as (s1, s2, s3)):
        xld = kb.stream("S")
        x_base = xld.i
        with nc.Block() as blk:
            @blk.sync
            def _(sy):
                for c in range(NCH):
                    xld.issue(sy, ybuf[:, c * T:(c + 1) * T], d["x_in"][b, c])

            @blk.tensor
            def _(t):
                for c in range(NCH):
                    for tt in range(4):
                        slot = c * 4 + tt
                        if slot >= 4:
                            t.wait_ge(s2, slot - 3)
                        for kc in range(NCH):
                            ins = t.matmul(
                                wps[:, tt * 512:(tt + 1) * 512],
                                wom_sb[:, kc * D + c * 128:
                                       kc * D + (c + 1) * 128],
                                mo_sb[:, kc * T + tt * 512:
                                      kc * T + (tt + 1) * 512],
                                start=(kc == 0), stop=(kc == NCH - 1))
                        ins.then_inc(s1, 1)

            @blk.vector
            def _(v):
                for c in range(NCH):
                    xld.wait(v, x_base + c + 1)
                    for tt in range(4):
                        slot = c * 4 + tt
                        v.wait_ge(s1, slot + 1)
                        v.tensor_add(ybuf[:, c * T + tt * 512:
                                          c * T + (tt + 1) * 512],
                                     ybuf[:, c * T + tt * 512:
                                          c * T + (tt + 1) * 512],
                                     wps[:, tt * 512:(tt + 1) * 512]
                                     ).then_inc(s2, 1)

    postln(kb, b, f"mo{b}", ybuf, rowb, "strA", sqf,
           dst2name="x1s")

    with nc.Block() as blk:
        @blk.sync
        def _(sy):
            st = kb.stream("S")
            for c in range(NCH):
                st.issue(sy, kb.dram["strA"][b, c, :, 0:1], zcol[:, :], slow=True)
            st.wait_all(sy)


def postln(kb, b, tag, ybuf, rowb, dstname, sqf, dst_off=1,
           dst2name=None):
    nc = kb.nc
    P = kb.P
    d = kb.dram
    st = kb.stream("S")

    with (nc.psum_tensor(f"ln_p1{tag}", [1, 2048], PS) as sp1,
          nc.psum_tensor(f"ln_p2{tag}", [1, 2048], PS) as sp2,
          SemScope(kb, 2, "lnQ") as (q1, q2)):
        with nc.Block() as blk:
            @blk.gpsimd
            def _(gp):
                for c in range(NCH):
                    if c >= 2:
                        gp.wait_ge(q2, c - 1)
                    gp.tensor_mul(sqf[:, (c % 2) * T:(c % 2 + 1) * T],
                                  ybuf[:, c * T:(c + 1) * T],
                                  ybuf[:, c * T:(c + 1) * T]).then_inc(q1, 1)

            @blk.tensor
            def _(t):
                for c in range(NCH):
                    for tt in range(4):
                        t.matmul(sp1[0:1, tt * 512:(tt + 1) * 512],
                                 P["onesf"][:, 0:1],
                                 ybuf[:, c * T + tt * 512:
                                      c * T + (tt + 1) * 512],
                                 start=(c == 0), stop=(c == NCH - 1),
                                 skip_group_check=True)
                for c in range(NCH):
                    t.wait_ge(q1, c + 1)
                    for tt in range(4):
                        ins = t.matmul(sp2[0:1, tt * 512:(tt + 1) * 512],
                                       P["onesf"][:, 0:1],
                                       sqf[:, (c % 2) * T + tt * 512:
                                           (c % 2) * T + (tt + 1) * 512],
                                       start=(c == 0), stop=(c == NCH - 1),
                                       skip_group_check=True)
                    ins.then_inc(q2, 1)
        with nc.Block() as blk:
            @blk.vector
            def _(v):
                v.tensor_scalar(rowb[0:1, 0:T], sp1[0:1, :], 1.0 / D, None,
                                AluOpType.mult)
                v.tensor_scalar(rowb[0:1, T:2 * T], sp2[0:1, :], 1.0 / D,
                                None, AluOpType.mult)
        with nc.Block() as blk:
            @blk.gpsimd
            def _(gp):
                gp.tensor_mul(rowb[0:1, 2 * T:3 * T], rowb[0:1, 0:T],
                              rowb[0:1, 0:T])
        with nc.Block() as blk:
            @blk.vector
            def _(v):
                v.tensor_sub(rowb[0:1, T:2 * T], rowb[0:1, T:2 * T],
                             rowb[0:1, 2 * T:3 * T])
        with nc.Block() as blk:
            @blk.scalar
            def _(s):
                s.activation(rowb[0:1, T:2 * T], rowb[0:1, T:2 * T], AF.Ln,
                             bias=P["epsc"][0:1, 1:2])
        with nc.Block() as blk:
            @blk.scalar
            def _(s):
                s.activation(rowb[0:1, T:2 * T], rowb[0:1, T:2 * T], AF.Exp,
                             scale=-0.5)

    with (nc.psum_tensor(f"ln_pb{tag}", [128, 2048], PS) as bps,
          nc.psum_tensor(f"ln_pm{tag}", [128, 2048], PS) as mps,
          SemScope(kb, 3, "lnR") as (r1, r2, r3)):
        with nc.Block() as blk:
            @blk.tensor
            def _(t):
                for tt in range(4):
                    t.matmul(mps[:, tt * 512:(tt + 1) * 512],
                             P["onesf"][0:1, :],
                             rowb[0:1, tt * 512:(tt + 1) * 512],
                             start=True, stop=True,
                             skip_group_check=True)
                    t.matmul(bps[:, tt * 512:(tt + 1) * 512],
                             P["onesf"][0:1, :],
                             rowb[0:1, T + tt * 512:T + (tt + 1) * 512],
                             start=True, stop=True,
                             skip_group_check=True).then_inc(r1, 1)

            @blk.vector
            def _(v):
                for c in range(NCH):
                    for tt in range(4):
                        v.wait_ge(r1, tt + 1)
                        v.tensor_sub(ybuf[:, c * T + tt * 512:
                                          c * T + (tt + 1) * 512],
                                     ybuf[:, c * T + tt * 512:
                                          c * T + (tt + 1) * 512],
                                     mps[:, tt * 512:(tt + 1) * 512]
                                     ).then_inc(r3, 1)
                for c in range(NCH):
                    for tt in range(4):
                        v.wait_ge(r3, c * 4 + tt + 1)
                        ins = v.tensor_mul(
                            ybuf[:, c * T + tt * 512:c * T + (tt + 1) * 512],
                            ybuf[:, c * T + tt * 512:c * T + (tt + 1) * 512],
                            bps[:, tt * 512:(tt + 1) * 512])
                    ins.then_inc(r2, 1)

            @blk.sync
            def _(sy):
                for c in range(NCH):
                    sy.wait_ge(r2, c + 1)
                    st.issue(sy, d[dstname][b, c, :,
                                            dst_off:dst_off + T],
                             ybuf[:, c * T:(c + 1) * T])
                    if dst2name is not None:
                        st.issue(sy, d[dst2name][b, c],
                                 ybuf[:, c * T:(c + 1) * T])
                st.wait_all(sy)








def tr_emit(kb):
    nc = kb.nc
    P = kb.P
    d = kb.dram
    with ExitStack() as stk:
        def mk(name, shape, dt):
            return stk.enter_context(nc.sbuf_tensor(name, shape, dt))
        g = dict(
            qsc=mk("tr_qsc", [128, 4], F32),
            mky=mk("tr_mk", [128, 2 * 512], F32),
            mk0=mk("tr_mk0", [128, 2 * 256], F32),
            xst=mk("tr_xst", [128, NCH * 513], F32),
            xbt=mk("tr_xb", [128, NCH * 513], BF16),
            sq=mk("tr_sq", [128, 2 * TT], BF16),
            ssq=mk("tr_ssq", [16, 2 * TT], F32),
            rstd=mk("tr_rstd", [16, 2 * TT], BF16),
            psb=mk("tr_psb", [128, 4 * 512], BF16),
            yst=mk("tr_yst", [128, 2 * TT], F32),
            zc=mk("tr_zc", [128, 1], F32),
        )
        g["tkA"] = {}
        g["tkB"] = {}
        with nc.Block() as blk:
            @blk.sync
            def _(sy):
                ld = kb.stream("S")
                ld.issue(sy, g["mky"][:, 0:512], d["wmask"][0])
                ld.issue(sy, g["mky"][:, 512:1024], d["wmask"][1])
                ld.issue(sy, g["mk0"][:, 0:256], d["wmask0"][0])
                ld.issue(sy, g["mk0"][:, 256:512], d["wmask0"][1])
                ld.wait_all(sy)

            @blk.gpsimd
            def _(gp):
                gp.memset(g["zc"][:, :], 0.0)

        for l in range(NL):
            tr_emit_attn_layer(kb, l, g)
            tr_emit_ffn_layer(kb, l, g)


def tr_emit_attn_layer(kb, l, gouter, wq_sb=None, wo_sb=None):
    nc = kb.nc
    P = kb.P
    d = kb.dram
    wl_ = kb.stream("G", depth=8)
    stk = ExitStack()
    stk.__enter__()

    def mk(name, shape, dt):
        return stk.enter_context(nc.sbuf_tensor(name, shape, dt))
    g = dict(gouter)
    g.update(
        csql=mk(f"tr_csql{l}", [128, T], BF16),
        ssql=mk(f"tr_ssql{l}", [128, T], BF16),
        cikl=mk(f"tr_cikl{l}", [128, T], BF16),
        sikl=mk(f"tr_sikl{l}", [128, T], BF16),
        qp=mk(f"tr_qp{l}", [128, NCH * TT], BF16),
        kp=mk(f"tr_kp{l}", [128, NCH * TT], BF16),
        kprev=mk(f"tr_kprev{l}", [128, NCH * 256], BF16),
        vt=mk(f"tr_vt{l}", [128, H * 256], BF16),
        vtprev=mk(f"tr_vtp{l}", [128, H * 128], BF16),
        ao=mk(f"tr_ao{l}", [128, NCH * TT], BF16),
        ml=mk(f"tr_ml{l}", [128, 4 * 512], F32),
        ptb=mk(f"tr_ptb{l}", [128, 2 * 1024], BF16),
        mx=mk(f"tr_mx{l}", [128, 16], F32),
    )
    g["qraw"] = g["ao"]
    wq_sb = mk(f"tr_wqkv{l}", [128, NCH * 3 * D], BF16)
    wo_sb = mk(f"tr_wo{l}", [128, NCH * D], BF16)

    with nc.Block() as blk:
        @blk.gpsimd
        def _(gp):
            for c in range(NCH):
                wl_.issue(gp, wq_sb[:, c * 3 * D:(c + 1) * 3 * D],
                          d["wqkv_t"][l, c])
                wl_.issue(gp, wo_sb[:, c * D:(c + 1) * D], d["wo_t"][l, c])
            wl_.wait_all(gp)

        @blk.sync
        def _(sy):
            ld = kb.stream("S")
            ld.wait_all(sy)
            XB = NCH * 513
            for c in range(NCH):
                tk = ld.issue(sy, g["xst"][:, c * 513:(c + 1) * 513],
                              d["strA"][0, c, :, 0:513])
            g["tkA"][0] = tk
            ld.issue(sy, g["qsc"][:, 0:1], d["qs2"][l])
            ld.issue(sy, g["qsc"][:, 1:2], d["qs2s"][l])
            ld.issue(sy, g["qsc"][:, 2:3], d["ks2"][l])
            ld.issue(sy, g["qsc"][:, 3:4], d["ks2s"][l])
            ld.wait_all(sy)
    with nc.Block() as blk:
        @blk.gpsimd
        def _(gp):
            ld = kb.stream("G", depth=8)
            for name, base in (("csql", "csq"), ("ssql", "ssq"),
                               ("cikl", "cik"), ("sikl", "sik")):
                ld.issue(gp, g[name][:, :], d[base])
            ld.wait_all(gp)
    with nc.Block() as blk:
        @blk.vector
        def _(v):
            for col, name in enumerate(("csql", "ssql", "cikl", "sikl")):
                v.tensor_scalar(g[name][:, :], g[name][:, :],
                                g["qsc"][:, col:col + 1], None,
                                AluOpType.mult)

    with nc.Block() as blk:
        @blk.sync
        def _(sy):
            st = kb.stream("S")
            for b in range(B):
                for c in range(NCH):
                    st.issue(sy, d["strB"][b, c, :, 0:1], g["zc"][:, :], slow=True)
            st.wait_all(sy)

    for tt in range(8):
        tr_emit_attn_tile(kb, l, g, wq_sb, wo_sb, tt, tt // 4, tt % 4)
    stk.__exit__(None, None, None)


def tr_emit_attn_tile(kb, l, g, wq_sb, wo_sb, tt, b, kk):
    nc = kb.nc
    P = kb.P
    d = kb.dram
    xs = kb.stream("S")
    first = (kk == 0)

    tkx = {}
    with nc.Block() as blk:
        @blk.sync
        def _(sy):
            if tt > 0:
                for c in range(4):
                    tkx["s"] = xs.issue(sy, g["xst"][:, c * 513:(c + 1) * 513],
                                        d["strA"][b, c, :, kk * TT:kk * TT + 513])

        @blk.gpsimd
        def _(gp):
            if tt > 0:
                xg = kb.stream("G", depth=8)
                for c in range(4, NCH):
                    tkx["g"] = xg.issue(gp, g["xst"][:, c * 513:(c + 1) * 513],
                                        d["strA"][b, c, :, kk * TT:kk * TT + 513])

        @blk.scalar
        def _(s):
            if tt > 0:
                xs.wait(s, tkx["s"])
                kb.stream("G", depth=8).wait(s, tkx["g"])
            else:
                xs.wait(s, g["tkA"][0])
            for c in range(NCH):
                s.copy(g["xbt"][:, c * 513:(c + 1) * 513],
                       g["xst"][:, c * 513:(c + 1) * 513])

    def rhs_sh(kc):
        off = 1 if kc < 4 else 0
        return g["xbt"][:, kc * 513 + off:kc * 513 + off + TT]

    with (nc.psum_tensor(f"tr_pq{l}{tt}", [128, 2048], PS) as qps,
          nc.psum_tensor(f"tr_pss{l}{tt}", [16, 1024], PS) as sps,
          nc.psum_tensor(f"tr_ptr{l}{tt}", [128, 2048],
                         mybir.dt.bfloat16) as tps,
          SemScope(kb, 7, "trQ") as (s1, s2, s3, s4, s5, s6, s7)):
        with nc.Block() as blk:
            @blk.tensor
            def _(t):
                for i, nco in enumerate(list(range(16, 24)) +
                                        list(range(0, 16))):
                    bank = i % 4
                    if i >= 4:
                        t.wait_ge(s2, i - 3)
                    for kc in range(NCH):
                        ins = t.matmul(
                            qps[:, bank * 512:(bank + 1) * 512],
                            wq_sb[:, kc * 3 * D + nco * 128:
                                  kc * 3 * D + (nco + 1) * 128],
                            rhs_sh(kc),
                            start=(kc == 0), stop=(kc == NCH - 1))
                    ins.then_inc(s1, 1)
                    if 4 <= i < 12:
                        c = i - 4
                        if i == 4:
                            t.wait_ge(s2, 1)
                        for ts4 in range(4):
                            slot = c * 4 + ts4
                            if slot >= 2:
                                t.wait_ge(s4, slot - 1)
                            t.transpose(tps[:, (slot % 2) * 1024:
                                            (slot % 2) * 1024 + 128],
                                        g["qraw"][:, c * TT + ts4 * 128:
                                                  c * TT + (ts4 + 1) * 128],
                                        P["identb"][:, :]).then_inc(s3, 1)
                for i in range(16):
                    t.wait_ge(s5, i + 1)
                    qk, c = i // 8, i % 8
                    t.matmul(sps[:, qk * 512:(qk + 1) * 512],
                             P["sel8b"][:, c * 16:(c + 1) * 16],
                             g["sq"][:, (i % 2) * TT:(i % 2 + 1) * TT],
                             start=(c == 0), stop=(c == NCH - 1),
                             skip_group_check=True).then_inc(s6, 1)

            @blk.scalar
            def _(s):
                for i in range(24):
                    bank = i % 4
                    s.wait_ge(s1, i + 1)
                    if i < 8:
                        dst = g["qraw"][:, i * TT:(i + 1) * TT]
                    elif i < 16:
                        s.wait_ge(s3, 4 * (i - 8 + 1))
                        dst = g["qp"][:, (i - 8) * TT:(i - 7) * TT]
                    else:
                        dst = g["kp"][:, (i - 16) * TT:(i - 15) * TT]
                    s.copy(dst, qps[:, bank * 512:(bank + 1) * 512]
                           ).then_inc(s2, 1)
                s.wait_ge(s7, 2)
                s.activation(g["ssq"][:, :], g["ssq"][:, :], AF.Ln
                             ).then_inc(s7, 1)
                s.wait_ge(s7, 3)
                s.activation(g["rstd"][:, :], g["ssq"][:, :], AF.Exp,
                             scale=-0.5)

            @blk.vector
            def _(v):
                for c in range(NCH):
                    for ts4 in range(4):
                        slot = c * 4 + ts4
                        v.wait_ge(s3, slot + 1)
                        for pj in range(2):
                            h = 2 * c + pj
                            ins = v.tensor_copy(
                                g["vt"][:, h * 256 + ts4 * 64:
                                        h * 256 + (ts4 + 1) * 64],
                                tps[:, (slot % 2) * 1024 + pj * 64:
                                    (slot % 2) * 1024 + (pj + 1) * 64])
                        ins.then_inc(s4, 1)
                for i in range(16):
                    v.wait_ge(s2, 8 + i + 1)
                    if i >= 2:
                        v.wait_ge(s6, i - 1)
                    src = (g["qp"][:, (i % 8) * TT:(i % 8 + 1) * TT]
                           if i < 8 else
                           g["kp"][:, (i % 8) * TT:(i % 8 + 1) * TT])
                    v.tensor_mul(g["sq"][:, (i % 2) * TT:(i % 2 + 1) * TT],
                                 src, src).then_inc(s5, 1)
                for qk in range(2):
                    v.wait_ge(s6, 16)
                    v.tensor_scalar(g["ssq"][:, qk * TT:(qk + 1) * TT],
                                    sps[:, qk * 512:(qk + 1) * 512],
                                    1e-24, None, AluOpType.max
                                    ).then_inc(s7, 1)

    for qk, srcb, dstb, ctab, stab in (
            (0, g["qp"], g["qp"], "csql", "ssql"),
            (1, g["kp"], g["kp"], "cikl", "sikl")):
        with (nc.psum_tensor(f"tr_pr{l}{tt}{qk}", [128, 2048], PS) as rps,
              SemScope(kb, 3, "trR") as (s1, s2, s3)):
            with nc.Block() as blk:
                @blk.tensor
                def _(t):
                    for c in range(NCH):
                        if c >= 2:
                            t.wait_ge(s2, 2 * (c - 2) + 1)
                        t.matmul(rps[:, (c % 2) * 512:(c % 2 + 1) * 512],
                                 P["selT8b"][:, c * 128:(c + 1) * 128],
                                 g["rstd"][:, qk * TT:(qk + 1) * TT],
                                 start=True, stop=True,
                                 skip_group_check=True).then_inc(s1, 1)
                        t.wait_ge(s2, 2 * c + 1)
                        if c >= 2:
                            t.wait_ge(s2, 2 * (c - 2) + 2)
                        t.matmul(rps[:, (2 + c % 2) * 512:(3 + c % 2) * 512],
                                 P["pswap"][:, :],
                                 srcb[:, c * TT:(c + 1) * TT],
                                 start=True, stop=True,
                                 skip_group_check=True).then_inc(s1, 1)

                @blk.vector
                def _(v):
                    for c in range(NCH):
                        v.wait_ge(s1, 2 * c + 1)
                        v.tensor_mul(srcb[:, c * TT:(c + 1) * TT],
                                     srcb[:, c * TT:(c + 1) * TT],
                                     rps[:, (c % 2) * 512:(c % 2 + 1) * 512]
                                     ).then_inc(s2, 1)
                        v.wait_ge(s1, 2 * c + 2)
                        if c >= 2:
                            v.wait_ge(s3, 2 * (c - 2) + 2)
                        v.tensor_mul(g["sq"][:, (c % 2) * TT:(c % 2 + 1) * TT],
                                     rps[:, (2 + c % 2) * 512:
                                         (3 + c % 2) * 512],
                                     g[stab][:, kk * TT:(kk + 1) * TT]
                                     ).then_inc(s2, 1)

                @blk.gpsimd
                def _(gp):
                    for c in range(NCH):
                        gp.wait_ge(s2, 2 * c + 1)
                        if c >= 2:
                            gp.wait_ge(s3, 2 * (c - 2) + 2)
                        gp.tensor_mul(g["psb"][:, (c % 2) * 512:
                                               (c % 2 + 1) * 512],
                                      srcb[:, c * TT:(c + 1) * TT],
                                      g[ctab][:, kk * TT:(kk + 1) * TT]
                                      ).then_inc(s3, 1)
                        gp.wait_ge(s2, 2 * c + 2)
                        gp.wait_ge(s3, 2 * c + 1)
                        gp.tensor_add(dstb[:, c * TT:(c + 1) * TT],
                                      g["psb"][:, (c % 2) * 512:
                                               (c % 2 + 1) * 512],
                                      g["sq"][:, (c % 2) * TT:(c % 2 + 1) * TT]
                                      ).then_inc(s3, 1)

    tr_emit_attention(kb, l, g, tt, b, kk, first)
    tr_emit_wo(kb, l, g, wo_sb, tt, b, kk)

    with nc.Block() as blk:
        @blk.gpsimd
        def _(gp):
            for c in range(NCH):
                gp.tensor_copy(g["kprev"][:, c * 256:(c + 1) * 256],
                               g["kp"][:, c * TT + 256:(c + 1) * TT])
            for h in range(H):
                gp.tensor_copy(g["vtprev"][:, h * 128:(h + 1) * 128],
                               g["vt"][:, h * 256 + 128:(h + 1) * 256])


def tr_emit_attention(kb, l, g, tt, b, kk, first):
    nc = kb.nc
    P = kb.P

    units = []
    for h in range(H):
        for wl in range(2):
            w0 = first and wl == 0
            units.append((h, wl, w0, 2 if w0 else 4))
    trc = [0]
    for u in units:
        trc.append(trc[-1] + u[3])

    with (nc.psum_tensor(f"tr_pa{l}{tt}", [128, 2048], PS) as aps,
          nc.psum_tensor(f"tr_po{l}{tt}", [128, 1024], PS) as ops,
          nc.psum_tensor(f"tr_pp{l}{tt}", [128, 2048],
                         mybir.dt.bfloat16) as pps,
          SemScope(kb, 11, "trA") as
          (s_sim, s_ml, s_mx, s_ex, s_rc, s_ps, s_tr, s_pt, s_om, s_oe,
           s_ng)):
        with nc.Block() as blk:
            @blk.tensor
            def _(t):
                for step in range(len(units) + 2):
                    if step < len(units):
                        idx = step
                        h, wl, w0, nks = units[idx]
                        c, base = h // 2, (h % 2) * 64
                        for qt in range(2):
                            bank = (2 * idx + qt) % 4
                            if 2 * idx + qt >= 4:
                                t.wait_ge(s_ml, 2 * idx + qt - 3)
                            qsl = g["qp"][base:base + 64,
                                          c * TT + wl * 256 + qt * 128:
                                          c * TT + wl * 256 + (qt + 1) * 128]
                            col = 0
                            if not w0:
                                if wl == 0:
                                    ksl = g["kprev"][base:base + 64,
                                                     c * 256:(c + 1) * 256]
                                else:
                                    ksl = g["kp"][base:base + 64,
                                                  c * TT:c * TT + 256]
                                t.matmul(aps[:, bank * 512:bank * 512 + 256],
                                         qsl, ksl, start=True, stop=True,
                                         skip_group_check=True)
                                col = 256
                            kcur = g["kp"][base:base + 64,
                                           c * TT + wl * 256:
                                           c * TT + (wl + 1) * 256]
                            t.matmul(aps[:, bank * 512 + col:
                                         bank * 512 + col + 256],
                                     qsl, kcur, start=True, stop=True,
                                     skip_group_check=True).then_inc(s_sim, 1)
                    if step >= 2:
                        idx = step - 2
                        h, wl, w0, nks = units[idx]
                        c, base = h // 2, (h % 2) * 64
                        t.wait_ge(s_ps, 2 * idx + 2)
                        for ks in range(nks):
                            pos = trc[idx] + ks
                            if pos >= 2:
                                t.wait_ge(s_pt, pos - 1)
                            for qt in range(2):
                                ins = t.transpose(
                                    pps[:, (pos % 2) * 1024 + qt * 128:
                                        (pos % 2) * 1024 + (qt + 1) * 128],
                                    g["psb"][:, ((idx % 2) * 2 + qt) * 512 +
                                             ks * 128:
                                             ((idx % 2) * 2 + qt) * 512 +
                                             (ks + 1) * 128],
                                    P["identb"][:, :])
                            ins.then_inc(s_tr, 1)
                        t.wait_ge(s_pt, trc[idx] + nks)
                        if idx >= 2:
                            t.wait_ge(s_oe, idx - 1)
                        for ks in range(nks):
                            if w0:
                                vsl = g["vt"][:, h * 256 + ks * 64:
                                              h * 256 + (ks + 1) * 64]
                            elif wl == 0:
                                if ks < 2:
                                    vsl = g["vtprev"][:, h * 128 + ks * 64:
                                                      h * 128 + (ks + 1) * 64]
                                else:
                                    vsl = g["vt"][:, h * 256 + (ks - 2) * 64:
                                                  h * 256 + (ks - 1) * 64]
                            else:
                                vsl = g["vt"][:, h * 256 + ks * 64:
                                              h * 256 + (ks + 1) * 64]
                            ins = t.matmul(ops[0:64, (idx % 2) * 512:
                                               (idx % 2) * 512 + 256],
                                           vsl,
                                           g["ptb"][:, (idx % 2) * 1024 +
                                                    ks * 256:
                                                    (idx % 2) * 1024 +
                                                    (ks + 1) * 256],
                                           start=(ks == 0),
                                           stop=(ks == nks - 1),
                                           skip_group_check=True)
                        ins.then_inc(s_om, 1)

            @blk.vector
            def _(v):
                for idx, (h, wl, w0, nks) in enumerate(units):
                    p8 = (idx % 2) * 8
                    nk = 256 if w0 else 512
                    for qt in range(2):
                        v.wait_ge(s_sim, 2 * idx + qt + 1)
                        if idx >= 2:
                            v.wait_ge(s_ex, 2 * (idx - 2) + 2)
                        msk = (g["mk0"][:, qt * 256:(qt + 1) * 256] if w0
                               else g["mky"][:, qt * 512:(qt + 1) * 512])
                        v.scalar_tensor_tensor(
                            g["ml"][:, ((idx % 2) * 2 + qt) * 512:
                                    ((idx % 2) * 2 + qt) * 512 + nk],
                            aps[:, ((2 * idx + qt) % 4) * 512:
                                ((2 * idx + qt) % 4) * 512 + nk],
                            1.0, msk, AluOpType.mult, AluOpType.add
                            ).then_inc(s_ml, 1)
                        v.wait_ge(s_ml, 2 * idx + qt + 1)
                        v.reduce_max(g["mx"][:, p8 + qt:p8 + qt + 1],
                                     g["ml"][:, ((idx % 2) * 2 + qt) * 512:
                                             ((idx % 2) * 2 + qt) * 512 + nk],
                                     axis=AX).then_inc(s_mx, 1)
                    for qt in range(2):
                        v.wait_ge(s_mx, 2 * idx + qt + 1)
                        v.tensor_scalar(g["mx"][:, p8 + 4 + qt:p8 + 5 + qt],
                                        g["mx"][:, p8 + qt:p8 + qt + 1],
                                        -1.0, None, AluOpType.mult
                                        ).then_inc(s_ng, 1)
                    for qt in range(2):
                        v.wait_ge(s_ex, 2 * idx + qt + 1)
                        v.reciprocal(g["mx"][:, p8 + 6 + qt:p8 + 7 + qt],
                                     g["mx"][:, p8 + 2 + qt:p8 + 3 + qt]
                                     ).then_inc(s_rc, 1)
                    for qt in range(2):
                        v.wait_ge(s_rc, 2 * idx + qt + 1)
                        v.tensor_scalar(
                            g["psb"][:, ((idx % 2) * 2 + qt) * 512:
                                     ((idx % 2) * 2 + qt) * 512 + nk],
                            g["psb"][:, ((idx % 2) * 2 + qt) * 512:
                                     ((idx % 2) * 2 + qt) * 512 + nk],
                            g["mx"][:, p8 + 6 + qt:p8 + 7 + qt], None,
                            AluOpType.mult).then_inc(s_ps, 1)

            @blk.scalar
            def _(s):
                for idx, (h, wl, w0, nks) in enumerate(units):
                    p8 = (idx % 2) * 8
                    nk = 256 if w0 else 512
                    c, base = h // 2, (h % 2) * 64
                    for qt in range(2):
                        s.wait_ge(s_ng, 2 * idx + qt + 1)
                        if idx >= 2:
                            s.wait_ge(s_ps, 2 * (idx - 2) + 2)
                            s.wait_ge(s_tr, trc[idx - 2] + units[idx - 2][3])
                        s.activation(g["psb"][:, ((idx % 2) * 2 + qt) * 512:
                                              ((idx % 2) * 2 + qt) * 512 + nk],
                                     g["ml"][:, ((idx % 2) * 2 + qt) * 512:
                                             ((idx % 2) * 2 + qt) * 512 + nk],
                                     AF.Exp,
                                     bias=g["mx"][:, p8 + 4 + qt:p8 + 5 + qt],
                                     accum_out=g["mx"][:, p8 + 2 + qt:
                                                       p8 + 3 + qt]
                                     ).then_inc(s_ex, 1)
                    for ks in range(nks):
                        pos = trc[idx] + ks
                        s.wait_ge(s_tr, pos + 1)
                        s.copy(g["ptb"][:, (idx % 2) * 1024 + ks * 256:
                                        (idx % 2) * 1024 + (ks + 1) * 256],
                               pps[:, (pos % 2) * 1024:(pos % 2) * 1024 + 256]
                               ).then_inc(s_pt, 1)
                    s.wait_ge(s_om, idx + 1)
                    s.copy(g["ao"][base:base + 64,
                                   c * TT + wl * 256:c * TT + (wl + 1) * 256],
                           ops[0:64, (idx % 2) * 512:(idx % 2) * 512 + 256]
                           ).then_inc(s_oe, 1)


def tr_emit_wo(kb, l, g, wo_sb, tt, b, kk):
    nc = kb.nc
    d = kb.dram
    ys = kb.stream("S")
    with (nc.psum_tensor(f"tr_pw{l}{tt}", [128, 2048], PS) as wps,
          SemScope(kb, 3, "trW") as (s1, s2, s3)):
        y_base = ys.i
        with nc.Block() as blk:
            @blk.tensor
            def _(t):
                for c in range(NCH):
                    bank = c % 4
                    if c >= 4:
                        t.wait_ge(s2, c - 3)
                    for kc in range(NCH):
                        ins = t.matmul(wps[:, bank * 512:(bank + 1) * 512],
                                       wo_sb[:, kc * D + c * 128:
                                             kc * D + (c + 1) * 128],
                                       g["ao"][:, kc * TT:(kc + 1) * TT],
                                       start=(kc == 0), stop=(kc == NCH - 1))
                    ins.then_inc(s1, 1)

            @blk.vector
            def _(v):
                for c in range(NCH):
                    v.wait_ge(s1, c + 1)
                    if c >= 2:
                        ys.wait(v, y_base + c - 1)
                    v.tensor_add(g["yst"][:, (c % 2) * TT:(c % 2 + 1) * TT],
                                 g["xst"][:, c * 513 + 1:(c + 1) * 513],
                                 wps[:, (c % 4) * 512:(c % 4 + 1) * 512]
                                 ).then_inc(s2, 1)

            @blk.sync
            def _(sy):
                for c in range(NCH):
                    sy.wait_ge(s2, c + 1)
                    ys.issue(sy, d["strB"][b, c, :, 1 + kk * TT:
                                           1 + (kk + 1) * TT],
                             g["yst"][:, (c % 2) * TT:(c % 2 + 1) * TT])
                ys.wait_all(sy)


def tr_emit_ffn_layer(kb, l, g):
    nc = kb.nc
    P = kb.P
    d = kb.dram
    wl_ = kb.stream("G", depth=8)

    with (nc.sbuf_tensor(f"tr_w1{l}", [128, NCH * 2 * DH1], BF16) as w1_sb,
          nc.sbuf_tensor(f"tr_w2{l}", [128, 3 * D], BF16) as w2_sb,
          nc.sbuf_tensor(f"tr_bb{l}", [128, 2 * NHC + 3 * NCH], F32) as bb_sb,
          nc.sbuf_tensor(f"tr_xn{l}", [128, NCH * TT], BF16) as xnb,
          nc.sbuf_tensor(f"tr_G{l}", [128, NHC * TT], BF16) as Gb,
          nc.sbuf_tensor(f"tr_row{l}", [1, 4 * TT], F32) as rowb):
        with nc.Block() as blk:
            @blk.gpsimd
            def _(gp):
                for c in range(NCH):
                    wl_.issue(gp, w1_sb[:, c * 2 * DH1:(c + 1) * 2 * DH1],
                              d["w1p"][l, c])
                wl_.wait_all(gp)

            @blk.sync
            def _(sy):
                ld = kb.stream("S")
                ld.wait_all(sy)
                for c in range(NCH):
                    tk = ld.issue(sy, g["xst"][:, c * 513:(c + 1) * 513],
                                  d["strB"][0, c, :, 0:513])
                g["tkB"][0] = tk
                ld.issue(sy, bb_sb[:, 0:2 * NHC], d["b1c"][l])
                ld.issue(sy, bb_sb[:, 2 * NHC:2 * NHC + NCH], d["lng"][l])
                ld.issue(sy, bb_sb[:, 2 * NHC + NCH:2 * NHC + 2 * NCH],
                         d["lnbv"][l])
                ld.issue(sy, bb_sb[:, 2 * NHC + 2 * NCH:2 * NHC + 3 * NCH],
                         d["b2c"][l])
                ld.wait_all(sy)

        for tt in range(8):
            tr_emit_ffn_tile(kb, l, g, w1_sb, w2_sb, bb_sb, xnb, Gb, rowb,
                          tt, tt // 4, tt % 4)


def tr_emit_ffn_tile(kb, l, g, w1_sb, w2_sb, bb_sb, xnb, Gb, rowb, tt, b, kk):
    nc = kb.nc
    P = kb.P
    d = kb.dram
    xs = kb.stream("S")
    ys = kb.stream("S")
    w2s = kb.stream("G", depth=8)

    tkx = {}
    with nc.Block() as blk:
        @blk.sync
        def _(sy):
            if tt > 0:
                for c in range(4):
                    tkx["s"] = xs.issue(sy, g["xst"][:, c * 513:(c + 1) * 513],
                                        d["strB"][b, c, :, kk * TT:kk * TT + 513])

        @blk.gpsimd
        def _(gp):
            if tt > 0:
                xg = kb.stream("G", depth=8)
                for c in range(4, NCH):
                    tkx["g"] = xg.issue(gp, g["xst"][:, c * 513:(c + 1) * 513],
                                        d["strB"][b, c, :, kk * TT:kk * TT + 513])

        @blk.scalar
        def _(s):
            if tt > 0:
                xs.wait(s, tkx["s"])
                kb.stream("G", depth=8).wait(s, tkx["g"])
            else:
                xs.wait(s, g["tkB"][0])
            for c in range(NCH):
                off = 1 if c < 4 else 0
                s.copy(g["xbt"][:, c * TT:(c + 1) * TT],
                       g["xst"][:, c * 513 + off:c * 513 + off + TT])

    with (nc.psum_tensor(f"tr_f1{l}{tt}", [1, 512], PS) as sp1,
          nc.psum_tensor(f"tr_f2{l}{tt}", [1, 512], PS) as sp2,
          SemScope(kb, 4, "trF") as (q1, q2, f1, f2)):
        with nc.Block() as blk:
            @blk.vector
            def _(v):
                for c in range(NCH):
                    if c >= 2:
                        v.wait_ge(q2, c - 1)
                    v.tensor_mul(g["sq"][:, (c % 2) * TT:(c % 2 + 1) * TT],
                                 g["xbt"][:, c * TT:(c + 1) * TT],
                                 g["xbt"][:, c * TT:(c + 1) * TT]
                                 ).then_inc(q1, 1)

            @blk.tensor
            def _(t):
                for c in range(NCH):
                    t.matmul(sp1[0:1, :], P["onesb"][:, 0:1],
                             g["xbt"][:, c * TT:(c + 1) * TT],
                             start=(c == 0), stop=(c == NCH - 1),
                             skip_group_check=True)
                for c in range(NCH):
                    t.wait_ge(q1, c + 1)
                    t.matmul(sp2[0:1, :], P["onesb"][:, 0:1],
                             g["sq"][:, (c % 2) * TT:(c % 2 + 1) * TT],
                             start=(c == 0), stop=(c == NCH - 1),
                             skip_group_check=True).then_inc(q2, 1)
        with nc.Block() as blk:
            @blk.vector
            def _(v):
                v.tensor_scalar(rowb[0:1, 0:TT], sp1[0:1, :], 1.0 / D, None,
                                AluOpType.mult).then_inc(f1, 1)
                v.tensor_scalar(rowb[0:1, TT:2 * TT], sp2[0:1, :], 1.0 / D,
                                None, AluOpType.mult).then_inc(f1, 1)
                v.wait_ge(f1, 2)
                v.tensor_mul(rowb[0:1, 2 * TT:3 * TT], rowb[0:1, 0:TT],
                             rowb[0:1, 0:TT]).then_inc(f1, 1)
                v.wait_ge(f1, 3)
                v.tensor_sub(rowb[0:1, TT:2 * TT], rowb[0:1, TT:2 * TT],
                             rowb[0:1, 2 * TT:3 * TT]).then_inc(f1, 1)

            @blk.scalar
            def _(s):
                s.wait_ge(f1, 4)
                s.activation(rowb[0:1, TT:2 * TT], rowb[0:1, TT:2 * TT],
                             AF.Ln, bias=P["epsc"][0:1, 1:2]).then_inc(f2, 1)
                s.wait_ge(f2, 1)
                s.activation(rowb[0:1, TT:2 * TT], rowb[0:1, TT:2 * TT],
                             AF.Exp, scale=-0.5)

    with (nc.psum_tensor(f"tr_fb{l}{tt}", [128, 1024], PS) as bps,
          SemScope(kb, 2, "trN") as (r1, r2)):
        with nc.Block() as blk:
            @blk.tensor
            def _(t):
                t.matmul(bps[:, 0:512], P["onesf"][0:1, :],
                         rowb[0:1, 0:TT],
                         start=True, stop=True, skip_group_check=True)
                t.matmul(bps[:, 512:1024], P["onesf"][0:1, :],
                         rowb[0:1, TT:2 * TT],
                         start=True, stop=True,
                         skip_group_check=True).then_inc(r1, 1)

            @blk.vector
            def _(v):
                v.wait_ge(r1, 1)
                for c in range(NCH):
                    if c >= 2:
                        v.wait_ge(r1, 1 + c - 1)
                    v.tensor_sub(g["yst"][:, (c % 2) * TT:(c % 2 + 1) * TT],
                                 g["xbt"][:, c * TT:(c + 1) * TT],
                                 bps[:, 0:512]).then_inc(r2, 1)
                    v.wait_ge(r2, 2 * c + 1)
                    v.tensor_mul(g["yst"][:, (c % 2) * TT:(c % 2 + 1) * TT],
                                 g["yst"][:, (c % 2) * TT:(c % 2 + 1) * TT],
                                 bps[:, 512:1024]).then_inc(r2, 1)

            @blk.gpsimd
            def _(gp):
                for c in range(NCH):
                    gp.wait_ge(r2, 2 * c + 2)
                    gp.tensor_scalar(xnb[:, c * TT:(c + 1) * TT],
                                     g["yst"][:, (c % 2) * TT:(c % 2 + 1) * TT],
                                     bb_sb[:, 2 * NHC + c:2 * NHC + c + 1],
                                     bb_sb[:, 2 * NHC + NCH + c:
                                            2 * NHC + NCH + c + 1],
                                     AluOpType.mult, AluOpType.add
                                     ).then_inc(r1, 1)

    with (nc.psum_tensor(f"tr_fg{l}{tt}", [128, 2048], PS) as gps,
          SemScope(kb, 3, "trG") as (u1, u2, u3)):
        with nc.Block() as blk:
            @blk.tensor
            def _(t):
                for hc in range(NHC):
                    p = hc % 2
                    if hc >= 2:
                        t.wait_ge(u3, hc - 1)
                    for kc in range(NCH):
                        t.matmul(gps[:, (2 * p) * 512:(2 * p + 1) * 512],
                                 w1_sb[:, kc * 2 * DH1 + hc * 128:
                                       kc * 2 * DH1 + (hc + 1) * 128],
                                 xnb[:, kc * TT:(kc + 1) * TT],
                                 start=(kc == 0), stop=(kc == NCH - 1),
                                 skip_group_check=True)
                    for kc in range(NCH):
                        ins = t.matmul(
                            gps[:, (2 * p + 1) * 512:(2 * p + 2) * 512],
                            w1_sb[:, kc * 2 * DH1 + DH1 + hc * 128:
                                  kc * 2 * DH1 + DH1 + (hc + 1) * 128],
                            xnb[:, kc * TT:(kc + 1) * TT],
                            start=(kc == 0), stop=(kc == NCH - 1),
                            skip_group_check=True)
                    ins.then_inc(u1, 1)

            @blk.scalar
            def _(s):
                for hc in range(NHC):
                    p = hc % 2
                    s.wait_ge(u1, hc + 1)
                    s.activation(g["psb"][:, p * 512:(p + 1) * 512],
                                 gps[:, (2 * p + 1) * 512:(2 * p + 2) * 512],
                                 AF.Gelu,
                                 bias=bb_sb[:, NHC + hc:NHC + hc + 1]
                                 ).then_inc(u2, 1)

            @blk.vector
            def _(v):
                for hc in range(NHC):
                    p = hc % 2
                    v.wait_ge(u2, hc + 1)
                    v.scalar_tensor_tensor(
                        Gb[:, hc * TT:(hc + 1) * TT],
                        gps[:, (2 * p) * 512:(2 * p + 1) * 512],
                        bb_sb[:, hc:hc + 1],
                        g["psb"][:, p * 512:(p + 1) * 512],
                        AluOpType.add, AluOpType.mult).then_inc(u3, 1)

    with (nc.psum_tensor(f"tr_fw{l}{tt}", [128, 4096], PS) as wps2,
          SemScope(kb, 3, "trH") as (u1, u2, u3)):
        w2s_base = w2s.i
        with nc.Block() as blk:
            @blk.gpsimd
            def _(gp):
                for hc in range(NHC):
                    if hc >= 3:
                        gp.wait_ge(u1, hc - 2)
                    w2s.issue(gp, w2_sb[:, (hc % 3) * D:(hc % 3 + 1) * D],
                              d["w2p"][l, hc])

            @blk.tensor
            def _(t):
                for hc in range(NHC):
                    w2s.wait(t, w2s_base + hc + 1)
                    for co in range(NCH):
                        ins = t.matmul(
                            wps2[:, co * 512:(co + 1) * 512],
                            w2_sb[:, (hc % 3) * D + co * 128:
                                  (hc % 3) * D + (co + 1) * 128],
                            Gb[:, hc * TT:(hc + 1) * TT],
                            start=(hc == 0), stop=(hc == NHC - 1),
                            skip_group_check=True)
                    ins.then_inc(u1, 1)

            @blk.vector
            def _(v):
                ys2_base = ys.i
                for co in range(NCH):
                    v.wait_ge(u1, NHC)
                    if co >= 2:
                        ys.wait(v, ys2_base + co - 1)
                    v.scalar_tensor_tensor(
                        g["yst"][:, (co % 2) * TT:(co % 2 + 1) * TT],
                        wps2[:, co * 512:(co + 1) * 512],
                        bb_sb[:, 2 * NHC + 2 * NCH + co:
                               2 * NHC + 2 * NCH + co + 1],
                        g["xst"][:, co * 513 + 1:(co + 1) * 513],
                        AluOpType.add, AluOpType.add).then_inc(u2, 1)

            @blk.sync
            def _(sy):
                for co in range(NCH):
                    sy.wait_ge(u2, co + 1)
                    ys.issue(sy, d["strA"][b, co, :, 1 + kk * TT:
                                           1 + (kk + 1) * TT],
                             g["yst"][:, (co % 2) * TT:(co % 2 + 1) * TT])
                ys.wait_all(sy)





def fin_emit(kb):
    nc = kb.nc
    d = kb.dram
    ld = kb.stream("S")
    with (nc.sbuf_tensor("fin_y", [128, NCH * T], F32) as ybuf,
          nc.sbuf_tensor("fin_x1", [128, NCH * T], F32) as x1b,
          nc.sbuf_tensor("fin_sq", [128, 2 * T], F32) as sqf,
          nc.sbuf_tensor("fin_row", [1, 3 * T], F32) as rowb):
        for b in range(B):
            with nc.Block() as blk:
                @blk.sync
                def _(sy):
                    for c in range(NCH):
                        ld.issue(sy, ybuf[:, c * T:(c + 1) * T],
                                 d["strA"][b, c, :, 1:T + 1])
                        ld.issue(sy, x1b[:, c * T:(c + 1) * T], d["x1s"][b, c])
                    ld.wait_all(sy)
            with nc.Block() as blk:
                @blk.vector
                def _(v):
                    for c in range(NCH):
                        v.tensor_add(ybuf[:, c * T:(c + 1) * T],
                                     ybuf[:, c * T:(c + 1) * T],
                                     x1b[:, c * T:(c + 1) * T])
            postln(kb, b, f"fin{b}", ybuf, rowb, "out_f",
                         sqf, dst_off=0)



# version 12 (final)
# speedup vs baseline: 1.0061x; 1.0061x over previous
import numpy as np
import ml_dtypes
import concourse.bass as bass
import concourse.mybir as mybir
from concourse.bass_utils import run_bass_kernel_spmd
from concourse.alu_op_type import AluOpType
from contextlib import ExitStack



PS = mybir.dt.float32
TT = 512


def kernel(**inputs):
    in_maps = stage_inputs(inputs)
    kb = build(nphase=99)
    res = run_bass_kernel_spmd(kb.nc, in_maps, core_ids=list(range(8)))
    return gather_output([r for r in res.results])




F32 = mybir.dt.float32
BF16 = mybir.dt.bfloat16
AF = mybir.ActivationFunctionType
AX = mybir.AxisListType.X

B = 2
T = 2048
D = 1024
NCH = 8
H = 16
DK = 64
CC = 128
NCK = T // CC
NIT = 15
TT = 512
NTT = T // TT
WIN = 256
NW = T // WIN
NL = 6
DH1 = 2816
NHC = 22

NEG = -3.0e38


def host_constants():
    c = {}
    c["identf"] = np.eye(128, dtype=np.float32)
    c["identb"] = np.eye(128, dtype=ml_dtypes.bfloat16)
    sig = np.arange(128)
    sig = (sig // 64) * 64 + ((sig % 64) + 32) % 64
    psw = np.zeros((128, 128), np.float32)
    psw[sig, np.arange(128)] = 1.0
    c["pswap"] = psw.astype(ml_dtypes.bfloat16)
    s = np.arange(CC)
    c["cgmask"] = np.where(s[:, None] <= s[None, :], 0.0, NEG).astype(np.float32)
    sel8 = np.zeros((NCH, 128, 16), np.float32)
    for ch in range(NCH):
        sel8[ch, 0:64, 2 * ch] = 1.0
        sel8[ch, 64:128, 2 * ch + 1] = 1.0
    c["sel8"] = sel8
    c["selT8"] = sel8.transpose(0, 2, 1).copy()
    bo2 = np.zeros((2, 128), np.float32)
    bo2[0, 0:64] = 1.0
    bo2[1, 64:128] = 1.0
    c["bo2p"] = bo2
    c["bo2n"] = -bo2
    c["blk2"] = bo2.T.copy()
    c["onescol"] = np.ones((128, 128), np.float32)
    zm = np.zeros((128, 2), np.float32)
    zm[0:64, 0] = 1.0
    zm[64:128, 1] = 1.0
    c["zm2"] = zm
    half = np.arange(0, DK, 2, dtype=np.float32)
    inv_freq = 1.0 / 10000.0 ** (half / DK)
    t = np.arange(T, dtype=np.float32)
    freqs = t[:, None] * inv_freq[None, :]
    freqs = np.concatenate([freqs, freqs], -1)
    base_scale = (half + 0.4 * DK) / (1.4 * DK)
    power = (t - T // 2) / (WIN // 2)
    scale = base_scale[None, :] ** power[:, None]
    scale = np.concatenate([scale, scale], -1)
    cos, sin = np.cos(freqs), np.sin(freqs)
    sgn = np.where(np.arange(DK) < 32, -1.0, 1.0)[None, :]
    QKS = 8.0
    csq = (QKS * cos * scale).T
    ssq = (QKS * sin * scale * sgn).T
    cik = (cos / scale).T
    sik = (sin / scale * sgn).T
    dup = lambda a: np.concatenate([a, a], 0).astype(np.float32)
    c["csq"], c["ssq"] = dup(csq), dup(ssq)
    c["cik"], c["sik"] = dup(cik), dup(sik)
    i = np.arange(128)
    j = np.arange(512)
    m = np.zeros((2, 128, 512), np.float32)
    for qt in range(2):
        lo = qt * 128 + i
        valid = (j[None, :] >= lo[:, None]) & (j[None, :] <= lo[:, None] + 256)
        m[qt] = np.where(valid, 0.0, NEG)
    c["wmask"] = m
    j2 = np.arange(256)
    m0 = np.zeros((2, 128, 256), np.float32)
    for qt in range(2):
        ig = qt * 128 + i
        valid = (j2[None, :] <= ig[:, None]) & (j2[None, :] >= ig[:, None] - 256)
        m0[qt] = np.where(valid, 0.0, NEG)
    c["wmask0"] = m0
    return c


def stage_inputs(full):
    c = host_constants()
    sig = np.arange(64)
    sig = (sig + 32) % 64
    w2full = full["L0_ffn_W2"]
    w2p = np.zeros((NL, NHC * 128, D), np.float32)
    w2p[:, :2730] = w2full
    w1full = full["L0_ffn_W1"]
    w1p = np.zeros((NL, D, 2 * DH1), np.float32)
    w1p[:, :, 0:2730] = w1full[:, :, 0:2730]
    w1p[:, :, DH1:DH1 + 2730] = w1full[:, :, 2730:5460]
    b1full = full["L0_ffn_b1"]
    b1p = np.zeros((NL, 2 * DH1), np.float32)
    b1p[:, 0:2730] = b1full[:, 0:2730]
    b1p[:, DH1:DH1 + 2730] = b1full[:, 2730:5460]
    b1c = b1p.reshape(NL, 2 * NHC, 128).transpose(0, 2, 1).copy()

    shared = dict(
        wq=full["L0_attn_Wq"].reshape(NCH, 128, D),
        wk=full["L0_attn_Wk"].reshape(NCH, 128, D),
        wv=full["L0_attn_Wv"].reshape(NCH, 128, D),
        wom=full["L0_attn_Wo"].reshape(NCH, 128, D),
        wa=full["L0_attn_Wa"].reshape(NCH, 128, H),
        wb=full["L0_attn_Wb"].reshape(NCH, 128, H),
        sm16=np.stack([full["L0_attn_bb"], full["L0_attn_A_log"],
                       full["L0_attn_dt_bias"]], 1).astype(np.float32),
        lamb_c=full["L0_attn_lambda_params"].reshape(NCH, 128).T.copy(),
        wcq=full["L0_attn_Wconv_q"].reshape(NCH, 128, 4),
        wck=full["L0_attn_Wconv_k"].reshape(NCH, 128, 4),
        normw2=np.tile(full["L0_attn_norm_w"], 2)[:, None],
        wqkv_tb=full["L0_ffn_Wqkv"].reshape(NL, NCH, 128, 3 * D)
            .astype(ml_dtypes.bfloat16),
        wo_tb=full["L0_ffn_Wo"].reshape(NL, NCH, 128, D)
            .astype(ml_dtypes.bfloat16),
        w1pb=w1p.reshape(NL, NCH, 128, 2 * DH1).astype(ml_dtypes.bfloat16),
        b1c=b1c,
        w2p=w2p.reshape(NL, NHC, 128, D),
        b2c=full["L0_ffn_b2"].reshape(NL, 8, 128).transpose(0, 2, 1).copy(),
        lng=full["L0_ffn_ln_g"].reshape(NL, 8, 128).transpose(0, 2, 1).copy(),
        lnbv=full["L0_ffn_ln_b"].reshape(NL, 8, 128).transpose(0, 2, 1).copy(),
        qs2=np.tile(full["L0_ffn_q_scale"], (1, 2))[:, :, None],
        ks2=np.tile(full["L0_ffn_k_scale"], (1, 2))[:, :, None],
        qs2s=np.tile(full["L0_ffn_q_scale"][:, sig], (1, 2))[:, :, None],
        ks2s=np.tile(full["L0_ffn_k_scale"][:, sig], (1, 2))[:, :, None],
        **c,
    )
    shared = {k: np.ascontiguousarray(v, dtype=v.dtype) for k, v in shared.items()}

    x = full["L0_attn_hidden_states"]
    in_maps = []
    for core in range(8):
        xs = x[2 * core:2 * core + 2]
        xfm = np.ascontiguousarray(
            xs.transpose(0, 2, 1).reshape(B, NCH, 128, T), dtype=np.float32)
        m = dict(shared)
        m["x_in"] = xfm
        in_maps.append(m)
    return in_maps


def gather_output(results):
    outs = []
    for r in results:
        o = r["out_f"]
        outs.append(o.reshape(B, D, T).transpose(0, 2, 1))
    return np.ascontiguousarray(np.concatenate(outs, 0), dtype=np.float32)


class SemScope:

    def __init__(self, kb, n, tag):
        self.kb = kb
        self.n = n
        self.tag = tag

    def __enter__(self):
        self.handles = [self.kb.nc.semaphore(f"{self.tag}_{i}")
                        for i in range(self.n)]
        self.sems = [h.__enter__() for h in self.handles]
        return self.sems

    def __exit__(self, *a):
        nc = self.kb.nc
        with nc.Block() as blk:
            @blk.gpsimd
            def _(g):
                for s in self.sems:
                    g.sem_clear(s)
        for h in reversed(self.handles):
            h.__exit__(None, None, None)
        return False


class KB:

    def __init__(self, nphase=99, debug_outs=()):
        self.nphase = nphase
        self.debug_outs = set(debug_outs)
        self.nc = bass.Bass("TRN2", target_bir_lowering=False)
        self.dram = {}
        self.ctx = []

    def dt(self, name, shape, dtype=F32, kind=None):
        if kind is None and name in self.debug_outs:
            kind = "ExternalOutput"
        kw = {} if kind is None else {"kind": kind}
        h = self.nc.dram_tensor(name, list(shape), dtype, **kw)
        self.dram[name] = h.ap()
        return self.dram[name]

    def sb(self, name, shape, dtype):
        h = self.nc.sbuf_tensor(name, list(shape), dtype)
        v = h.__enter__()
        self.ctx.append(h)
        return v

    def ps(self, name, shape, dtype=F32):
        h = self.nc.psum_tensor(name, list(shape), dtype)
        v = h.__enter__()
        self.ctx.append(h)
        return v

    def sem(self, name):
        h = self.nc.semaphore(name)
        v = h.__enter__()
        self.ctx.append(h)
        return v

    def init_dma_sems(self):
        self.streams = {}

    def stream(self, tag, depth=4):
        tag = {"S": "S", "G": "G"}.get(tag, tag)
        if tag not in self.streams:
            self.streams[tag] = DmaStream(self, tag, depth)
        return self.streams[tag]

    def close(self):
        for h in reversed(self.ctx):
            h.__exit__(None, None, None)


class DmaStream:

    def __init__(self, kb, tag, depth=4):
        self.depth = depth
        self.sems = [kb.sem(f"dma_{tag}_{j}") for j in range(depth)]
        self.i = 0

    def issue(self, eng, dst, src, slow=False):
        j = self.i % self.depth
        if self.i >= self.depth:
            eng.wait_ge(self.sems[j], (self.i // self.depth) * 16)
        eng.dma_start(dst, src,
                      allow_slow_non_contiguous=slow).then_inc(self.sems[j], 16)
        self.i += 1
        return self.i

    def wait(self, eng, ticket):
        j = (ticket - 1) % self.depth
        eng.wait_ge(self.sems[j], ((ticket - 1) // self.depth + 1) * 16)

    def wait_all(self, eng):
        for j in range(self.depth):
            cnt = (self.i - 1 - j) // self.depth + 1 if self.i > j else 0
            if cnt:
                eng.wait_ge(self.sems[j], cnt * 16)


def build(nphase=99, debug_outs=()):
    kb = KB(nphase, debug_outs)
    nc = kb.nc
    dt = kb.dt

    x_in = dt("x_in", [B, NCH, 128, T], F32, "ExternalInput")
    for nm, sh in [("wq", [NCH, 128, D]), ("wk", [NCH, 128, D]),
                   ("wv", [NCH, 128, D]), ("wom", [NCH, 128, D]),
                   ("wa", [NCH, 128, H]), ("wb", [NCH, 128, H]),
                   ("sm16", [H, 3]), ("lamb_c", [128, NCH]),
                   ("wcq", [NCH, 128, 4]), ("wck", [NCH, 128, 4]),
                   ("normw2", [128, 1]),
                   ("b1c", [NL, 128, 2 * NHC]),
                   ("w2p", [NL, NHC, 128, D]),
                   ("b2c", [NL, 128, NCH]), ("lng", [NL, 128, NCH]),
                   ("lnbv", [NL, 128, NCH]),
                   ("qs2", [NL, 128, 1]), ("ks2", [NL, 128, 1]),
                   ("qs2s", [NL, 128, 1]), ("ks2s", [NL, 128, 1]),
                   ("identf", [128, 128]), ("cgmask", [128, 128]),
                   ("sel8", [NCH, 128, 16]), ("selT8", [NCH, 16, 128]),
                   ("bo2p", [2, 128]), ("bo2n", [2, 128]), ("blk2", [128, 2]),
                   ("zm2", [128, 2]),
                   ("onescol", [128, 128]),
                   ("csq", [128, T]), ("ssq", [128, T]),
                   ("cik", [128, T]), ("sik", [128, T]),
                   ("wmask", [2, 128, 512]), ("wmask0", [2, 128, 256])]:
        dt(nm, sh, F32, "ExternalInput")
    for nm, sh in [("identb", [128, 128]), ("pswap", [128, 128]),
               ("wqkv_tb", [NL, NCH, 128, 3 * D]),
               ("wo_tb", [NL, NCH, 128, D]),
               ("w1pb", [NL, NCH, 128, 2 * DH1])]:
        dt(nm, sh, BF16, "ExternalInput")

    out_f = dt("out_f", [B, NCH, 128, T], F32, "ExternalOutput")

    qh = dt("qh", [B, NCH, 128, T], BF16)
    kh = dt("kh", [B, NCH, 128, T], BF16)
    kn_s = dt("kn_s", [B, H, NCK, 128, DK], BF16)
    vn_s = dt("vn_s", [B, H, NCK, 128, DK], BF16)
    mo = dt("mo", [B, NCH, 128, T], BF16)
    strA = dt("strA", [B, NCH, 128, T + 1], F32)
    strB = dt("strB", [B, NCH, 128, T + 1], F32)
    x1s = dt("x1s", [B, NCH, 128, T], F32)
    dt("dbg_rstd", [16, T], BF16)
    dt("dbg_ssq", [16, T], F32)
    dt("dbg_qsil", [128, NCH * T], BF16)
    dt("dbg_raw", [128, NCH * (T + 4)], BF16)

    P = {}
    kb.init_dma_sems()
    P["identf"] = kb.sb("c_identf", [128, 128], F32)
    P["identb"] = kb.sb("c_identb", [128, 128], BF16)
    P["pswap"] = kb.sb("c_pswap", [128, 128], BF16)
    P["cgmask"] = kb.sb("c_cgmask", [128, 128], F32)
    P["sel8b"] = kb.sb("c_sel8b", [128, NCH * 16], BF16)
    P["selT8b"] = kb.sb("c_selT8b", [16, NCH * 128], BF16)
    P["bo2p"] = kb.sb("c_bo2p", [2, 128], F32)
    P["bo2n"] = kb.sb("c_bo2n", [2, 128], F32)
    P["bo2pb"] = kb.sb("c_bo2pb", [2, 128], BF16)
    P["bo2nb"] = kb.sb("c_bo2nb", [2, 128], BF16)
    P["blk2b"] = kb.sb("c_blk2b", [128, 2], BF16)
    P["onesf"] = kb.sb("c_onesf", [128, 128], F32)
    P["onesb"] = kb.sb("c_onesb", [128, 128], BF16)
    P["sm16"] = kb.sb("c_sm16", [16, 3], F32)
    P["negexpA"] = kb.sb("c_negexpA", [16, 1], F32)
    P["lamsp"] = kb.sb("c_lamsp", [128, NCH], F32)
    P["normw2"] = kb.sb("c_normw2", [128, 1], F32)
    P["epsc"] = kb.sb("c_epsc", [128, 4], F32)
    P["zm2"] = kb.sb("c_zm2", [128, 2], F32)
    P["d_hm"] = kb.sb("p_dhm", [16, B * T], F32)
    P["gamT"] = kb.sb("p_gamT", [128, B * NCK * 16], F32)
    P["lnbmgT"] = kb.sb("p_lnbmgT", [128, B * NCK * 16], F32)
    P["ceT"] = kb.sb("p_ceT", [128, B * NCK * 16], F32)
    kb.P = P

    m1_emit_consts(kb)
    if kb.nphase >= 1:
        m1_emit(kb)
    if kb.nphase >= 2:
        cg_emit(kb)
    if kb.nphase >= 3:
        mo_emit(kb)
    if kb.nphase >= 4:
        tr_emit(kb)
    if kb.nphase >= 5:
        fin_emit(kb)

    kb.close()
    return kb





def m1_emit_consts(kb):
    nc = kb.nc
    P = kb.P
    d = kb.dram
    scl = kb.stream("S")
    gcl = kb.stream("G", depth=8)
    with nc.Block() as blk:
        @blk.sync
        def _(sy):
            for dst, src in [(P["identf"], d["identf"]),
                             (P["identb"], d["identb"]),
                             (P["pswap"], d["pswap"]),
                             (P["cgmask"], d["cgmask"]),
                             (P["bo2p"], d["bo2p"]),
                             (P["bo2n"], d["bo2n"]),
                             (P["sm16"], d["sm16"]),
                             (P["normw2"], d["normw2"]),
                             (P["zm2"], d["zm2"])]:
                scl.issue(sy, dst[:, :], src)

        @blk.gpsimd
        def _(g):
            for c in range(NCH):
                gcl.issue(g, P["sel8b"][:, c * 16:(c + 1) * 16], d["sel8"][c])
                gcl.issue(g, P["selT8b"][:, c * 128:(c + 1) * 128],
                          d["selT8"][c])
            gcl.issue(g, P["blk2b"][:, :], d["blk2"])
            gcl.issue(g, P["bo2pb"][:, :], d["bo2p"])
            gcl.issue(g, P["bo2nb"][:, :], d["bo2n"])
            g.memset(P["onesf"][:, :], 1.0)
            g.memset(P["onesb"][:, :], 1.0)
            g.memset(P["epsc"][:, 0:1], 1e-6)
            g.memset(P["epsc"][:, 1:2], 1e-5)
            g.memset(P["epsc"][:, 2:3], 1e-12)
            g.memset(P["epsc"][:, 3:4], 0.0)

    with nc.Block() as blk:
        @blk.sync
        def _(sy):
            scl.wait_all(sy)
            gcl.wait_all(sy)

    with (nc.sbuf_tensor("m1c_lraw", [128, NCH], F32) as lraw,
          nc.sbuf_tensor("m1c_t1", [128, NCH], F32) as t1,
          nc.sbuf_tensor("m1c_t2", [128, NCH], F32) as t2):
        with nc.Block() as blk:
            @blk.sync
            def _(sy):
                tk = scl.issue(sy, lraw[:, :], kb.dram["lamb_c"])
                scl.wait(sy, tk)
        with nc.Block() as blk:
            @blk.scalar
            def _(s):
                s.activation(t1[:, :], lraw[:, :], AF.Abs)
        with nc.Block() as blk:
            @blk.scalar
            def _(s):
                s.activation(t1[:, :], t1[:, :], AF.Exp, scale=-1.0)
        with nc.Block() as blk:
            @blk.scalar
            def _(s):
                s.activation(t1[:, :], t1[:, :], AF.Ln, bias=1.0)
                s.activation(P["negexpA"][:, :], P["sm16"][:, 1:2], AF.Exp)
        with nc.Block() as blk:
            @blk.vector
            def _(v):
                v.tensor_scalar(t2[:, :], lraw[:, :], 0.0, None, AluOpType.max)
                v.tensor_scalar(P["negexpA"][:, :], P["negexpA"][:, :], -1.0,
                                None, AluOpType.mult)
        with nc.Block() as blk:
            @blk.vector
            def _(v):
                v.scalar_tensor_tensor(P["lamsp"][:, :], t1[:, :], 0.25,
                                       t2[:, :], AluOpType.add, AluOpType.add)


def softplus_16(nc, zsb, t1, out, sign):
    with nc.Block() as blk:
        @blk.scalar
        def _(s):
            s.activation(t1[:, :], zsb[:, :], AF.Abs)
    with nc.Block() as blk:
        @blk.scalar
        def _(s):
            s.activation(t1[:, :], t1[:, :], AF.Exp, scale=-1.0)
    with nc.Block() as blk:
        @blk.scalar
        def _(s):
            s.activation(t1[:, :], t1[:, :], AF.Ln, bias=1.0)

        @blk.vector
        def _(v):
            if sign > 0:
                v.tensor_scalar(out[:, :], zsb[:, :], 0.0, None, AluOpType.max)
            else:
                v.tensor_scalar(out[:, :], zsb[:, :], 0.0, None, AluOpType.min)
    with nc.Block() as blk:
        @blk.vector
        def _(v):
            if sign > 0:
                v.tensor_add(out[:, :], out[:, :], t1[:, :])
            else:
                v.tensor_sub(out[:, :], out[:, :], t1[:, :])


def m1_emit(kb):
    nc = kb.nc
    d = kb.dram
    wl = kb.stream("G", depth=8)
    ws = kb.stream("S")

    with (nc.sbuf_tensor("m1_wab", [128, NCH * 2 * H], BF16) as wab_sb,
          nc.sbuf_tensor("m1_wc", [128, NCH * 8], F32) as wc_sb,
          nc.sbuf_tensor("m1_xb", [128, NCH * T], BF16) as xb):

        with nc.Block() as blk:
            @blk.gpsimd
            def _(g):
                for c in range(NCH):
                    wl.issue(g, wab_sb[:, c * 2 * H:c * 2 * H + H], d["wa"][c])
                    wl.issue(g, wab_sb[:, c * 2 * H + H:(c + 1) * 2 * H],
                             d["wb"][c])
                wl.wait_all(g)

            @blk.sync
            def _(sy):
                for c in range(NCH):
                    ws.issue(sy, wc_sb[:, c * 8:c * 8 + 4], d["wcq"][c])
                    ws.issue(sy, wc_sb[:, c * 8 + 4:c * 8 + 8], d["wck"][c])
                ws.wait_all(sy)

        for b in range(B):
            with nc.Block() as blk:
                @blk.gpsimd
                def _(g):
                    for c in range(NCH):
                        wl.issue(g, xb[:, c * T:(c + 1) * T], d["x_in"][b, c])
                    wl.wait_all(g)
            with (nc.sbuf_tensor(f"m1_w{b}", [128, NCH * D], BF16) as w_sb,
                  nc.sbuf_tensor(f"m1_raw{b}", [128, NCH * (T + 4)], BF16)
                  as rawall,
                  nc.sbuf_tensor(f"m1_qsil{b}", [128, NCH * T], BF16) as qsil,
                  nc.sbuf_tensor(f"m1_convf{b}", [128, 2 * T], F32) as convf,
                  nc.sbuf_tensor(f"m1_sq{b}", [128, 2 * T], BF16) as sqb,
                  nc.sbuf_tensor(f"m1_ssq{b}", [16, T], F32) as ssq_sb,
                  nc.sbuf_tensor(f"m1_rstdb{b}", [16, T], BF16) as rstd_b,
                  nc.sbuf_tensor(f"m1_hat{b}", [128, 2 * T], BF16) as hatb,
                  nc.sbuf_tensor(f"m1_trev{b}", [128, 4 * 128], BF16) as trev):
                m1_emit_b(kb, b, w_sb, wab_sb, wc_sb, xb, rawall,
                       qsil, convf, sqb, ssq_sb, rstd_b, hatb, trev)
            m1_emit_gate(kb, b, wab_sb, xb)


def m1_emit_b(kb, b, w_sb, wab_sb, wc_sb, xb, rawall, qsil,
           convf, sqb, ssq_sb, rstd_b, hatb, trev):
    nc = kb.nc
    P = kb.P
    d = kb.dram
    wl = kb.stream("G", depth=8)
    hd = kb.stream("S")

    for which, wname, hdram in ((0, "wq", "qh"), (1, "wk", "kh")):
        wcoff = which * 4
        with nc.Block() as blk:
            @blk.gpsimd
            def _(g):
                for c in range(NCH):
                    wl.issue(g, w_sb[:, c * D:(c + 1) * D], d[wname][c])
                wl.wait_all(g)

        with nc.psum_tensor(f"m1_ps{b}{which}", [128, 2048],
                            mybir.dt.float32) as mps:
            with SemScope(kb, 2, "m1A") as (smm, sev):
                with nc.Block() as blk:
                    @blk.tensor
                    def _(t):
                        for c in range(NCH):
                            for tt in range(4):
                                slot = c * 4 + tt
                                if slot >= 4:
                                    t.wait_ge(sev, slot - 3)
                                for kc in range(NCH):
                                    ins = t.matmul(
                                        mps[:, tt * 512:(tt + 1) * 512],
                                        w_sb[:, kc * D + c * 128:
                                             kc * D + (c + 1) * 128],
                                        xb[:, kc * T + tt * 512:
                                           kc * T + (tt + 1) * 512],
                                        start=(kc == 0), stop=(kc == NCH - 1))
                                ins.then_inc(smm, 1)

                    @blk.scalar
                    def _(s):
                        for c in range(NCH):
                            s.memzero(rawall[:, c * (T + 4):c * (T + 4) + 4])
                            for tt in range(4):
                                slot = c * 4 + tt
                                s.wait_ge(smm, slot + 1)
                                s.copy(rawall[:, c * (T + 4) + 4 + tt * 512:
                                              c * (T + 4) + 4 + (tt + 1) * 512],
                                       mps[:, tt * 512:(tt + 1) * 512]
                                       ).then_inc(sev, 1)

            with (nc.psum_tensor(f"m1_ss{b}{which}", [16, 2048],
                                 mybir.dt.float32) as ssps,
                  SemScope(kb, 5, "m1B") as (scv, ssl, ssp, smq, sch)):
                nch_chain = [0]
                with nc.Block() as blk:
                    @blk.vector
                    def _(v):
                        for c in range(NCH):
                            if c >= 2:
                                v.wait_ge(ssl, c - 1)
                            buf = convf[:, (c % 2) * T:(c % 2 + 1) * T]
                            ro = c * (T + 4) + 1
                            wco = c * 8 + wcoff
                            v.tensor_scalar(buf, rawall[:, ro:ro + T],
                                            wc_sb[:, wco:wco + 1], None,
                                            AluOpType.mult).then_inc(sch, 1)
                            nch_chain[0] += 1
                            for i in range(1, 4):
                                v.wait_ge(sch, nch_chain[0])
                                ins = v.scalar_tensor_tensor(
                                    buf, rawall[:, ro + i:ro + i + T],
                                    wc_sb[:, wco + i:wco + i + 1],
                                    buf, AluOpType.mult, AluOpType.add)
                                if i < 3:
                                    ins.then_inc(sch, 1)
                                    nch_chain[0] += 1
                            ins.then_inc(scv, 1)

                    @blk.scalar
                    def _(s):
                        for c in range(NCH):
                            s.wait_ge(scv, c + 1)
                            s.activation(qsil[:, c * T:(c + 1) * T],
                                         convf[:, (c % 2) * T:(c % 2 + 1) * T],
                                         AF.Silu).then_inc(ssl, 1)

                    @blk.gpsimd
                    def _(g):
                        for c in range(NCH):
                            g.wait_ge(ssl, c + 1)
                            if c >= 2:
                                g.wait_ge(smq, c - 1)
                            g.tensor_mul(sqb[:, (c % 2) * T:(c % 2 + 1) * T],
                                         qsil[:, c * T:(c + 1) * T],
                                         qsil[:, c * T:(c + 1) * T]
                                         ).then_inc(ssp, 1)

                    @blk.tensor
                    def _(t):
                        for c in range(NCH):
                            t.wait_ge(ssp, c + 1)
                            for tt in range(4):
                                ins = t.matmul(
                                    ssps[:, tt * 512:(tt + 1) * 512],
                                    P["sel8b"][:, c * 16:(c + 1) * 16],
                                    sqb[:, (c % 2) * T + tt * 512:
                                        (c % 2) * T + (tt + 1) * 512],
                                    start=(c == 0), stop=(c == NCH - 1),
                                    skip_group_check=True)
                            ins.then_inc(smq, 1)

                with nc.Block() as blk:
                    @blk.vector
                    def _(v):
                        for tt in range(4):
                            v.tensor_copy(ssq_sb[:, tt * 512:(tt + 1) * 512],
                                          ssps[:, tt * 512:(tt + 1) * 512])
                with nc.Block() as blk:
                    @blk.scalar
                    def _(s):
                        s.activation(ssq_sb[:, :], ssq_sb[:, :], AF.Ln,
                                     bias=P["epsc"][0:16, 0:1])
                with nc.Block() as blk:
                    @blk.scalar
                    def _(s):
                        s.activation(rstd_b[:, :], ssq_sb[:, :], AF.Exp,
                                     scale=-0.5)
                if b == 0 and which == 0 and "dbg_rstd" in kb.dram:
                    with nc.Block() as blk:
                        @blk.sync
                        def _(sy):
                            st = kb.stream("S")
                            tks = [st.issue(sy, kb.dram["dbg_rstd"], rstd_b[:, :]),
                                   st.issue(sy, kb.dram["dbg_ssq"], ssq_sb[:, :]),
                                   st.issue(sy, kb.dram["dbg_qsil"], qsil[:, :]),
                                   st.issue(sy, kb.dram["dbg_raw"], rawall[:, :])]
                            for tk in tks:
                                st.wait(sy, tk)

            with SemScope(kb, 2, "m1D") as (sbc, sht):
                hd_base = hd.i
                with nc.Block() as blk:
                    @blk.tensor
                    def _(t):
                        for c in range(NCH):
                            for tt in range(4):
                                slot = c * 4 + tt
                                if slot >= 4:
                                    t.wait_ge(sht, slot - 3)
                                t.matmul(mps[:, tt * 512:(tt + 1) * 512],
                                         P["selT8b"][:, c * 128:(c + 1) * 128],
                                         rstd_b[:, tt * 512:(tt + 1) * 512],
                                         start=True, stop=True,
                                         skip_group_check=True
                                         ).then_inc(sbc, 1)

                    @blk.vector
                    def _(v):
                        for c in range(NCH):
                            if c >= 2:
                                hd.wait(v, hd_base + c - 1)
                            for tt in range(4):
                                slot = c * 4 + tt
                                v.wait_ge(sbc, slot + 1)
                                v.tensor_mul(
                                    hatb[:, (c % 2) * T + tt * 512:
                                         (c % 2) * T + (tt + 1) * 512],
                                    qsil[:, c * T + tt * 512:
                                         c * T + (tt + 1) * 512],
                                    mps[:, tt * 512:(tt + 1) * 512]
                                    ).then_inc(sht, 1)

                    @blk.sync
                    def _(sy):
                        for c in range(NCH):
                            sy.wait_ge(sht, c * 4 + 4)
                            hd.issue(sy, d[hdram][b, c],
                                     hatb[:, (c % 2) * T:(c % 2 + 1) * T])
                        hd.wait_all(sy)

        if which == 1:
            m1_emit_tr_tokmajor(kb, b, trev, "kn_s")

    with nc.Block() as blk:
        @blk.gpsimd
        def _(g):
            for c in range(NCH):
                wl.issue(g, w_sb[:, c * D:(c + 1) * D], d["wv"][c])
            wl.wait_all(g)

    with (nc.psum_tensor(f"m1_pv{b}", [128, 2048], mybir.dt.float32) as mps,
          SemScope(kb, 2, "m1V") as (smm, sev)):
        hd_base = hd.i
        with nc.Block() as blk:
            @blk.tensor
            def _(t):
                for c in range(NCH):
                    for tt in range(4):
                        slot = c * 4 + tt
                        if slot >= 4:
                            t.wait_ge(sev, slot - 3)
                        for kc in range(NCH):
                            ins = t.matmul(
                                mps[:, tt * 512:(tt + 1) * 512],
                                w_sb[:, kc * D + c * 128:kc * D + (c + 1) * 128],
                                xb[:, kc * T + tt * 512:kc * T + (tt + 1) * 512],
                                start=(kc == 0), stop=(kc == NCH - 1))
                        ins.then_inc(smm, 1)

            @blk.scalar
            def _(s):
                for c in range(NCH):
                    if c >= 2:
                        hd.wait(s, hd_base + c - 1)
                    for tt in range(4):
                        slot = c * 4 + tt
                        s.wait_ge(smm, slot + 1)
                        s.copy(hatb[:, (c % 2) * T + tt * 512:
                                    (c % 2) * T + (tt + 1) * 512],
                               mps[:, tt * 512:(tt + 1) * 512]).then_inc(sev, 1)

            @blk.sync
            def _(sy):
                for c in range(NCH):
                    sy.wait_ge(sev, c * 4 + 4)
                    hd.issue(sy, d["mo"][b, c],
                             hatb[:, (c % 2) * T:(c % 2 + 1) * T])
                hd.wait_all(sy)

    m1_emit_tr_tokmajor(kb, b, trev, "vn_s")


def m1_emit_tr_tokmajor(kb, b, trev, dram_name):
    nc = kb.nc
    P = kb.P
    d = kb.dram
    src = {"kn_s": "kh", "vn_s": "mo"}[dram_name]
    tl = kb.stream("S")
    tg = kb.stream("G", depth=8)
    with (nc.sbuf_tensor(f"m1t_in{b}{dram_name}", [128, 2 * T], BF16) as tin,
          nc.psum_tensor(f"m1t_ps{b}{dram_name}", [128, 2048],
                         mybir.dt.bfloat16) as tps,
          SemScope(kb, 2, "m1T") as (str_, sev)):
        tl_base = tl.i
        tg_base = tg.i
        with nc.Block() as blk:
            @blk.sync
            def _(sy):
                for c in range(NCH):
                    if c >= 2:
                        sy.wait_ge(str_, (c - 1) * 16)
                    tl.issue(sy, tin[:, (c % 2) * T:(c % 2 + 1) * T],
                             d[src][b, c])

            @blk.tensor
            def _(t):
                for c in range(NCH):
                    tl.wait(t, tl_base + c + 1)
                    for ts in range(16):
                        slot = c * 16 + ts
                        if slot >= 2:
                            t.wait_ge(sev, slot - 1)
                        t.transpose(tps[:, (slot % 2) * 1024:
                                        (slot % 2) * 1024 + 128],
                                    tin[:, (c % 2) * T + ts * 128:
                                        (c % 2) * T + (ts + 1) * 128],
                                    P["identb"][:, :]).then_inc(str_, 1)

            @blk.vector
            def _(v):
                for c in range(NCH):
                    for ts in range(16):
                        slot = c * 16 + ts
                        v.wait_ge(str_, slot + 1)
                        if slot >= 4:
                            tg.wait(v, tg_base + 2 * (slot - 4) + 1)
                            tg.wait(v, tg_base + 2 * (slot - 4) + 2)
                        v.tensor_copy(trev[:, (slot % 4) * 128:
                                           (slot % 4 + 1) * 128],
                                      tps[:, (slot % 2) * 1024:
                                          (slot % 2) * 1024 + 128]
                                      ).then_inc(sev, 1)

            @blk.gpsimd
            def _(g):
                for c in range(NCH):
                    for ts in range(16):
                        slot = c * 16 + ts
                        g.wait_ge(sev, slot + 1)
                        tg.issue(g, d[dram_name][b, 2 * c, ts],
                                 trev[:, (slot % 4) * 128:(slot % 4) * 128 + 64])
                        tg.issue(g, d[dram_name][b, 2 * c + 1, ts],
                                 trev[:, (slot % 4) * 128 + 64:
                                      (slot % 4 + 1) * 128])
                tg.wait_all(g)


def m1_emit_gate(kb, b, wab_sb, xb):
    nc = kb.nc
    P = kb.P
    with (nc.sbuf_tensor(f"m1g_za{b}", [16, T], F32) as za,
          nc.sbuf_tensor(f"m1g_zb{b}", [16, T], F32) as zb,
          nc.sbuf_tensor(f"m1g_t1{b}", [16, T], F32) as gt1,
          nc.sbuf_tensor(f"m1g_sp{b}", [16, T], F32) as sp,
          nc.sbuf_tensor(f"m1g_lnb{b}", [16, T], F32) as lnb_s,
          nc.sbuf_tensor(f"m1g_gam{b}", [16, T], F32) as gam_s,
          nc.sbuf_tensor(f"m1g_ce{b}", [16, T], F32) as ce_s):
        with (nc.psum_tensor(f"m1g_pa{b}", [16, 2048], mybir.dt.float32) as pa,
              nc.psum_tensor(f"m1g_pb{b}", [16, 2048], mybir.dt.float32) as pb):
            with nc.Block() as blk:
                @blk.tensor
                def _(t):
                    for tt in range(4):
                        for kc in range(NCH):
                            t.matmul(pa[:, tt * 512:(tt + 1) * 512],
                                     wab_sb[:, kc * 2 * H:kc * 2 * H + H],
                                     xb[:, kc * T + tt * 512:
                                        kc * T + (tt + 1) * 512],
                                     start=(kc == 0), stop=(kc == NCH - 1))
                        for kc in range(NCH):
                            t.matmul(pb[:, tt * 512:(tt + 1) * 512],
                                     wab_sb[:, kc * 2 * H + H:(kc + 1) * 2 * H],
                                     xb[:, kc * T + tt * 512:
                                        kc * T + (tt + 1) * 512],
                                     start=(kc == 0), stop=(kc == NCH - 1))
            with nc.Block() as blk:
                @blk.vector
                def _(v):
                    for tt in range(4):
                        sl = slice(tt * 512, (tt + 1) * 512)
                        v.tensor_scalar(za[:, sl], pa[:, sl],
                                        P["sm16"][:, 2:3], None, AluOpType.add)
                        v.tensor_scalar(zb[:, sl], pb[:, sl],
                                        P["sm16"][:, 0:1], None, AluOpType.add)

        softplus_16(nc, za, gt1, sp, +1)
        with nc.Block() as blk:
            @blk.vector
            def _(v):
                v.tensor_scalar(za[:, :], sp[:, :], P["negexpA"][:, 0:1],
                                None, AluOpType.mult)
        softplus_16(nc, zb, gt1, lnb_s, -1)
        with nc.Block() as blk:
            @blk.vector
            def _(v):
                v.tensor_tensor_scan(sp[:, :], za[:, :], za[:, :], 0.0,
                                     AluOpType.add, AluOpType.bypass)
        with nc.Block() as blk:
            @blk.vector
            def _(v):
                v.tensor_copy(gam_s[:, 0:CC], sp[:, 0:CC])
                for c in range(1, NCK):
                    v.tensor_scalar(gam_s[:, c * CC:(c + 1) * CC],
                                    sp[:, c * CC:(c + 1) * CC],
                                    sp[:, c * CC - 1:c * CC], None,
                                    AluOpType.subtract)
        with nc.Block() as blk:
            @blk.scalar
            def _(s):
                s.activation(P["d_hm"][:, b * T:(b + 1) * T], gam_s[:, :],
                             AF.Exp)
        with nc.Block() as blk:
            @blk.vector
            def _(v):
                v.tensor_sub(sp[:, :], lnb_s[:, :], gam_s[:, :])
                for c in range(NCK):
                    sl = slice(c * CC, (c + 1) * CC)
                    v.scalar_tensor_tensor(
                        ce_s[:, sl], gam_s[:, sl],
                        gam_s[:, (c + 1) * CC - 1:(c + 1) * CC], lnb_s[:, sl],
                        AluOpType.subtract, AluOpType.subtract)
        with nc.Block() as blk:
            @blk.scalar
            def _(s):
                s.activation(ce_s[:, :], ce_s[:, :], AF.Exp, scale=-1.0)

        for srcbuf, dstname in ((gam_s, "gamT"), (sp, "lnbmgT"), (ce_s, "ceT")):
            dst = kb.P[dstname]
            with (nc.psum_tensor(f"m1g_tr{b}{dstname}", [128, 1024],
                                 mybir.dt.float32) as trf,
                  SemScope(kb, 2, "m1Gt") as (st1, st2)):
                with nc.Block() as blk:
                    @blk.tensor
                    def _(t):
                        for c in range(NCK):
                            if c >= 2:
                                t.wait_ge(st2, c - 1)
                            t.transpose(trf[:, (c % 2) * 512:(c % 2) * 512 + 16],
                                        srcbuf[:, c * CC:(c + 1) * CC],
                                        P["identf"][0:16, 0:16]
                                        ).then_inc(st1, 1)

                    @blk.vector
                    def _(v):
                        for c in range(NCK):
                            v.wait_ge(st1, c + 1)
                            col = (b * NCK + c) * 16
                            v.tensor_copy(dst[:, col:col + 16],
                                          trf[:, (c % 2) * 512:
                                              (c % 2) * 512 + 16]
                                          ).then_inc(st2, 1)





CGLIM = 9


def cg_emit(kb):
    nc = kb.nc
    with ExitStack() as stk:
        def mk(name, shape, dt):
            return stk.enter_context(nc.sbuf_tensor(name, shape, dt))
        bufs = dict(
            ktm=mk("cg_ktm", [128, 4 * T], BF16),
            q2=mk("cg_q", [128, 2 * T], BF16),
            kn4=mk("cg_kn", [128, 4 * NCK * DK], BF16),
            vn4=mk("cg_vn", [128, 4 * NCK * DK], BF16),
            A4=mk("cg_A", [128, 4 * NCK * CC], BF16),
            D2=mk("cg_D", [128, 2 * T], BF16),
            X2=mk("cg_X", [128, 2 * T], BF16),
            R2=mk("cg_R", [128, 2 * T], BF16),
            P2=mk("cg_P", [128, 2 * T], BF16),
            Pt2=mk("cg_Pt", [128, 2 * T], BF16),
            Skkb=mk("cg_Skk", [128, 2 * NCK * CC], BF16),
            Skvb=mk("cg_Skv", [128, 2 * NCK * CC], BF16),
            Lb2=mk("cg_Lb", [128, 2 * 128], BF16),
            Ktl=mk("cg_Ktl", [128, 4 * 64], BF16),
            eIb=mk("cg_eI", [128, 2 * 128], BF16),
            W4=mk("cg_W", [128, 4 * 512], BF16),
            tmb=mk("cg_tm", [128, 2 * 512], BF16),
            Apb=mk("cg_Apb", [128, 2 * 512], BF16),
            mub=mk("cg_mu", [128, 6 * 512], BF16),
            r2b=mk("cg_r2", [128, 2 * 512], BF16),
            rsA=mk("cg_rsA", [2, 2 * T], BF16),
            rsB=mk("cg_rsB", [2, 2 * T], BF16),
            al2=mk("cg_al", [2, 2 * 512], BF16),
            be2=mk("cg_be", [2, 2 * 512], BF16),
            sc1=mk("cg_sc1", [2, 4 * 512], F32),
            sc2=mk("cg_sc2", [2, 4 * 512], BF16),
            dhb=mk("cg_dh", [16, T], BF16),
            crb=mk("cg_cr", [128, 2 * 128], F32),
            dlb=mk("cg_dl", [128, 2 * 128], F32),
            dEnd=mk("cg_dEnd", [128, 32], F32),
        )
        bufs["ob2"] = bufs["q2"]
        ngroups = getattr(kb, "cg_ngroups", 8)
        for g in range(ngroups):
            cg_emit_group(kb, g, g // 4, 4 * (g % 4), bufs)


def cg_emit_group(kb, g, b, hbase, bf):
    nc = kb.nc
    P = kb.P
    d = kb.dram
    ld = kb.stream("S")

    q2, kn4, vn4, A4 = bf["q2"], bf["kn4"], bf["vn4"], bf["A4"]
    D2, X2, R2, P2, Pt2 = bf["D2"], bf["X2"], bf["R2"], bf["P2"], bf["Pt2"]
    Skkb, Skvb, Lb2 = bf["Skkb"], bf["Skvb"], bf["Lb2"]
    Ktl, eIb, W4, tmb, Apb = (bf["Ktl"], bf["eIb"], bf["W4"], bf["tmb"],
                              bf["Apb"])
    mub, r2b, rsA, rsB = bf["mub"], bf["r2b"], bf["rsA"], bf["rsB"]
    al2, be2, sc1, sc2 = bf["al2"], bf["be2"], bf["sc1"], bf["sc2"]
    dhb, crb, dlb, ob2 = bf["dhb"], bf["crb"], bf["dlb"], bf["ob2"]
    dEnd = bf["dEnd"]

    ktm = bf["ktm"]
    with nc.Block() as blk:
        @blk.sync
        def _(sy):
            for j in range(2):
                hp = hbase // 2 + j
                ld.issue(sy, ktm[:, (2 * j) * T:(2 * j + 1) * T],
                         d["kh"][b, hp])
                ld.issue(sy, ktm[:, (2 * j + 1) * T:(2 * j + 2) * T],
                         d["kh"][b, hp])
                ld.issue(sy, q2[:, j * T:(j + 1) * T], d["qh"][b, hp])
            for hh in range(4):
                h = hbase + hh
                for c in range(NCK):
                    ld.issue(sy, kn4[:, (hh * NCK + c) * DK:
                                     (hh * NCK + c + 1) * DK],
                             d["kn_s"][b, h, c])
                    ld.issue(sy, vn4[:, (hh * NCK + c) * DK:
                                     (hh * NCK + c + 1) * DK],
                             d["vn_s"][b, h, c])
            ld.wait_all(sy)

    with nc.Block() as blk:
        @blk.vector
        def _(v):
            for hh in range(4):
                pj = hh % 2
                v.tensor_scalar(ktm[:, hh * T:(hh + 1) * T],
                                ktm[:, hh * T:(hh + 1) * T],
                                P["zm2"][:, pj:pj + 1], None,
                                AluOpType.mult)

    if CGLIM < 1:
        return
    with (nc.psum_tensor(f"cg_psD{g}", [128, 2048], PS) as psD,
          SemScope(kb, 2, "cgD") as (s1, s2)):
        with nc.Block() as blk:
            @blk.vector
            def _(v):
                v.tensor_copy(dhb[:, :], P["d_hm"][0:16, b * T:(b + 1) * T])
        with nc.Block() as blk:
            @blk.tensor
            def _(t):
                for j in range(2):
                    hp = hbase // 2 + j
                    for tt in range(4):
                        if j == 1:
                            t.wait_ge(s2, tt + 1)
                        t.matmul(psD[:, tt * 512:(tt + 1) * 512],
                                 P["selT8b"][:, hp * 128:(hp + 1) * 128],
                                 dhb[0:16, tt * 512:(tt + 1) * 512],
                                 start=True, stop=True,
                                 skip_group_check=True).then_inc(s1, 1)

            @blk.vector
            def _(v):
                for j in range(2):
                    for tt in range(4):
                        v.wait_ge(s1, j * 4 + tt + 1)
                        v.tensor_copy(D2[:, j * T + tt * 512:
                                         j * T + (tt + 1) * 512],
                                      psD[:, tt * 512:(tt + 1) * 512])
                        for dc in range(4):
                            c = tt * 4 + dc
                            ins = v.tensor_copy(
                                dEnd[:, j * 16 + c:j * 16 + c + 1],
                                psD[:, tt * 512 + dc * 128 + 127:
                                    tt * 512 + dc * 128 + 128])
                        ins.then_inc(s2, 1)

            @blk.gpsimd
            def _(gp):
                for j in range(2):
                    hp = hbase // 2 + j
                    gp.tensor_scalar(Lb2[:, j * 128:(j + 1) * 128],
                                     P["identb"][:, :],
                                     P["lamsp"][:, hp:hp + 1], None,
                                     AluOpType.mult)

    if CGLIM < 2:
        return
    with (nc.psum_tensor(f"cg_psA{g}", [128, 1024], PS) as psA,
          SemScope(kb, 4, "cgA") as (s1, s2, s3, s4)):
        with nc.Block() as blk:
            @blk.vector
            def _(v):
                for i in range(66):
                    if i < 64:
                        hh, c = i // NCK, i % NCK
                        col = (b * NCK + c) * 16 + hbase + hh
                        if i >= 2:
                            v.wait_ge(s2, i - 1)
                        v.tensor_scalar(crb[:, (i % 2) * 128:(i % 2 + 1) * 128],
                                        P["onesf"][:, :],
                                        P["gamT"][:, col:col + 1], None,
                                        AluOpType.mult).then_inc(s1, 1)
                    if i >= 2:
                        k = i - 2
                        hh, c = k // NCK, k % NCK
                        col = (b * NCK + c) * 16 + hbase + hh
                        v.wait_ge(s2, k + 1)
                        if k >= 2:
                            v.wait_ge(s4, k - 1)
                        v.scalar_tensor_tensor(
                            dlb[:, (k % 2) * 128:(k % 2 + 1) * 128],
                            psA[:, (k % 2) * 512:(k % 2) * 512 + 128],
                            P["lnbmgT"][:, col:col + 1], P["cgmask"][:, :],
                            AluOpType.add, AluOpType.add).then_inc(s3, 1)

            @blk.tensor
            def _(t):
                for i in range(64):
                    t.wait_ge(s1, i + 1)
                    if i >= 2:
                        t.wait_ge(s3, i - 1)
                    t.transpose(psA[:, (i % 2) * 512:(i % 2) * 512 + 128],
                                crb[:, (i % 2) * 128:(i % 2 + 1) * 128],
                                P["identf"][:, :]).then_inc(s2, 1)

            @blk.scalar
            def _(s):
                for i in range(64):
                    hh, c = i // NCK, i % NCK
                    s.wait_ge(s3, i + 1)
                    s.activation(A4[:, (hh * NCK + c) * CC:
                                    (hh * NCK + c + 1) * CC],
                                 dlb[:, (i % 2) * 128:(i % 2 + 1) * 128],
                                 AF.Exp).then_inc(s4, 1)

    if CGLIM < 3:
        return
    with (nc.psum_tensor(f"cg_psS{g}", [128, 512], PS) as psS,
          SemScope(kb, 3, "cgS") as (s1, s2, s3)):
        with nc.Block() as blk:
            @blk.vector
            def _(v):
                for j in range(2):
                    v.memset(Skkb[:, j * NCK * CC:j * NCK * CC + CC], 0.0)
                    v.memset(Skvb[:, j * NCK * CC:j * NCK * CC + CC], 0.0)
        with nc.Block() as blk:
            @blk.gpsimd
            def _(gp):
                for c in range(NCK - 1):
                    if c >= 1:
                        gp.wait_ge(s2, c)
                    for j in range(2):
                        for pj in range(2):
                            hh = 2 * j + pj
                            col = (b * NCK + c) * 16 + hbase + hh
                            gp.tensor_scalar(
                                Ktl[:, hh * 64:(hh + 1) * 64],
                                kn4[:, (hh * NCK + c) * DK:
                                    (hh * NCK + c + 1) * DK],
                                P["ceT"][:, col:col + 1], None,
                                AluOpType.mult)
                        ins = gp.tensor_scalar(
                            eIb[:, j * 128:(j + 1) * 128], P["identb"][:, :],
                            dEnd[:, j * 16 + c:j * 16 + c + 1],
                            None, AluOpType.mult)
                    ins.then_inc(s1, 1)

            @blk.tensor
            def _(t):
                for c in range(NCK - 1):
                    t.wait_ge(s1, c + 1)
                    if c >= 1:
                        t.wait_ge(s3, c)
                    for j in range(2):
                        for kind, src, dstb in ((0, kn4, Skkb), (1, vn4, Skvb)):
                            pso = (2 * j + kind) * 128
                            t.matmul(psS[:, pso:pso + 128],
                                     eIb[:, j * 128:(j + 1) * 128],
                                     dstb[:, j * NCK * CC + c * CC:
                                          j * NCK * CC + (c + 1) * CC],
                                     start=True, stop=False,
                                     skip_group_check=True)
                            for pj in range(2):
                                hh = 2 * j + pj
                                ins = t.matmul(
                                    psS[pj * 64:(pj + 1) * 64,
                                        pso + pj * 64:pso + (pj + 1) * 64],
                                    Ktl[:, hh * 64:(hh + 1) * 64],
                                    src[:, (hh * NCK + c) * DK:
                                        (hh * NCK + c + 1) * DK],
                                    start=False, stop=(pj == 1),
                                    skip_group_check=True)
                    ins.then_inc(s2, 1)

            @blk.vector
            def _(v):
                for c in range(NCK - 1):
                    v.wait_ge(s2, c + 1)
                    for j in range(2):
                        v.tensor_copy(
                            Skkb[:, j * NCK * CC + (c + 1) * CC:
                                 j * NCK * CC + (c + 2) * CC],
                            psS[:, (2 * j) * 128:(2 * j + 1) * 128])
                        ins = v.tensor_copy(
                            Skvb[:, j * NCK * CC + (c + 1) * CC:
                                 j * NCK * CC + (c + 2) * CC],
                            psS[:, (2 * j + 1) * 128:(2 * j + 2) * 128])
                    ins.then_inc(s3, 1)

    if CGLIM < 4:
        return
    with (nc.psum_tensor(f"cg_psI{g}", [2, 1024], PS) as psI,
          SemScope(kb, 3, "cgI") as (s1, s2, s3)):
        with nc.Block() as blk:
            @blk.vector
            def _(v):
                for j in range(2):
                    v.memset(X2[:, j * T:(j + 1) * T], 0.0)
                    v.tensor_copy(R2[:, j * T:(j + 1) * T],
                                  q2[:, j * T:(j + 1) * T])
                    v.tensor_copy(P2[:, j * T:(j + 1) * T],
                                  q2[:, j * T:(j + 1) * T])
        with nc.Block() as blk:
            @blk.gpsimd
            def _(gp):
                for w in range(4):
                    for j in range(2):
                        if 2 * w + j >= 2:
                            gp.wait_ge(s2, 2 * w + j - 1)
                        gp.tensor_mul(r2b[:, j * 512:(j + 1) * 512],
                                      q2[:, j * T + w * 512:
                                         j * T + (w + 1) * 512],
                                      q2[:, j * T + w * 512:
                                         j * T + (w + 1) * 512]
                                      ).then_inc(s1, 1)

            @blk.tensor
            def _(t):
                for w in range(4):
                    for j in range(2):
                        t.wait_ge(s1, 2 * w + j + 1)
                        if 2 * w + j >= 2:
                            t.wait_ge(s3, 2 * w + j - 1)
                        t.matmul(psI[:, j * 512:(j + 1) * 512],
                                 P["blk2b"][:, :],
                                 r2b[:, j * 512:(j + 1) * 512],
                                 start=True, stop=True,
                                 skip_group_check=True).then_inc(s2, 1)

            @blk.scalar
            def _(s):
                for w in range(4):
                    for j in range(2):
                        s.wait_ge(s2, 2 * w + j + 1)
                        s.copy(rsA[:, j * T + w * 512:j * T + (w + 1) * 512],
                               psI[:, j * 512:(j + 1) * 512]).then_inc(s3, 1)

    if CGLIM >= 50:
        kb_cgprobe.emit_iter_probe(kb, g, b, hbase, bf, CGLIM - 50)
        return
    if CGLIM < 5:
        return
    emit_iters(kb, g, b, hbase, bf)

    if CGLIM < 6:
        return
    cg_emit_out(kb, g, b, hbase, bf)


def emit_iters(kb, g, b, hbase, bf):
    nc = kb.nc
    P = kb.P
    kn4, A4 = bf["kn4"], bf["A4"]
    D2, X2, R2, P2, Pt2 = bf["D2"], bf["X2"], bf["R2"], bf["P2"], bf["Pt2"]
    Skkb, Lb2 = bf["Skkb"], bf["Lb2"]
    W4, tmb, Apb, mub, r2b = (bf["W4"], bf["tmb"], bf["Apb"], bf["mub"],
                              bf["r2b"])
    al2, be2, sc1, sc2 = bf["al2"], bf["be2"], bf["sc1"], bf["sc2"]
    rsAB = (bf["rsA"], bf["rsB"])

    with (nc.psum_tensor(f"cg_psU{g}", [128, 2048], PS) as psU,
          nc.psum_tensor(f"cg_psa{g}", [128, 1024], PS) as psa,
          nc.psum_tensor(f"cg_pss{g}", [2, 1024], PS) as pss,
          SemScope(kb, 17, "cgit") as sems):
        (s_u, s_wd, s_ap, s_tm, s_ae, s_pp, s_lnA, s_expA, s_al, s_bc,
         s_mu, s_pt, s_r2, s_rs, s_rc, s_be, s_ga) = sems

        with nc.Block() as blk:
            @blk.tensor
            def _(t):
                for it in range(NIT):
                    for w in range(4):
                        W = 4 * it + w
                        for hh in range(4):
                            j, base = hh // 2, (hh % 2) * 64
                            if W >= 1:
                                t.wait_ge(s_wd, 4 * (W - 1) + hh + 1)
                                t.wait_ge(s_mu, 6 * (W - 1) + 6)
                            if W >= 4:
                                t.wait_ge(s_ga, 6 * (W - 4) + 6)
                            for dc in range(4):
                                c = 4 * w + dc
                                t.matmul(
                                    psU[:, hh * 512 + dc * 128:
                                        hh * 512 + (dc + 1) * 128],
                                    bf["ktm"][:, hh * T + c * CC:
                                              hh * T + (c + 1) * CC],
                                    P2[:, j * T + c * CC:
                                       j * T + (c + 1) * CC],
                                    start=True, stop=True,
                                    skip_group_check=True).then_inc(s_u, 1)
                        t.wait_ge(s_pt, 8 * it + 2 * w + 2)
                        for j in range(2):
                            t.wait_ge(s_wd, 4 * W + 2 * j + 2)
                            if W >= 1:
                                t.wait_ge(s_tm, 2 * (W - 1) + j + 1)
                                t.wait_ge(s_ae, 2 * (W - 1) + j + 1)
                                t.wait_ge(s_mu, 6 * (W - 1) + 4 + j + 1)
                            for dc in range(4):
                                c = 4 * w + dc
                                po = j * 512 + dc * 128
                                t.matmul(psa[:, po:po + 128],
                                         Skkb[:, j * NCK * CC + c * CC:
                                              j * NCK * CC + (c + 1) * CC],
                                         Pt2[:, j * T + c * CC:
                                             j * T + (c + 1) * CC],
                                         start=True, stop=False,
                                         skip_group_check=True)
                                t.matmul(psa[:, po:po + 128],
                                         Lb2[:, j * 128:(j + 1) * 128],
                                         P2[:, j * T + c * CC:
                                            j * T + (c + 1) * CC],
                                         start=False, stop=False,
                                         skip_group_check=True)
                                t.matmul(psa[0:64, po:po + 128],
                                         kn4[:, (2 * j * NCK + c) * DK:
                                             (2 * j * NCK + c + 1) * DK],
                                         W4[:, 2 * j * 512 + dc * 128:
                                            2 * j * 512 + (dc + 1) * 128],
                                         start=False, stop=False,
                                         skip_group_check=True)
                                t.matmul(psa[64:128, po:po + 128],
                                         kn4[:, ((2 * j + 1) * NCK + c) * DK:
                                             ((2 * j + 1) * NCK + c + 1) * DK],
                                         W4[:, (2 * j + 1) * 512 + dc * 128:
                                            (2 * j + 1) * 512 + (dc + 1) * 128],
                                         start=False, stop=True,
                                         skip_group_check=True
                                         ).then_inc(s_ap, 1)
                        for j in range(2):
                            t.wait_ge(s_tm, 2 * W + j + 1)
                            if W >= 1:
                                t.wait_ge(s_rc, 2 * (W - 1) + j + 1)
                            t.matmul(pss[:, j * 512:(j + 1) * 512],
                                     P["blk2b"][:, :],
                                     tmb[:, j * 512:(j + 1) * 512],
                                     start=True, stop=True,
                                     skip_group_check=True).then_inc(s_pp, 1)
                        for j in range(2):
                            t.wait_ge(s_al, 2 * W + j + 1)
                            t.wait_ge(s_wd, 4 * W + j + 1)
                            t.matmul(psU[:, j * 512:(j + 1) * 512],
                                     P["bo2pb"][:, :],
                                     al2[:, j * 512:(j + 1) * 512],
                                     start=True, stop=True,
                                     skip_group_check=True).then_inc(s_bc, 1)
                        for j in range(2):
                            t.wait_ge(s_wd, 4 * W + 2 + j + 1)
                            t.matmul(psU[:, (2 + j) * 512:(3 + j) * 512],
                                     P["bo2nb"][:, :],
                                     al2[:, j * 512:(j + 1) * 512],
                                     start=True, stop=True,
                                     skip_group_check=True).then_inc(s_bc, 1)
                        for j in range(2):
                            t.wait_ge(s_r2, 2 * W + j + 1)
                            t.wait_ge(s_lnA, 4 * W + j + 1)
                            t.matmul(pss[:, j * 512:(j + 1) * 512],
                                     P["blk2b"][:, :],
                                     r2b[:, j * 512:(j + 1) * 512],
                                     start=True, stop=True,
                                     skip_group_check=True).then_inc(s_rs, 1)
                        for j in range(2):
                            t.wait_ge(s_be, 2 * W + j + 1)
                            t.wait_ge(s_tm, 2 * W + j + 1)
                            t.wait_ge(s_ae, 2 * W + j + 1)
                            t.matmul(psa[:, j * 512:(j + 1) * 512],
                                     P["bo2pb"][:, :],
                                     be2[:, j * 512:(j + 1) * 512],
                                     start=True, stop=True,
                                     skip_group_check=True).then_inc(s_bc, 1)

            @blk.vector
            def _(v):
                for it in range(NIT):
                    rsO = rsAB[it % 2]
                    rsN = rsAB[(it + 1) % 2]
                    for w in range(4):
                        W = 4 * it + w
                        for hh in range(4):
                            v.wait_ge(s_u, 16 * W + 4 * (hh + 1))
                            if W >= 1:
                                v.wait_ge(s_ap, 8 * (W - 1) + 8)
                            v.tensor_mul(W4[:, hh * 512:(hh + 1) * 512],
                                         A4[:, hh * NCK * CC + 4 * w * CC:
                                            hh * NCK * CC + 4 * (w + 1) * CC],
                                         psU[:, hh * 512:(hh + 1) * 512]
                                         ).then_inc(s_wd, 1)
                        for j in range(2):
                            v.wait_ge(s_ap, 8 * W + 4 * (j + 1))
                            if W >= 1:
                                v.wait_ge(s_pp, 2 * (W - 1) + j + 1)
                            v.tensor_mul(tmb[:, j * 512:(j + 1) * 512],
                                         P2[:, j * T + w * 512:
                                            j * T + (w + 1) * 512],
                                         psa[:, j * 512:(j + 1) * 512]
                                         ).then_inc(s_tm, 1)
                        for j in range(2):
                            v.wait_ge(s_expA, 4 * W + j + 1)
                            if W >= 1:
                                v.wait_ge(s_bc, 6 * (W - 1) + 4)
                            if W >= 4:
                                v.wait_ge(s_rc, 2 * (W - 4) + 2)
                            v.tensor_mul(al2[:, j * 512:(j + 1) * 512],
                                         rsO[0:2, j * T + w * 512:
                                             j * T + (w + 1) * 512],
                                         sc2[:, j * 512:(j + 1) * 512]
                                         ).then_inc(s_al, 1)
                        for j in range(2):
                            v.wait_ge(s_bc, 6 * W + j + 1)
                            if W >= 1:
                                v.wait_ge(s_ga, 6 * (W - 1) + j + 1)
                            v.tensor_mul(mub[:, (3 * j) * 512:
                                             (3 * j + 1) * 512],
                                         P2[:, j * T + w * 512:
                                            j * T + (w + 1) * 512],
                                         psU[:, j * 512:(j + 1) * 512]
                                         ).then_inc(s_mu, 1)
                        for j in range(2):
                            v.wait_ge(s_bc, 6 * W + 2 + j + 1)
                            v.wait_ge(s_ae, 2 * W + j + 1)
                            if W >= 1:
                                v.wait_ge(s_ga, 6 * (W - 1) + 2 + j + 1)
                            v.tensor_mul(mub[:, (3 * j + 1) * 512:
                                             (3 * j + 2) * 512],
                                         Apb[:, j * 512:(j + 1) * 512],
                                         psU[:, (2 + j) * 512:(3 + j) * 512]
                                         ).then_inc(s_mu, 1)
                        for j in range(2):
                            v.wait_ge(s_rc, 2 * W + j + 1)
                            v.wait_ge(s_expA, 4 * W + 2 + j + 1)
                            if W >= 1:
                                v.wait_ge(s_bc, 6 * (W - 1) + 6)
                            v.tensor_mul(be2[:, j * 512:(j + 1) * 512],
                                         rsN[0:2, j * T + w * 512:
                                             j * T + (w + 1) * 512],
                                         sc2[:, (2 + j) * 512:(3 + j) * 512]
                                         ).then_inc(s_be, 1)
                        for j in range(2):
                            v.wait_ge(s_bc, 6 * W + 4 + j + 1)
                            if W >= 1:
                                v.wait_ge(s_ga, 6 * (W - 1) + 4 + j + 1)
                            v.tensor_mul(mub[:, (3 * j + 2) * 512:
                                             (3 * j + 3) * 512],
                                         P2[:, j * T + w * 512:
                                            j * T + (w + 1) * 512],
                                         psa[:, j * 512:(j + 1) * 512]
                                         ).then_inc(s_mu, 1)

            @blk.scalar
            def _(s):
                for it in range(NIT):
                    rsO = rsAB[it % 2]
                    rsN = rsAB[(it + 1) % 2]
                    for w in range(4):
                        W = 4 * it + w
                        for j in range(2):
                            s.wait_ge(s_ap, 8 * W + 4 * (j + 1))
                            s.wait_ge(s_tm, 2 * W + j + 1)
                            if W >= 1:
                                s.wait_ge(s_mu, 6 * (W - 1) + 2 + j + 1)
                            s.copy(Apb[:, j * 512:(j + 1) * 512],
                                   psa[:, j * 512:(j + 1) * 512]
                                   ).then_inc(s_ae, 1)
                        for j in range(2):
                            s.wait_ge(s_pp, 2 * W + j + 1)
                            s.activation(sc1[:, j * 512:(j + 1) * 512],
                                         pss[:, j * 512:(j + 1) * 512], AF.Ln,
                                         bias=P["epsc"][0:2, 2:3]
                                         ).then_inc(s_lnA, 1)
                        for j in range(2):
                            s.wait_ge(s_lnA, 4 * W + j + 1)
                            if W >= 1:
                                s.wait_ge(s_al, 2 * (W - 1) + j + 1)
                            s.activation(sc2[:, j * 512:(j + 1) * 512],
                                         sc1[:, j * 512:(j + 1) * 512],
                                         AF.Exp,
                                         scale=-1.0).then_inc(s_expA, 1)
                        for j in range(2):
                            s.activation(sc1[:, (2 + j) * 512:(3 + j) * 512],
                                         rsO[0:2, j * T + w * 512:
                                             j * T + (w + 1) * 512], AF.Ln,
                                         bias=P["epsc"][0:2, 2:3]
                                         ).then_inc(s_lnA, 1)
                        for j in range(2):
                            s.wait_ge(s_lnA, 4 * W + 2 + j + 1)
                            if W >= 1:
                                s.wait_ge(s_be, 2 * (W - 1) + j + 1)
                            s.activation(sc2[:, (2 + j) * 512:(3 + j) * 512],
                                         sc1[:, (2 + j) * 512:(3 + j) * 512],
                                         AF.Exp, scale=-1.0
                                         ).then_inc(s_expA, 1)
                        for j in range(2):
                            s.wait_ge(s_rs, 2 * W + j + 1)
                            if W >= 4:
                                s.wait_ge(s_al, 2 * (W - 4) + 2)
                            s.copy(rsN[0:2, j * T + w * 512:
                                       j * T + (w + 1) * 512],
                                   pss[:, j * 512:(j + 1) * 512]
                                   ).then_inc(s_rc, 1)

            @blk.gpsimd
            def _(gp):
                for w in range(4):
                    for j in range(2):
                        gp.tensor_mul(Pt2[:, j * T + w * 512:
                                          j * T + (w + 1) * 512],
                                      P2[:, j * T + w * 512:
                                         j * T + (w + 1) * 512],
                                      D2[:, j * T + w * 512:
                                         j * T + (w + 1) * 512]
                                      ).then_inc(s_pt, 1)
                for it in range(NIT):
                    for w in range(4):
                        W = 4 * it + w
                        for j in range(2):
                            gp.wait_ge(s_mu, 6 * W + j + 1)
                            gp.tensor_add(X2[:, j * T + w * 512:
                                             j * T + (w + 1) * 512],
                                          X2[:, j * T + w * 512:
                                             j * T + (w + 1) * 512],
                                          mub[:, (3 * j) * 512:
                                              (3 * j + 1) * 512]
                                          ).then_inc(s_ga, 1)
                        for j in range(2):
                            gp.wait_ge(s_mu, 6 * W + 2 + j + 1)
                            gp.tensor_add(R2[:, j * T + w * 512:
                                             j * T + (w + 1) * 512],
                                          R2[:, j * T + w * 512:
                                             j * T + (w + 1) * 512],
                                          mub[:, (3 * j + 1) * 512:
                                              (3 * j + 2) * 512]
                                          ).then_inc(s_ga, 1)
                        for j in range(2):
                            if W >= 1:
                                gp.wait_ge(s_rs, 2 * (W - 1) + j + 1)
                            gp.wait_ge(s_ga, 6 * W + 2 + j + 1)
                            gp.tensor_mul(r2b[:, j * 512:(j + 1) * 512],
                                          R2[:, j * T + w * 512:
                                             j * T + (w + 1) * 512],
                                          R2[:, j * T + w * 512:
                                             j * T + (w + 1) * 512]
                                          ).then_inc(s_r2, 1)
                        for j in range(2):
                            gp.wait_ge(s_mu, 6 * W + 4 + j + 1)
                            gp.wait_ge(s_ga, 6 * W + 2 + j + 1)
                            gp.tensor_add(P2[:, j * T + w * 512:
                                             j * T + (w + 1) * 512],
                                          R2[:, j * T + w * 512:
                                             j * T + (w + 1) * 512],
                                          mub[:, (3 * j + 2) * 512:
                                              (3 * j + 3) * 512]
                                          ).then_inc(s_ga, 1)
                        for j in range(2):
                            gp.wait_ge(s_ga, 6 * W + 4 + j + 1)
                            gp.tensor_mul(Pt2[:, j * T + w * 512:
                                              j * T + (w + 1) * 512],
                                          P2[:, j * T + w * 512:
                                             j * T + (w + 1) * 512],
                                          D2[:, j * T + w * 512:
                                             j * T + (w + 1) * 512]
                                          ).then_inc(s_pt, 1)


def cg_emit_iter(kb, g, b, hbase, bf, it, rsO, rsN):
    nc = kb.nc
    P = kb.P
    kn4, A4 = bf["kn4"], bf["A4"]
    D2, X2, R2, P2, Pt2 = bf["D2"], bf["X2"], bf["R2"], bf["P2"], bf["Pt2"]
    Skkb, Lb2 = bf["Skkb"], bf["Lb2"]
    W4, tmb, Apb, mub, r2b = (bf["W4"], bf["tmb"], bf["Apb"], bf["mub"],
                              bf["r2b"])
    al2, be2, sc1, sc2 = bf["al2"], bf["be2"], bf["sc1"], bf["sc2"]

    with (nc.psum_tensor(f"cg_psU{g}_{it}", [128, 2048], PS) as psU,
          nc.psum_tensor(f"cg_psa{g}_{it}", [128, 1024], PS) as psa,
          nc.psum_tensor(f"cg_pss{g}_{it}", [2, 1024], PS) as pss,
          SemScope(kb, 17, "cgit") as sems):
        (s_u, s_wd, s_ap, s_tm, s_ae, s_pp, s_lnA, s_expA, s_al, s_bc,
         s_mu, s_pt, s_r2, s_rs, s_rc, s_be, s_ga) = sems

        with nc.Block() as blk:
            @blk.tensor
            def _(t):
                for w in range(4):
                    for hh in range(4):
                        j, base = hh // 2, (hh % 2) * 64
                        if w >= 1:
                            t.wait_ge(s_wd, 4 * (w - 1) + hh + 1)
                            t.wait_ge(s_mu, 6 * (w - 1) + 6)
                        for dc in range(4):
                            c = 4 * w + dc
                            t.matmul(
                                psU[:, hh * 512 + dc * 128:
                                    hh * 512 + (dc + 1) * 128],
                                bf["ktm"][:, hh * T + c * CC:
                                          hh * T + (c + 1) * CC],
                                P2[:, j * T + c * CC:j * T + (c + 1) * CC],
                                start=True, stop=True,
                                skip_group_check=True).then_inc(s_u, 1)
                    t.wait_ge(s_pt, 2)
                    for j in range(2):
                        t.wait_ge(s_wd, 4 * w + 2 * j + 2)
                        if w >= 1:
                            t.wait_ge(s_tm, 2 * (w - 1) + j + 1)
                            t.wait_ge(s_ae, 2 * (w - 1) + j + 1)
                            t.wait_ge(s_mu, 6 * (w - 1) + 4 + j + 1)
                        for dc in range(4):
                            c = 4 * w + dc
                            po = j * 512 + dc * 128
                            t.matmul(psa[:, po:po + 128],
                                     Skkb[:, j * NCK * CC + c * CC:
                                          j * NCK * CC + (c + 1) * CC],
                                     Pt2[:, j * T + c * CC:
                                         j * T + (c + 1) * CC],
                                     start=True, stop=False,
                                     skip_group_check=True)
                            t.matmul(psa[:, po:po + 128],
                                     Lb2[:, j * 128:(j + 1) * 128],
                                     P2[:, j * T + c * CC:
                                        j * T + (c + 1) * CC],
                                     start=False, stop=False,
                                     skip_group_check=True)
                            t.matmul(psa[0:64, po:po + 128],
                                     kn4[:, (2 * j * NCK + c) * DK:
                                         (2 * j * NCK + c + 1) * DK],
                                     W4[:, 2 * j * 512 + dc * 128:
                                        2 * j * 512 + (dc + 1) * 128],
                                     start=False, stop=False,
                                     skip_group_check=True)
                            t.matmul(psa[64:128, po:po + 128],
                                     kn4[:, ((2 * j + 1) * NCK + c) * DK:
                                         ((2 * j + 1) * NCK + c + 1) * DK],
                                     W4[:, (2 * j + 1) * 512 + dc * 128:
                                        (2 * j + 1) * 512 + (dc + 1) * 128],
                                     start=False, stop=True,
                                     skip_group_check=True
                                     ).then_inc(s_ap, 1)
                    for j in range(2):
                        t.wait_ge(s_tm, 2 * w + j + 1)
                        if w >= 1:
                            t.wait_ge(s_rc, 2 * (w - 1) + j + 1)
                        t.matmul(pss[:, j * 512:(j + 1) * 512],
                                 P["blk2b"][:, :],
                                 tmb[:, j * 512:(j + 1) * 512],
                                 start=True, stop=True,
                                 skip_group_check=True).then_inc(s_pp, 1)
                    for j in range(2):
                        t.wait_ge(s_al, 2 * w + j + 1)
                        t.wait_ge(s_wd, 4 * w + j + 1)
                        t.matmul(psU[:, j * 512:(j + 1) * 512],
                                 P["bo2pb"][:, :],
                                 al2[:, j * 512:(j + 1) * 512],
                                 start=True, stop=True,
                                 skip_group_check=True).then_inc(s_bc, 1)
                    for j in range(2):
                        t.wait_ge(s_wd, 4 * w + 2 + j + 1)
                        t.matmul(psU[:, (2 + j) * 512:(3 + j) * 512],
                                 P["bo2nb"][:, :],
                                 al2[:, j * 512:(j + 1) * 512],
                                 start=True, stop=True,
                                 skip_group_check=True).then_inc(s_bc, 1)
                    for j in range(2):
                        t.wait_ge(s_r2, 2 * w + j + 1)
                        t.wait_ge(s_lnA, 4 * w + j + 1)
                        t.matmul(pss[:, j * 512:(j + 1) * 512],
                                 P["blk2b"][:, :],
                                 r2b[:, j * 512:(j + 1) * 512],
                                 start=True, stop=True,
                                 skip_group_check=True).then_inc(s_rs, 1)
                    for j in range(2):
                        t.wait_ge(s_be, 2 * w + j + 1)
                        t.wait_ge(s_tm, 2 * w + j + 1)
                        t.wait_ge(s_ae, 2 * w + j + 1)
                        t.matmul(psa[:, j * 512:(j + 1) * 512],
                                 P["bo2pb"][:, :],
                                 be2[:, j * 512:(j + 1) * 512],
                                 start=True, stop=True,
                                 skip_group_check=True).then_inc(s_bc, 1)

            @blk.vector
            def _(v):
                for w in range(4):
                    for hh in range(4):
                        v.wait_ge(s_u, 16 * w + 4 * (hh + 1))
                        if w >= 1:
                            v.wait_ge(s_ap, 8 * (w - 1) + 8)
                        v.tensor_mul(W4[:, hh * 512:(hh + 1) * 512],
                                     A4[:, hh * NCK * CC + 4 * w * CC:
                                        hh * NCK * CC + 4 * (w + 1) * CC],
                                     psU[:, hh * 512:(hh + 1) * 512]
                                     ).then_inc(s_wd, 1)
                    for j in range(2):
                        v.wait_ge(s_ap, 8 * w + 4 * (j + 1))
                        if w >= 1:
                            v.wait_ge(s_pp, 2 * (w - 1) + j + 1)
                        v.tensor_mul(tmb[:, j * 512:(j + 1) * 512],
                                     P2[:, j * T + w * 512:
                                        j * T + (w + 1) * 512],
                                     psa[:, j * 512:(j + 1) * 512]
                                     ).then_inc(s_tm, 1)
                    for j in range(2):
                        v.wait_ge(s_expA, 4 * w + j + 1)
                        if w >= 1:
                            v.wait_ge(s_bc, 6 * (w - 1) + 4)
                        v.tensor_mul(al2[:, j * 512:(j + 1) * 512],
                                     rsO[0:2, j * T + w * 512:
                                         j * T + (w + 1) * 512],
                                     sc2[:, j * 512:(j + 1) * 512]
                                     ).then_inc(s_al, 1)
                    for j in range(2):
                        v.wait_ge(s_bc, 6 * w + j + 1)
                        if w >= 1:
                            v.wait_ge(s_ga, 6 * (w - 1) + j + 1)
                        v.tensor_mul(mub[:, (3 * j) * 512:(3 * j + 1) * 512],
                                     P2[:, j * T + w * 512:
                                        j * T + (w + 1) * 512],
                                     psU[:, j * 512:(j + 1) * 512]
                                     ).then_inc(s_mu, 1)
                    for j in range(2):
                        v.wait_ge(s_bc, 6 * w + 2 + j + 1)
                        v.wait_ge(s_ae, 2 * w + j + 1)
                        if w >= 1:
                            v.wait_ge(s_ga, 6 * (w - 1) + 2 + j + 1)
                        v.tensor_mul(mub[:, (3 * j + 1) * 512:
                                         (3 * j + 2) * 512],
                                     Apb[:, j * 512:(j + 1) * 512],
                                     psU[:, (2 + j) * 512:(3 + j) * 512]
                                     ).then_inc(s_mu, 1)
                    for j in range(2):
                        v.wait_ge(s_rc, 2 * w + j + 1)
                        v.wait_ge(s_expA, 4 * w + 2 + j + 1)
                        if w >= 1:
                            v.wait_ge(s_bc, 6 * (w - 1) + 6)
                        v.tensor_mul(be2[:, j * 512:(j + 1) * 512],
                                     rsN[0:2, j * T + w * 512:
                                         j * T + (w + 1) * 512],
                                     sc2[:, (2 + j) * 512:(3 + j) * 512]
                                     ).then_inc(s_be, 1)
                    for j in range(2):
                        v.wait_ge(s_bc, 6 * w + 4 + j + 1)
                        if w >= 1:
                            v.wait_ge(s_ga, 6 * (w - 1) + 4 + j + 1)
                        v.tensor_mul(mub[:, (3 * j + 2) * 512:
                                         (3 * j + 3) * 512],
                                     P2[:, j * T + w * 512:
                                        j * T + (w + 1) * 512],
                                     psa[:, j * 512:(j + 1) * 512]
                                     ).then_inc(s_mu, 1)

            @blk.scalar
            def _(s):
                for w in range(4):
                    for j in range(2):
                        s.wait_ge(s_ap, 8 * w + 4 * (j + 1))
                        s.wait_ge(s_tm, 2 * w + j + 1)
                        if w >= 1:
                            s.wait_ge(s_mu, 6 * (w - 1) + 2 + j + 1)
                        s.copy(Apb[:, j * 512:(j + 1) * 512],
                               psa[:, j * 512:(j + 1) * 512]).then_inc(s_ae, 1)
                    for j in range(2):
                        s.wait_ge(s_pp, 2 * w + j + 1)
                        s.activation(sc1[:, j * 512:(j + 1) * 512],
                                     pss[:, j * 512:(j + 1) * 512], AF.Ln,
                                     bias=P["epsc"][0:2, 2:3]
                                     ).then_inc(s_lnA, 1)
                    for j in range(2):
                        s.wait_ge(s_lnA, 4 * w + j + 1)
                        if w >= 1:
                            s.wait_ge(s_al, 2 * (w - 1) + j + 1)
                        s.activation(sc2[:, j * 512:(j + 1) * 512],
                                     sc1[:, j * 512:(j + 1) * 512], AF.Exp,
                                     scale=-1.0).then_inc(s_expA, 1)
                    for j in range(2):
                        s.activation(sc1[:, (2 + j) * 512:(3 + j) * 512],
                                     rsO[0:2, j * T + w * 512:
                                         j * T + (w + 1) * 512], AF.Ln,
                                     bias=P["epsc"][0:2, 2:3]
                                     ).then_inc(s_lnA, 1)
                    for j in range(2):
                        s.wait_ge(s_lnA, 4 * w + 2 + j + 1)
                        if w >= 1:
                            s.wait_ge(s_be, 2 * (w - 1) + j + 1)
                        s.activation(sc2[:, (2 + j) * 512:(3 + j) * 512],
                                     sc1[:, (2 + j) * 512:(3 + j) * 512],
                                     AF.Exp, scale=-1.0).then_inc(s_expA, 1)
                    for j in range(2):
                        if w >= 1:
                            s.wait_ge(s_rs, 2 * (w - 1) + j + 1)
                        s.wait_ge(s_ga, 6 * w + 2 + j + 1)
                        s.activation(r2b[:, j * 512:(j + 1) * 512],
                                     bf["R2"][:, j * T + w * 512:
                                              j * T + (w + 1) * 512],
                                     AF.Square).then_inc(s_r2, 1)
                    for j in range(2):
                        s.wait_ge(s_rs, 2 * w + j + 1)
                        s.copy(rsN[0:2, j * T + w * 512:j * T + (w + 1) * 512],
                               pss[:, j * 512:(j + 1) * 512]).then_inc(s_rc, 1)

            @blk.gpsimd
            def _(gp):
                for j in range(2):
                    gp.tensor_mul(Pt2[:, j * T:(j + 1) * T],
                                  P2[:, j * T:(j + 1) * T],
                                  D2[:, j * T:(j + 1) * T]).then_inc(s_pt, 1)
                for w in range(4):
                    for j in range(2):
                        gp.wait_ge(s_mu, 6 * w + j + 1)
                        gp.tensor_add(X2[:, j * T + w * 512:
                                         j * T + (w + 1) * 512],
                                      X2[:, j * T + w * 512:
                                         j * T + (w + 1) * 512],
                                      mub[:, (3 * j) * 512:(3 * j + 1) * 512]
                                      ).then_inc(s_ga, 1)
                    for j in range(2):
                        gp.wait_ge(s_mu, 6 * w + 2 + j + 1)
                        gp.tensor_add(R2[:, j * T + w * 512:
                                         j * T + (w + 1) * 512],
                                      R2[:, j * T + w * 512:
                                         j * T + (w + 1) * 512],
                                      mub[:, (3 * j + 1) * 512:
                                          (3 * j + 2) * 512]).then_inc(s_ga, 1)
                    for j in range(2):
                        gp.wait_ge(s_mu, 6 * w + 4 + j + 1)
                        gp.wait_ge(s_ga, 6 * w + 2 + j + 1)
                        gp.tensor_add(P2[:, j * T + w * 512:
                                         j * T + (w + 1) * 512],
                                      R2[:, j * T + w * 512:
                                         j * T + (w + 1) * 512],
                                      mub[:, (3 * j + 2) * 512:
                                          (3 * j + 3) * 512]).then_inc(s_ga, 1)


def cg_emit_out(kb, g, b, hbase, bf):
    nc = kb.nc
    P = kb.P
    d = kb.dram
    kn4, vn4, A4 = bf["kn4"], bf["vn4"], bf["A4"]
    D2, X2, Pt2 = bf["D2"], bf["X2"], bf["Pt2"]
    Skvb, W4, ob2 = bf["Skvb"], bf["W4"], bf["ob2"]
    st = kb.stream("S")

    with (nc.psum_tensor(f"cg_psO{g}", [128, 2048], PS) as psU,
          nc.psum_tensor(f"cg_pso{g}", [128, 1024], PS) as psa,
          SemScope(kb, 5, "cgo") as (s_x, s_u, s_wd, s_ap, s_ev)):
        with nc.Block() as blk:
            @blk.gpsimd
            def _(gp):
                for j in range(2):
                    gp.tensor_mul(Pt2[:, j * T:(j + 1) * T],
                                  X2[:, j * T:(j + 1) * T],
                                  D2[:, j * T:(j + 1) * T]).then_inc(s_x, 1)

            @blk.tensor
            def _(t):
                t.wait_ge(s_x, 2)
                for w in range(4):
                    for hh in range(4):
                        j, base = hh // 2, (hh % 2) * 64
                        if w >= 1:
                            t.wait_ge(s_wd, 4 * (w - 1) + hh + 1)
                        for dc in range(4):
                            c = 4 * w + dc
                            t.matmul(
                                psU[:, hh * 512 + dc * 128:
                                    hh * 512 + (dc + 1) * 128],
                                bf["ktm"][:, hh * T + c * CC:
                                          hh * T + (c + 1) * CC],
                                X2[:, j * T + c * CC:j * T + (c + 1) * CC],
                                start=True, stop=True,
                                skip_group_check=True).then_inc(s_u, 1)
                    for j in range(2):
                        t.wait_ge(s_wd, 4 * w + 2 * j + 2)
                        if w >= 1:
                            t.wait_ge(s_ev, 2 * (w - 1) + j + 1)
                        for dc in range(4):
                            c = 4 * w + dc
                            po = j * 512 + dc * 128
                            t.matmul(psa[:, po:po + 128],
                                     Skvb[:, j * NCK * CC + c * CC:
                                          j * NCK * CC + (c + 1) * CC],
                                     Pt2[:, j * T + c * CC:
                                         j * T + (c + 1) * CC],
                                     start=True, stop=False,
                                     skip_group_check=True)
                            t.matmul(psa[0:64, po:po + 128],
                                     vn4[:, (2 * j * NCK + c) * DK:
                                         (2 * j * NCK + c + 1) * DK],
                                     W4[:, 2 * j * 512 + dc * 128:
                                        2 * j * 512 + (dc + 1) * 128],
                                     start=False, stop=False,
                                     skip_group_check=True)
                            ins = t.matmul(
                                     psa[64:128, po:po + 128],
                                     vn4[:, ((2 * j + 1) * NCK + c) * DK:
                                         ((2 * j + 1) * NCK + c + 1) * DK],
                                     W4[:, (2 * j + 1) * 512 + dc * 128:
                                        (2 * j + 1) * 512 + (dc + 1) * 128],
                                     start=False, stop=True,
                                     skip_group_check=True)
                        ins.then_inc(s_ap, 1)

            @blk.vector
            def _(v):
                for w in range(4):
                    for hh in range(4):
                        v.wait_ge(s_u, 16 * w + 4 * (hh + 1))
                        if w >= 1:
                            v.wait_ge(s_ap, 2 * (w - 1) + 2)
                        v.tensor_mul(W4[:, hh * 512:(hh + 1) * 512],
                                     A4[:, hh * NCK * CC + 4 * w * CC:
                                        hh * NCK * CC + 4 * (w + 1) * CC],
                                     psU[:, hh * 512:(hh + 1) * 512]
                                     ).then_inc(s_wd, 1)
                    for j in range(2):
                        v.wait_ge(s_ap, 2 * w + j + 1)
                        v.tensor_copy(ob2[:, j * T + w * 512:
                                          j * T + (w + 1) * 512],
                                      psa[:, j * 512:(j + 1) * 512]
                                      ).then_inc(s_ev, 1)

            @blk.sync
            def _(sy):
                sy.wait_ge(s_ev, 8)
                for j in range(2):
                    hp = hbase // 2 + j
                    st.issue(sy, d["mo"][b, hp], ob2[:, j * T:(j + 1) * T])
                st.wait_all(sy)







def mo_emit(kb):
    nc = kb.nc
    P = kb.P
    d = kb.dram
    ld = kb.stream("S")

    with ExitStack() as stk:
        def mk(name, shape, dt):
            return stk.enter_context(nc.sbuf_tensor(name, shape, dt))
        mo_sb = mk("mo_m", [128, NCH * T], BF16)
        wom_sb = mk("mo_w", [128, NCH * D], BF16)
        sqv = mk("mo_sq", [128, 2 * T], BF16)
        ssq = mk("mo_ssq", [16, T], F32)
        rstd = mk("mo_rstd", [16, T], BF16)
        ybuf = mk("mo_y", [128, NCH * T], F32)
        sqf = mk("mo_sqf", [128, 2 * T], F32)
        rowb = mk("mo_row", [1, 3 * T], F32)
        zcol = mk("mo_z", [128, 1], F32)

        gld = kb.stream("G", depth=8)
        with nc.Block() as blk:
            @blk.gpsimd
            def _(g):
                for c in range(NCH):
                    gld.issue(g, wom_sb[:, c * D:(c + 1) * D], d["wom"][c])
                g.memset(zcol[:, :], 0.0)
                gld.wait_all(g)

        for b in range(B):
            mo_emit_b(kb, b, mo_sb, wom_sb, sqv, ssq, rstd, ybuf, sqf, rowb,
                   zcol, ld)


def mo_emit_b(kb, b, mo_sb, wom_sb, sqv, ssq, rstd, ybuf, sqf, rowb, zcol, ld):
    nc = kb.nc
    P = kb.P
    d = kb.dram

    with nc.Block() as blk:
        @blk.sync
        def _(sy):
            for c in range(NCH):
                ld.issue(sy, mo_sb[:, c * T:(c + 1) * T], d["mo"][b, c])
            ld.wait_all(sy)

    with (nc.psum_tensor(f"mo_ps{b}", [16, 2048], PS) as ssps,
          SemScope(kb, 2, "moA") as (s1, s2)):
        with nc.Block() as blk:
            @blk.gpsimd
            def _(gp):
                for c in range(NCH):
                    if c >= 2:
                        gp.wait_ge(s2, c - 1)
                    gp.tensor_mul(sqv[:, (c % 2) * T:(c % 2 + 1) * T],
                                  mo_sb[:, c * T:(c + 1) * T],
                                  mo_sb[:, c * T:(c + 1) * T]).then_inc(s1, 1)

            @blk.tensor
            def _(t):
                for c in range(NCH):
                    t.wait_ge(s1, c + 1)
                    for tt in range(4):
                        ins = t.matmul(ssps[:, tt * 512:(tt + 1) * 512],
                                       P["sel8b"][:, c * 16:(c + 1) * 16],
                                       sqv[:, (c % 2) * T + tt * 512:
                                           (c % 2) * T + (tt + 1) * 512],
                                       start=(c == 0), stop=(c == NCH - 1),
                                       skip_group_check=True)
                    ins.then_inc(s2, 1)
        with nc.Block() as blk:
            @blk.vector
            def _(v):
                for tt in range(4):
                    v.tensor_copy(ssq[:, tt * 512:(tt + 1) * 512],
                                  ssps[:, tt * 512:(tt + 1) * 512])
        with nc.Block() as blk:
            @blk.scalar
            def _(s):
                s.activation(ssq[:, :], ssq[:, :], AF.Ln,
                             bias=P["epsc"][0:16, 1:2], scale=1.0 / 64)
        with nc.Block() as blk:
            @blk.scalar
            def _(s):
                s.activation(rstd[:, :], ssq[:, :], AF.Exp, scale=-0.5)

    with (nc.psum_tensor(f"mo_pb{b}", [128, 2048], PS) as bps,
          SemScope(kb, 3, "moB") as (s1, s2, s3)):
        with nc.Block() as blk:
            @blk.tensor
            def _(t):
                for c in range(NCH):
                    for tt in range(4):
                        slot = c * 4 + tt
                        if slot >= 4:
                            t.wait_ge(s2, slot - 3)
                        t.matmul(bps[:, tt * 512:(tt + 1) * 512],
                                 P["selT8b"][:, c * 128:(c + 1) * 128],
                                 rstd[:, tt * 512:(tt + 1) * 512],
                                 start=True, stop=True,
                                 skip_group_check=True).then_inc(s1, 1)

            @blk.vector
            def _(v):
                for c in range(NCH):
                    for tt in range(4):
                        slot = c * 4 + tt
                        v.wait_ge(s1, slot + 1)
                        v.tensor_mul(mo_sb[:, c * T + tt * 512:
                                           c * T + (tt + 1) * 512],
                                     mo_sb[:, c * T + tt * 512:
                                           c * T + (tt + 1) * 512],
                                     bps[:, tt * 512:(tt + 1) * 512]
                                     ).then_inc(s2, 1)

            @blk.gpsimd
            def _(gp):
                for c in range(NCH):
                    gp.wait_ge(s2, c * 4 + 4)
                    gp.tensor_scalar(mo_sb[:, c * T:(c + 1) * T],
                                     mo_sb[:, c * T:(c + 1) * T],
                                     P["normw2"][:, 0:1], None,
                                     AluOpType.mult).then_inc(s3, 1)

    with (nc.psum_tensor(f"mo_pw{b}", [128, 2048], PS) as wps,
          SemScope(kb, 3, "moC") as (s1, s2, s3)):
        xld = kb.stream("S")
        x_base = xld.i
        with nc.Block() as blk:
            @blk.sync
            def _(sy):
                for c in range(NCH):
                    xld.issue(sy, ybuf[:, c * T:(c + 1) * T], d["x_in"][b, c])

            @blk.tensor
            def _(t):
                for c in range(NCH):
                    for tt in range(4):
                        slot = c * 4 + tt
                        if slot >= 4:
                            t.wait_ge(s2, slot - 3)
                        for kc in range(NCH):
                            ins = t.matmul(
                                wps[:, tt * 512:(tt + 1) * 512],
                                wom_sb[:, kc * D + c * 128:
                                       kc * D + (c + 1) * 128],
                                mo_sb[:, kc * T + tt * 512:
                                      kc * T + (tt + 1) * 512],
                                start=(kc == 0), stop=(kc == NCH - 1))
                        ins.then_inc(s1, 1)

            @blk.vector
            def _(v):
                for c in range(NCH):
                    xld.wait(v, x_base + c + 1)
                    for tt in range(4):
                        slot = c * 4 + tt
                        v.wait_ge(s1, slot + 1)
                        v.tensor_add(ybuf[:, c * T + tt * 512:
                                          c * T + (tt + 1) * 512],
                                     ybuf[:, c * T + tt * 512:
                                          c * T + (tt + 1) * 512],
                                     wps[:, tt * 512:(tt + 1) * 512]
                                     ).then_inc(s2, 1)

    postln(kb, b, f"mo{b}", ybuf, rowb, "strA", sqf,
           dst2name="x1s")

    with nc.Block() as blk:
        @blk.sync
        def _(sy):
            st = kb.stream("S")
            for c in range(NCH):
                st.issue(sy, kb.dram["strA"][b, c, :, 0:1], zcol[:, :], slow=True)
            st.wait_all(sy)


def postln(kb, b, tag, ybuf, rowb, dstname, sqf, dst_off=1,
           dst2name=None):
    nc = kb.nc
    P = kb.P
    d = kb.dram
    st = kb.stream("S")

    with (nc.psum_tensor(f"ln_p1{tag}", [1, 2048], PS) as sp1,
          nc.psum_tensor(f"ln_p2{tag}", [1, 2048], PS) as sp2,
          SemScope(kb, 2, "lnQ") as (q1, q2)):
        with nc.Block() as blk:
            @blk.gpsimd
            def _(gp):
                for c in range(NCH):
                    if c >= 2:
                        gp.wait_ge(q2, c - 1)
                    gp.tensor_mul(sqf[:, (c % 2) * T:(c % 2 + 1) * T],
                                  ybuf[:, c * T:(c + 1) * T],
                                  ybuf[:, c * T:(c + 1) * T]).then_inc(q1, 1)

            @blk.tensor
            def _(t):
                for c in range(NCH):
                    for tt in range(4):
                        t.matmul(sp1[0:1, tt * 512:(tt + 1) * 512],
                                 P["onesf"][:, 0:1],
                                 ybuf[:, c * T + tt * 512:
                                      c * T + (tt + 1) * 512],
                                 start=(c == 0), stop=(c == NCH - 1),
                                 skip_group_check=True)
                for c in range(NCH):
                    t.wait_ge(q1, c + 1)
                    for tt in range(4):
                        ins = t.matmul(sp2[0:1, tt * 512:(tt + 1) * 512],
                                       P["onesf"][:, 0:1],
                                       sqf[:, (c % 2) * T + tt * 512:
                                           (c % 2) * T + (tt + 1) * 512],
                                       start=(c == 0), stop=(c == NCH - 1),
                                       skip_group_check=True)
                    ins.then_inc(q2, 1)
        with nc.Block() as blk:
            @blk.vector
            def _(v):
                v.tensor_scalar(rowb[0:1, 0:T], sp1[0:1, :], 1.0 / D, None,
                                AluOpType.mult)
                v.tensor_scalar(rowb[0:1, T:2 * T], sp2[0:1, :], 1.0 / D,
                                None, AluOpType.mult)
        with nc.Block() as blk:
            @blk.gpsimd
            def _(gp):
                gp.tensor_mul(rowb[0:1, 2 * T:3 * T], rowb[0:1, 0:T],
                              rowb[0:1, 0:T])
        with nc.Block() as blk:
            @blk.vector
            def _(v):
                v.tensor_sub(rowb[0:1, T:2 * T], rowb[0:1, T:2 * T],
                             rowb[0:1, 2 * T:3 * T])
        with nc.Block() as blk:
            @blk.scalar
            def _(s):
                s.activation(rowb[0:1, T:2 * T], rowb[0:1, T:2 * T], AF.Ln,
                             bias=P["epsc"][0:1, 1:2])
        with nc.Block() as blk:
            @blk.scalar
            def _(s):
                s.activation(rowb[0:1, T:2 * T], rowb[0:1, T:2 * T], AF.Exp,
                             scale=-0.5)

    with (nc.psum_tensor(f"ln_pb{tag}", [128, 2048], PS) as bps,
          nc.psum_tensor(f"ln_pm{tag}", [128, 2048], PS) as mps,
          SemScope(kb, 3, "lnR") as (r1, r2, r3)):
        with nc.Block() as blk:
            @blk.tensor
            def _(t):
                for tt in range(4):
                    t.matmul(mps[:, tt * 512:(tt + 1) * 512],
                             P["onesf"][0:1, :],
                             rowb[0:1, tt * 512:(tt + 1) * 512],
                             start=True, stop=True,
                             skip_group_check=True)
                    t.matmul(bps[:, tt * 512:(tt + 1) * 512],
                             P["onesf"][0:1, :],
                             rowb[0:1, T + tt * 512:T + (tt + 1) * 512],
                             start=True, stop=True,
                             skip_group_check=True).then_inc(r1, 1)

            @blk.vector
            def _(v):
                for c in range(NCH):
                    for tt in range(4):
                        v.wait_ge(r1, tt + 1)
                        v.tensor_sub(ybuf[:, c * T + tt * 512:
                                          c * T + (tt + 1) * 512],
                                     ybuf[:, c * T + tt * 512:
                                          c * T + (tt + 1) * 512],
                                     mps[:, tt * 512:(tt + 1) * 512]
                                     ).then_inc(r3, 1)
                for c in range(NCH):
                    for tt in range(4):
                        v.wait_ge(r3, c * 4 + tt + 1)
                        ins = v.tensor_mul(
                            ybuf[:, c * T + tt * 512:c * T + (tt + 1) * 512],
                            ybuf[:, c * T + tt * 512:c * T + (tt + 1) * 512],
                            bps[:, tt * 512:(tt + 1) * 512])
                    ins.then_inc(r2, 1)

            @blk.sync
            def _(sy):
                for c in range(NCH):
                    sy.wait_ge(r2, c + 1)
                    st.issue(sy, d[dstname][b, c, :,
                                            dst_off:dst_off + T],
                             ybuf[:, c * T:(c + 1) * T])
                    if dst2name is not None:
                        st.issue(sy, d[dst2name][b, c],
                                 ybuf[:, c * T:(c + 1) * T])
                st.wait_all(sy)








def tr_emit(kb):
    nc = kb.nc
    P = kb.P
    d = kb.dram
    with ExitStack() as stk:
        def mk(name, shape, dt):
            return stk.enter_context(nc.sbuf_tensor(name, shape, dt))
        g = dict(
            qsc=mk("tr_qsc", [128, 4], F32),
            mky=mk("tr_mk", [128, 2 * 512], F32),
            mk0=mk("tr_mk0", [128, 2 * 256], F32),
            xst=mk("tr_xst", [128, NCH * 513], F32),
            xbt=mk("tr_xb", [128, NCH * 513], BF16),
            sq=mk("tr_sq", [128, 2 * TT], BF16),
            ssq=mk("tr_ssq", [16, 2 * TT], F32),
            rstd=mk("tr_rstd", [16, 2 * TT], BF16),
            psb=mk("tr_psb", [128, 4 * 512], BF16),
            yst=mk("tr_yst", [128, 2 * TT], F32),
            zc=mk("tr_zc", [128, 1], F32),
        )
        g["tkA"] = {}
        g["tkB"] = {}
        with nc.Block() as blk:
            @blk.sync
            def _(sy):
                ld = kb.stream("S")
                ld.issue(sy, g["mky"][:, 0:512], d["wmask"][0])
                ld.issue(sy, g["mky"][:, 512:1024], d["wmask"][1])
                ld.issue(sy, g["mk0"][:, 0:256], d["wmask0"][0])
                ld.issue(sy, g["mk0"][:, 256:512], d["wmask0"][1])
                ld.wait_all(sy)

            @blk.gpsimd
            def _(gp):
                gp.memset(g["zc"][:, :], 0.0)

        for l in range(NL):
            tr_emit_attn_layer(kb, l, g)
            tr_emit_ffn_layer(kb, l, g)


def tr_emit_attn_layer(kb, l, gouter, wq_sb=None, wo_sb=None):
    nc = kb.nc
    P = kb.P
    d = kb.dram
    wl_ = kb.stream("G", depth=8)
    stk = ExitStack()
    stk.__enter__()

    def mk(name, shape, dt):
        return stk.enter_context(nc.sbuf_tensor(name, shape, dt))
    g = dict(gouter)
    g.update(
        csql=mk(f"tr_csql{l}", [128, T], BF16),
        ssql=mk(f"tr_ssql{l}", [128, T], BF16),
        cikl=mk(f"tr_cikl{l}", [128, T], BF16),
        sikl=mk(f"tr_sikl{l}", [128, T], BF16),
        qp=mk(f"tr_qp{l}", [128, NCH * TT], BF16),
        kp=mk(f"tr_kp{l}", [128, NCH * TT], BF16),
        kprev=mk(f"tr_kprev{l}", [128, NCH * 256], BF16),
        vt=mk(f"tr_vt{l}", [128, H * 256], BF16),
        vtprev=mk(f"tr_vtp{l}", [128, H * 128], BF16),
        ao=mk(f"tr_ao{l}", [128, NCH * TT], BF16),
        ml=mk(f"tr_ml{l}", [128, 4 * 512], F32),
        ptb=mk(f"tr_ptb{l}", [128, 2 * 1024], BF16),
        mx=mk(f"tr_mx{l}", [128, 16], F32),
    )
    g["qraw"] = g["ao"]
    wq_sb = mk(f"tr_wqkv{l}", [128, NCH * 3 * D], BF16)
    wo_sb = mk(f"tr_wo{l}", [128, NCH * D], BF16)

    with nc.Block() as blk:
        @blk.gpsimd
        def _(gp):
            for c in range(NCH):
                wl_.issue(gp, wq_sb[:, c * 3 * D:(c + 1) * 3 * D],
                          d["wqkv_tb"][l, c])
            wl_.wait_all(gp)

        @blk.sync
        def _(sy):
            ld = kb.stream("S")
            ld.wait_all(sy)
            XB = NCH * 513
            for c in range(NCH):
                tk = ld.issue(sy, g["xst"][:, c * 513:(c + 1) * 513],
                              d["strA"][0, c, :, 0:513])
            g["tkA"][0] = tk
            for c in range(NCH):
                ld.issue(sy, wo_sb[:, c * D:(c + 1) * D], d["wo_tb"][l, c])
            ld.issue(sy, g["qsc"][:, 0:1], d["qs2"][l])
            ld.issue(sy, g["qsc"][:, 1:2], d["qs2s"][l])
            ld.issue(sy, g["qsc"][:, 2:3], d["ks2"][l])
            ld.issue(sy, g["qsc"][:, 3:4], d["ks2s"][l])
            ld.wait_all(sy)
    with nc.Block() as blk:
        @blk.gpsimd
        def _(gp):
            ld = kb.stream("G", depth=8)
            for name, base in (("csql", "csq"), ("ssql", "ssq"),
                               ("cikl", "cik"), ("sikl", "sik")):
                ld.issue(gp, g[name][:, :], d[base])
            ld.wait_all(gp)
    with nc.Block() as blk:
        @blk.vector
        def _(v):
            for col, name in enumerate(("csql", "ssql", "cikl", "sikl")):
                v.tensor_scalar(g[name][:, :], g[name][:, :],
                                g["qsc"][:, col:col + 1], None,
                                AluOpType.mult)

    with nc.Block() as blk:
        @blk.sync
        def _(sy):
            st = kb.stream("S")
            for b in range(B):
                for c in range(NCH):
                    st.issue(sy, d["strB"][b, c, :, 0:1], g["zc"][:, :], slow=True)
            st.wait_all(sy)

    for tt in range(8):
        tr_emit_attn_tile(kb, l, g, wq_sb, wo_sb, tt, tt // 4, tt % 4)
    stk.__exit__(None, None, None)


def tr_emit_attn_tile(kb, l, g, wq_sb, wo_sb, tt, b, kk):
    nc = kb.nc
    P = kb.P
    d = kb.dram
    xs = kb.stream("S")
    first = (kk == 0)

    with nc.Block() as blk:
        @blk.sync
        def _(sy):
            if tt > 0:
                for c in range(4):
                    xs.issue(sy, g["xst"][:, c * 513:(c + 1) * 513],
                             d["strA"][b, c, :, kk * TT:kk * TT + 513])
                xs.wait_all(sy)
            else:
                xs.wait(sy, g["tkA"][0])

        @blk.gpsimd
        def _(gp):
            if tt > 0:
                xg = kb.stream("G", depth=8)
                for c in range(4, NCH):
                    xg.issue(gp, g["xst"][:, c * 513:(c + 1) * 513],
                             d["strA"][b, c, :, kk * TT:kk * TT + 513])
                xg.wait_all(gp)
    with nc.Block() as blk:
        @blk.scalar
        def _(s):
            for c in range(NCH):
                s.copy(g["xbt"][:, c * 513:(c + 1) * 513],
                       g["xst"][:, c * 513:(c + 1) * 513])

    def rhs_sh(kc):
        off = 1 if kc < 4 else 0
        return g["xbt"][:, kc * 513 + off:kc * 513 + off + TT]

    with (nc.psum_tensor(f"tr_pq{l}{tt}", [128, 2048], PS) as qps,
          nc.psum_tensor(f"tr_pss{l}{tt}", [16, 1024], PS) as sps,
          nc.psum_tensor(f"tr_ptr{l}{tt}", [128, 2048],
                         mybir.dt.bfloat16) as tps,
          SemScope(kb, 7, "trQ") as (s1, s2, s3, s4, s5, s6, s7)):
        with nc.Block() as blk:
            @blk.tensor
            def _(t):
                for i, nco in enumerate(list(range(16, 24)) +
                                        list(range(0, 16))):
                    bank = i % 4
                    if i >= 4:
                        t.wait_ge(s2, i - 3)
                    for kc in range(NCH):
                        ins = t.matmul(
                            qps[:, bank * 512:(bank + 1) * 512],
                            wq_sb[:, kc * 3 * D + nco * 128:
                                  kc * 3 * D + (nco + 1) * 128],
                            rhs_sh(kc),
                            start=(kc == 0), stop=(kc == NCH - 1))
                    ins.then_inc(s1, 1)
                    if 4 <= i < 12:
                        c = i - 4
                        if i == 4:
                            t.wait_ge(s2, 1)
                        for ts4 in range(4):
                            slot = c * 4 + ts4
                            if slot >= 2:
                                t.wait_ge(s4, slot - 1)
                            t.transpose(tps[:, (slot % 2) * 1024:
                                            (slot % 2) * 1024 + 128],
                                        g["qraw"][:, c * TT + ts4 * 128:
                                                  c * TT + (ts4 + 1) * 128],
                                        P["identb"][:, :]).then_inc(s3, 1)
                for i in range(16):
                    t.wait_ge(s5, i + 1)
                    qk, c = i // 8, i % 8
                    t.matmul(sps[:, qk * 512:(qk + 1) * 512],
                             P["sel8b"][:, c * 16:(c + 1) * 16],
                             g["sq"][:, (i % 2) * TT:(i % 2 + 1) * TT],
                             start=(c == 0), stop=(c == NCH - 1),
                             skip_group_check=True).then_inc(s6, 1)

            @blk.scalar
            def _(s):
                for i in range(24):
                    bank = i % 4
                    s.wait_ge(s1, i + 1)
                    if i < 8:
                        dst = g["qraw"][:, i * TT:(i + 1) * TT]
                    elif i < 16:
                        s.wait_ge(s3, 4 * (i - 8 + 1))
                        dst = g["qp"][:, (i - 8) * TT:(i - 7) * TT]
                    else:
                        dst = g["kp"][:, (i - 16) * TT:(i - 15) * TT]
                    s.copy(dst, qps[:, bank * 512:(bank + 1) * 512]
                           ).then_inc(s2, 1)
                s.wait_ge(s7, 2)
                s.activation(g["ssq"][:, :], g["ssq"][:, :], AF.Ln
                             ).then_inc(s7, 1)
                s.wait_ge(s7, 3)
                s.activation(g["rstd"][:, :], g["ssq"][:, :], AF.Exp,
                             scale=-0.5)

            @blk.vector
            def _(v):
                for c in range(NCH):
                    for ts4 in range(4):
                        slot = c * 4 + ts4
                        v.wait_ge(s3, slot + 1)
                        for pj in range(2):
                            h = 2 * c + pj
                            ins = v.tensor_copy(
                                g["vt"][:, h * 256 + ts4 * 64:
                                        h * 256 + (ts4 + 1) * 64],
                                tps[:, (slot % 2) * 1024 + pj * 64:
                                    (slot % 2) * 1024 + (pj + 1) * 64])
                        ins.then_inc(s4, 1)
                for i in range(16):
                    v.wait_ge(s2, 8 + i + 1)
                    if i >= 2:
                        v.wait_ge(s6, i - 1)
                    src = (g["qp"][:, (i % 8) * TT:(i % 8 + 1) * TT]
                           if i < 8 else
                           g["kp"][:, (i % 8) * TT:(i % 8 + 1) * TT])
                    v.tensor_mul(g["sq"][:, (i % 2) * TT:(i % 2 + 1) * TT],
                                 src, src).then_inc(s5, 1)
                for qk in range(2):
                    v.wait_ge(s6, 16)
                    v.tensor_scalar(g["ssq"][:, qk * TT:(qk + 1) * TT],
                                    sps[:, qk * 512:(qk + 1) * 512],
                                    1e-24, None, AluOpType.max
                                    ).then_inc(s7, 1)

    for qk, srcb, dstb, ctab, stab in (
            (0, g["qp"], g["qp"], "csql", "ssql"),
            (1, g["kp"], g["kp"], "cikl", "sikl")):
        with (nc.psum_tensor(f"tr_pr{l}{tt}{qk}", [128, 2048], PS) as rps,
              SemScope(kb, 3, "trR") as (s1, s2, s3)):
            with nc.Block() as blk:
                @blk.tensor
                def _(t):
                    for c in range(NCH):
                        if c >= 2:
                            t.wait_ge(s2, 2 * (c - 2) + 1)
                        t.matmul(rps[:, (c % 2) * 512:(c % 2 + 1) * 512],
                                 P["selT8b"][:, c * 128:(c + 1) * 128],
                                 g["rstd"][:, qk * TT:(qk + 1) * TT],
                                 start=True, stop=True,
                                 skip_group_check=True).then_inc(s1, 1)
                        t.wait_ge(s2, 2 * c + 1)
                        if c >= 2:
                            t.wait_ge(s2, 2 * (c - 2) + 2)
                        t.matmul(rps[:, (2 + c % 2) * 512:(3 + c % 2) * 512],
                                 P["pswap"][:, :],
                                 srcb[:, c * TT:(c + 1) * TT],
                                 start=True, stop=True,
                                 skip_group_check=True).then_inc(s1, 1)

                @blk.vector
                def _(v):
                    for c in range(NCH):
                        v.wait_ge(s1, 2 * c + 1)
                        v.tensor_mul(srcb[:, c * TT:(c + 1) * TT],
                                     srcb[:, c * TT:(c + 1) * TT],
                                     rps[:, (c % 2) * 512:(c % 2 + 1) * 512]
                                     ).then_inc(s2, 1)
                        v.wait_ge(s1, 2 * c + 2)
                        if c >= 2:
                            v.wait_ge(s3, 2 * (c - 2) + 2)
                        v.tensor_mul(g["sq"][:, (c % 2) * TT:(c % 2 + 1) * TT],
                                     rps[:, (2 + c % 2) * 512:
                                         (3 + c % 2) * 512],
                                     g[stab][:, kk * TT:(kk + 1) * TT]
                                     ).then_inc(s2, 1)

                @blk.gpsimd
                def _(gp):
                    for c in range(NCH):
                        gp.wait_ge(s2, 2 * c + 1)
                        if c >= 2:
                            gp.wait_ge(s3, 2 * (c - 2) + 2)
                        gp.tensor_mul(g["psb"][:, (c % 2) * 512:
                                               (c % 2 + 1) * 512],
                                      srcb[:, c * TT:(c + 1) * TT],
                                      g[ctab][:, kk * TT:(kk + 1) * TT]
                                      ).then_inc(s3, 1)
                        gp.wait_ge(s2, 2 * c + 2)
                        gp.wait_ge(s3, 2 * c + 1)
                        gp.tensor_add(dstb[:, c * TT:(c + 1) * TT],
                                      g["psb"][:, (c % 2) * 512:
                                               (c % 2 + 1) * 512],
                                      g["sq"][:, (c % 2) * TT:(c % 2 + 1) * TT]
                                      ).then_inc(s3, 1)

    tr_emit_attention(kb, l, g, tt, b, kk, first)
    tr_emit_wo(kb, l, g, wo_sb, tt, b, kk)

    with nc.Block() as blk:
        @blk.gpsimd
        def _(gp):
            for c in range(NCH):
                gp.tensor_copy(g["kprev"][:, c * 256:(c + 1) * 256],
                               g["kp"][:, c * TT + 256:(c + 1) * TT])
            for h in range(H):
                gp.tensor_copy(g["vtprev"][:, h * 128:(h + 1) * 128],
                               g["vt"][:, h * 256 + 128:(h + 1) * 256])


def tr_emit_attention(kb, l, g, tt, b, kk, first):
    nc = kb.nc
    P = kb.P

    units = []
    for h in range(H):
        for wl in range(2):
            w0 = first and wl == 0
            units.append((h, wl, w0, 2 if w0 else 4))
    trc = [0]
    for u in units:
        trc.append(trc[-1] + u[3])

    with (nc.psum_tensor(f"tr_pa{l}{tt}", [128, 2048], PS) as aps,
          nc.psum_tensor(f"tr_po{l}{tt}", [128, 1024], PS) as ops,
          nc.psum_tensor(f"tr_pp{l}{tt}", [128, 2048],
                         mybir.dt.bfloat16) as pps,
          SemScope(kb, 11, "trA") as
          (s_sim, s_ml, s_mx, s_ex, s_rc, s_ps, s_tr, s_pt, s_om, s_oe,
           s_ng)):
        with nc.Block() as blk:
            @blk.tensor
            def _(t):
                for step in range(len(units) + 2):
                    if step < len(units):
                        idx = step
                        h, wl, w0, nks = units[idx]
                        c, base = h // 2, (h % 2) * 64
                        for qt in range(2):
                            bank = (2 * idx + qt) % 4
                            if 2 * idx + qt >= 4:
                                t.wait_ge(s_ml, 2 * idx + qt - 3)
                            qsl = g["qp"][base:base + 64,
                                          c * TT + wl * 256 + qt * 128:
                                          c * TT + wl * 256 + (qt + 1) * 128]
                            col = 0
                            if not w0:
                                if wl == 0:
                                    ksl = g["kprev"][base:base + 64,
                                                     c * 256:(c + 1) * 256]
                                else:
                                    ksl = g["kp"][base:base + 64,
                                                  c * TT:c * TT + 256]
                                t.matmul(aps[:, bank * 512:bank * 512 + 256],
                                         qsl, ksl, start=True, stop=True,
                                         skip_group_check=True)
                                col = 256
                            kcur = g["kp"][base:base + 64,
                                           c * TT + wl * 256:
                                           c * TT + (wl + 1) * 256]
                            t.matmul(aps[:, bank * 512 + col:
                                         bank * 512 + col + 256],
                                     qsl, kcur, start=True, stop=True,
                                     skip_group_check=True).then_inc(s_sim, 1)
                    if step >= 2:
                        idx = step - 2
                        h, wl, w0, nks = units[idx]
                        c, base = h // 2, (h % 2) * 64
                        t.wait_ge(s_ps, 2 * idx + 2)
                        for ks in range(nks):
                            pos = trc[idx] + ks
                            if pos >= 2:
                                t.wait_ge(s_pt, pos - 1)
                            for qt in range(2):
                                ins = t.transpose(
                                    pps[:, (pos % 2) * 1024 + qt * 128:
                                        (pos % 2) * 1024 + (qt + 1) * 128],
                                    g["psb"][:, ((idx % 2) * 2 + qt) * 512 +
                                             ks * 128:
                                             ((idx % 2) * 2 + qt) * 512 +
                                             (ks + 1) * 128],
                                    P["identb"][:, :])
                            ins.then_inc(s_tr, 1)
                        t.wait_ge(s_pt, trc[idx] + nks)
                        if idx >= 2:
                            t.wait_ge(s_oe, idx - 1)
                        for ks in range(nks):
                            if w0:
                                vsl = g["vt"][:, h * 256 + ks * 64:
                                              h * 256 + (ks + 1) * 64]
                            elif wl == 0:
                                if ks < 2:
                                    vsl = g["vtprev"][:, h * 128 + ks * 64:
                                                      h * 128 + (ks + 1) * 64]
                                else:
                                    vsl = g["vt"][:, h * 256 + (ks - 2) * 64:
                                                  h * 256 + (ks - 1) * 64]
                            else:
                                vsl = g["vt"][:, h * 256 + ks * 64:
                                              h * 256 + (ks + 1) * 64]
                            ins = t.matmul(ops[0:64, (idx % 2) * 512:
                                               (idx % 2) * 512 + 256],
                                           vsl,
                                           g["ptb"][:, (idx % 2) * 1024 +
                                                    ks * 256:
                                                    (idx % 2) * 1024 +
                                                    (ks + 1) * 256],
                                           start=(ks == 0),
                                           stop=(ks == nks - 1),
                                           skip_group_check=True)
                        ins.then_inc(s_om, 1)

            @blk.vector
            def _(v):
                for idx, (h, wl, w0, nks) in enumerate(units):
                    p8 = (idx % 2) * 8
                    nk = 256 if w0 else 512
                    for qt in range(2):
                        v.wait_ge(s_sim, 2 * idx + qt + 1)
                        if idx >= 2:
                            v.wait_ge(s_ex, 2 * (idx - 2) + 2)
                        msk = (g["mk0"][:, qt * 256:(qt + 1) * 256] if w0
                               else g["mky"][:, qt * 512:(qt + 1) * 512])
                        v.scalar_tensor_tensor(
                            g["ml"][:, ((idx % 2) * 2 + qt) * 512:
                                    ((idx % 2) * 2 + qt) * 512 + nk],
                            aps[:, ((2 * idx + qt) % 4) * 512:
                                ((2 * idx + qt) % 4) * 512 + nk],
                            1.0, msk, AluOpType.mult, AluOpType.add
                            ).then_inc(s_ml, 1)
                        v.wait_ge(s_ml, 2 * idx + qt + 1)
                        v.reduce_max(g["mx"][:, p8 + qt:p8 + qt + 1],
                                     g["ml"][:, ((idx % 2) * 2 + qt) * 512:
                                             ((idx % 2) * 2 + qt) * 512 + nk],
                                     axis=AX).then_inc(s_mx, 1)
                    for qt in range(2):
                        v.wait_ge(s_mx, 2 * idx + qt + 1)
                        v.tensor_scalar(g["mx"][:, p8 + 4 + qt:p8 + 5 + qt],
                                        g["mx"][:, p8 + qt:p8 + qt + 1],
                                        -1.0, None, AluOpType.mult
                                        ).then_inc(s_ng, 1)
                    for qt in range(2):
                        v.wait_ge(s_ex, 2 * idx + qt + 1)
                        v.reciprocal(g["mx"][:, p8 + 6 + qt:p8 + 7 + qt],
                                     g["mx"][:, p8 + 2 + qt:p8 + 3 + qt]
                                     ).then_inc(s_rc, 1)
                    for qt in range(2):
                        v.wait_ge(s_rc, 2 * idx + qt + 1)
                        v.tensor_scalar(
                            g["psb"][:, ((idx % 2) * 2 + qt) * 512:
                                     ((idx % 2) * 2 + qt) * 512 + nk],
                            g["psb"][:, ((idx % 2) * 2 + qt) * 512:
                                     ((idx % 2) * 2 + qt) * 512 + nk],
                            g["mx"][:, p8 + 6 + qt:p8 + 7 + qt], None,
                            AluOpType.mult).then_inc(s_ps, 1)

            @blk.scalar
            def _(s):
                for idx, (h, wl, w0, nks) in enumerate(units):
                    p8 = (idx % 2) * 8
                    nk = 256 if w0 else 512
                    c, base = h // 2, (h % 2) * 64
                    for qt in range(2):
                        s.wait_ge(s_ng, 2 * idx + qt + 1)
                        if idx >= 2:
                            s.wait_ge(s_ps, 2 * (idx - 2) + 2)
                            s.wait_ge(s_tr, trc[idx - 2] + units[idx - 2][3])
                        s.activation(g["psb"][:, ((idx % 2) * 2 + qt) * 512:
                                              ((idx % 2) * 2 + qt) * 512 + nk],
                                     g["ml"][:, ((idx % 2) * 2 + qt) * 512:
                                             ((idx % 2) * 2 + qt) * 512 + nk],
                                     AF.Exp,
                                     bias=g["mx"][:, p8 + 4 + qt:p8 + 5 + qt],
                                     accum_out=g["mx"][:, p8 + 2 + qt:
                                                       p8 + 3 + qt]
                                     ).then_inc(s_ex, 1)
                    for ks in range(nks):
                        pos = trc[idx] + ks
                        s.wait_ge(s_tr, pos + 1)
                        s.copy(g["ptb"][:, (idx % 2) * 1024 + ks * 256:
                                        (idx % 2) * 1024 + (ks + 1) * 256],
                               pps[:, (pos % 2) * 1024:(pos % 2) * 1024 + 256]
                               ).then_inc(s_pt, 1)
                    s.wait_ge(s_om, idx + 1)
                    s.copy(g["ao"][base:base + 64,
                                   c * TT + wl * 256:c * TT + (wl + 1) * 256],
                           ops[0:64, (idx % 2) * 512:(idx % 2) * 512 + 256]
                           ).then_inc(s_oe, 1)


def tr_emit_wo(kb, l, g, wo_sb, tt, b, kk):
    nc = kb.nc
    d = kb.dram
    ys = kb.stream("S")
    with (nc.psum_tensor(f"tr_pw{l}{tt}", [128, 2048], PS) as wps,
          SemScope(kb, 3, "trW") as (s1, s2, s3)):
        y_base = ys.i
        with nc.Block() as blk:
            @blk.tensor
            def _(t):
                for c in range(NCH):
                    bank = c % 4
                    if c >= 4:
                        t.wait_ge(s2, c - 3)
                    for kc in range(NCH):
                        ins = t.matmul(wps[:, bank * 512:(bank + 1) * 512],
                                       wo_sb[:, kc * D + c * 128:
                                             kc * D + (c + 1) * 128],
                                       g["ao"][:, kc * TT:(kc + 1) * TT],
                                       start=(kc == 0), stop=(kc == NCH - 1))
                    ins.then_inc(s1, 1)

            @blk.vector
            def _(v):
                for c in range(NCH):
                    v.wait_ge(s1, c + 1)
                    if c >= 2:
                        ys.wait(v, y_base + c - 1)
                    v.tensor_add(g["yst"][:, (c % 2) * TT:(c % 2 + 1) * TT],
                                 g["xst"][:, c * 513 + 1:(c + 1) * 513],
                                 wps[:, (c % 4) * 512:(c % 4 + 1) * 512]
                                 ).then_inc(s2, 1)

            @blk.sync
            def _(sy):
                for c in range(NCH):
                    sy.wait_ge(s2, c + 1)
                    ys.issue(sy, d["strB"][b, c, :, 1 + kk * TT:
                                           1 + (kk + 1) * TT],
                             g["yst"][:, (c % 2) * TT:(c % 2 + 1) * TT])
                ys.wait_all(sy)


def tr_emit_ffn_layer(kb, l, g):
    nc = kb.nc
    P = kb.P
    d = kb.dram
    wl_ = kb.stream("G", depth=8)

    with (nc.sbuf_tensor(f"tr_w1{l}", [128, NCH * 2 * DH1], BF16) as w1_sb,
          nc.sbuf_tensor(f"tr_w2{l}", [128, 3 * D], BF16) as w2_sb,
          nc.sbuf_tensor(f"tr_bb{l}", [128, 2 * NHC + 3 * NCH], F32) as bb_sb,
          nc.sbuf_tensor(f"tr_xn{l}", [128, NCH * TT], BF16) as xnb,
          nc.sbuf_tensor(f"tr_G{l}", [128, NHC * TT], BF16) as Gb,
          nc.sbuf_tensor(f"tr_row{l}", [1, 4 * TT], F32) as rowb):
        with nc.Block() as blk:
            @blk.gpsimd
            def _(gp):
                for c in range(4):
                    wl_.issue(gp, w1_sb[:, c * 2 * DH1:(c + 1) * 2 * DH1],
                              d["w1pb"][l, c])
                wl_.wait_all(gp)

            @blk.sync
            def _(sy):
                ld = kb.stream("S")
                ld.wait_all(sy)
                for c in range(NCH):
                    tk = ld.issue(sy, g["xst"][:, c * 513:(c + 1) * 513],
                                  d["strB"][0, c, :, 0:513])
                g["tkB"][0] = tk
                for c in range(4, NCH):
                    ld.issue(sy, w1_sb[:, c * 2 * DH1:(c + 1) * 2 * DH1],
                             d["w1pb"][l, c])
                ld.issue(sy, bb_sb[:, 0:2 * NHC], d["b1c"][l])
                ld.issue(sy, bb_sb[:, 2 * NHC:2 * NHC + NCH], d["lng"][l])
                ld.issue(sy, bb_sb[:, 2 * NHC + NCH:2 * NHC + 2 * NCH],
                         d["lnbv"][l])
                ld.issue(sy, bb_sb[:, 2 * NHC + 2 * NCH:2 * NHC + 3 * NCH],
                         d["b2c"][l])
                ld.wait_all(sy)

        for tt in range(8):
            tr_emit_ffn_tile(kb, l, g, w1_sb, w2_sb, bb_sb, xnb, Gb, rowb,
                          tt, tt // 4, tt % 4)


def tr_emit_ffn_tile(kb, l, g, w1_sb, w2_sb, bb_sb, xnb, Gb, rowb, tt, b, kk):
    nc = kb.nc
    P = kb.P
    d = kb.dram
    xs = kb.stream("S")
    ys = kb.stream("S")
    w2s = kb.stream("G", depth=8)

    with nc.Block() as blk:
        @blk.sync
        def _(sy):
            if tt > 0:
                for c in range(4):
                    xs.issue(sy, g["xst"][:, c * 513:(c + 1) * 513],
                             d["strB"][b, c, :, kk * TT:kk * TT + 513])
                xs.wait_all(sy)
            else:
                xs.wait(sy, g["tkB"][0])

        @blk.gpsimd
        def _(gp):
            if tt > 0:
                xg = kb.stream("G", depth=8)
                for c in range(4, NCH):
                    xg.issue(gp, g["xst"][:, c * 513:(c + 1) * 513],
                             d["strB"][b, c, :, kk * TT:kk * TT + 513])
                xg.wait_all(gp)
    with nc.Block() as blk:
        @blk.scalar
        def _(s):
            for c in range(NCH):
                off = 1 if c < 4 else 0
                s.copy(g["xbt"][:, c * TT:(c + 1) * TT],
                       g["xst"][:, c * 513 + off:c * 513 + off + TT])

    with (nc.psum_tensor(f"tr_f1{l}{tt}", [1, 512], PS) as sp1,
          nc.psum_tensor(f"tr_f2{l}{tt}", [1, 512], PS) as sp2,
          SemScope(kb, 4, "trF") as (q1, q2, f1, f2)):
        with nc.Block() as blk:
            @blk.vector
            def _(v):
                for c in range(NCH):
                    if c >= 2:
                        v.wait_ge(q2, c - 1)
                    v.tensor_mul(g["sq"][:, (c % 2) * TT:(c % 2 + 1) * TT],
                                 g["xbt"][:, c * TT:(c + 1) * TT],
                                 g["xbt"][:, c * TT:(c + 1) * TT]
                                 ).then_inc(q1, 1)

            @blk.tensor
            def _(t):
                for c in range(NCH):
                    t.matmul(sp1[0:1, :], P["onesb"][:, 0:1],
                             g["xbt"][:, c * TT:(c + 1) * TT],
                             start=(c == 0), stop=(c == NCH - 1),
                             skip_group_check=True)
                for c in range(NCH):
                    t.wait_ge(q1, c + 1)
                    t.matmul(sp2[0:1, :], P["onesb"][:, 0:1],
                             g["sq"][:, (c % 2) * TT:(c % 2 + 1) * TT],
                             start=(c == 0), stop=(c == NCH - 1),
                             skip_group_check=True).then_inc(q2, 1)
        with nc.Block() as blk:
            @blk.vector
            def _(v):
                v.tensor_scalar(rowb[0:1, 0:TT], sp1[0:1, :], 1.0 / D, None,
                                AluOpType.mult).then_inc(f1, 1)
                v.tensor_scalar(rowb[0:1, TT:2 * TT], sp2[0:1, :], 1.0 / D,
                                None, AluOpType.mult).then_inc(f1, 1)
                v.wait_ge(f1, 2)
                v.tensor_mul(rowb[0:1, 2 * TT:3 * TT], rowb[0:1, 0:TT],
                             rowb[0:1, 0:TT]).then_inc(f1, 1)
                v.wait_ge(f1, 3)
                v.tensor_sub(rowb[0:1, TT:2 * TT], rowb[0:1, TT:2 * TT],
                             rowb[0:1, 2 * TT:3 * TT]).then_inc(f1, 1)

            @blk.scalar
            def _(s):
                s.wait_ge(f1, 4)
                s.activation(rowb[0:1, TT:2 * TT], rowb[0:1, TT:2 * TT],
                             AF.Ln, bias=P["epsc"][0:1, 1:2]).then_inc(f2, 1)
                s.wait_ge(f2, 1)
                s.activation(rowb[0:1, TT:2 * TT], rowb[0:1, TT:2 * TT],
                             AF.Exp, scale=-0.5)

    with (nc.psum_tensor(f"tr_fb{l}{tt}", [128, 1024], PS) as bps,
          SemScope(kb, 2, "trN") as (r1, r2)):
        with nc.Block() as blk:
            @blk.tensor
            def _(t):
                t.matmul(bps[:, 0:512], P["onesf"][0:1, :],
                         rowb[0:1, 0:TT],
                         start=True, stop=True, skip_group_check=True)
                t.matmul(bps[:, 512:1024], P["onesf"][0:1, :],
                         rowb[0:1, TT:2 * TT],
                         start=True, stop=True,
                         skip_group_check=True).then_inc(r1, 1)

            @blk.vector
            def _(v):
                v.wait_ge(r1, 1)
                for c in range(NCH):
                    if c >= 2:
                        v.wait_ge(r1, 1 + c - 1)
                    v.tensor_sub(g["yst"][:, (c % 2) * TT:(c % 2 + 1) * TT],
                                 g["xbt"][:, c * TT:(c + 1) * TT],
                                 bps[:, 0:512]).then_inc(r2, 1)
                    v.wait_ge(r2, 2 * c + 1)
                    v.tensor_mul(g["yst"][:, (c % 2) * TT:(c % 2 + 1) * TT],
                                 g["yst"][:, (c % 2) * TT:(c % 2 + 1) * TT],
                                 bps[:, 512:1024]).then_inc(r2, 1)

            @blk.gpsimd
            def _(gp):
                for c in range(NCH):
                    gp.wait_ge(r2, 2 * c + 2)
                    gp.tensor_scalar(xnb[:, c * TT:(c + 1) * TT],
                                     g["yst"][:, (c % 2) * TT:(c % 2 + 1) * TT],
                                     bb_sb[:, 2 * NHC + c:2 * NHC + c + 1],
                                     bb_sb[:, 2 * NHC + NCH + c:
                                            2 * NHC + NCH + c + 1],
                                     AluOpType.mult, AluOpType.add
                                     ).then_inc(r1, 1)

    with (nc.psum_tensor(f"tr_fg{l}{tt}", [128, 2048], PS) as gps,
          SemScope(kb, 3, "trG") as (u1, u2, u3)):
        with nc.Block() as blk:
            @blk.tensor
            def _(t):
                for hc in range(NHC):
                    p = hc % 2
                    if hc >= 2:
                        t.wait_ge(u3, hc - 1)
                    for kc in range(NCH):
                        t.matmul(gps[:, (2 * p) * 512:(2 * p + 1) * 512],
                                 w1_sb[:, kc * 2 * DH1 + hc * 128:
                                       kc * 2 * DH1 + (hc + 1) * 128],
                                 xnb[:, kc * TT:(kc + 1) * TT],
                                 start=(kc == 0), stop=(kc == NCH - 1),
                                 skip_group_check=True)
                    for kc in range(NCH):
                        ins = t.matmul(
                            gps[:, (2 * p + 1) * 512:(2 * p + 2) * 512],
                            w1_sb[:, kc * 2 * DH1 + DH1 + hc * 128:
                                  kc * 2 * DH1 + DH1 + (hc + 1) * 128],
                            xnb[:, kc * TT:(kc + 1) * TT],
                            start=(kc == 0), stop=(kc == NCH - 1),
                            skip_group_check=True)
                    ins.then_inc(u1, 1)

            @blk.scalar
            def _(s):
                for hc in range(NHC):
                    p = hc % 2
                    s.wait_ge(u1, hc + 1)
                    s.activation(g["psb"][:, p * 512:(p + 1) * 512],
                                 gps[:, (2 * p + 1) * 512:(2 * p + 2) * 512],
                                 AF.Gelu,
                                 bias=bb_sb[:, NHC + hc:NHC + hc + 1]
                                 ).then_inc(u2, 1)

            @blk.vector
            def _(v):
                for hc in range(NHC):
                    p = hc % 2
                    v.wait_ge(u2, hc + 1)
                    v.scalar_tensor_tensor(
                        Gb[:, hc * TT:(hc + 1) * TT],
                        gps[:, (2 * p) * 512:(2 * p + 1) * 512],
                        bb_sb[:, hc:hc + 1],
                        g["psb"][:, p * 512:(p + 1) * 512],
                        AluOpType.add, AluOpType.mult).then_inc(u3, 1)

    with (nc.psum_tensor(f"tr_fw{l}{tt}", [128, 4096], PS) as wps2,
          SemScope(kb, 3, "trH") as (u1, u2, u3)):
        w2s_base = w2s.i
        with nc.Block() as blk:
            @blk.gpsimd
            def _(gp):
                for hc in range(NHC):
                    if hc >= 3:
                        gp.wait_ge(u1, hc - 2)
                    w2s.issue(gp, w2_sb[:, (hc % 3) * D:(hc % 3 + 1) * D],
                              d["w2p"][l, hc])

            @blk.tensor
            def _(t):
                for hc in range(NHC):
                    w2s.wait(t, w2s_base + hc + 1)
                    for co in range(NCH):
                        ins = t.matmul(
                            wps2[:, co * 512:(co + 1) * 512],
                            w2_sb[:, (hc % 3) * D + co * 128:
                                  (hc % 3) * D + (co + 1) * 128],
                            Gb[:, hc * TT:(hc + 1) * TT],
                            start=(hc == 0), stop=(hc == NHC - 1),
                            skip_group_check=True)
                    ins.then_inc(u1, 1)

            @blk.vector
            def _(v):
                ys2_base = ys.i
                for co in range(NCH):
                    v.wait_ge(u1, NHC)
                    if co >= 2:
                        ys.wait(v, ys2_base + co - 1)
                    v.scalar_tensor_tensor(
                        g["yst"][:, (co % 2) * TT:(co % 2 + 1) * TT],
                        wps2[:, co * 512:(co + 1) * 512],
                        bb_sb[:, 2 * NHC + 2 * NCH + co:
                               2 * NHC + 2 * NCH + co + 1],
                        g["xst"][:, co * 513 + 1:(co + 1) * 513],
                        AluOpType.add, AluOpType.add).then_inc(u2, 1)

            @blk.sync
            def _(sy):
                for co in range(NCH):
                    sy.wait_ge(u2, co + 1)
                    ys.issue(sy, d["strA"][b, co, :, 1 + kk * TT:
                                           1 + (kk + 1) * TT],
                             g["yst"][:, (co % 2) * TT:(co % 2 + 1) * TT])
                ys.wait_all(sy)





def fin_emit(kb):
    nc = kb.nc
    d = kb.dram
    ld = kb.stream("S")
    with (nc.sbuf_tensor("fin_y", [128, NCH * T], F32) as ybuf,
          nc.sbuf_tensor("fin_x1", [128, NCH * T], F32) as x1b,
          nc.sbuf_tensor("fin_sq", [128, 2 * T], F32) as sqf,
          nc.sbuf_tensor("fin_row", [1, 3 * T], F32) as rowb):
        for b in range(B):
            with nc.Block() as blk:
                @blk.sync
                def _(sy):
                    for c in range(NCH):
                        ld.issue(sy, ybuf[:, c * T:(c + 1) * T],
                                 d["strA"][b, c, :, 1:T + 1])
                        ld.issue(sy, x1b[:, c * T:(c + 1) * T], d["x1s"][b, c])
                    ld.wait_all(sy)
            with nc.Block() as blk:
                @blk.vector
                def _(v):
                    for c in range(NCH):
                        v.tensor_add(ybuf[:, c * T:(c + 1) * T],
                                     ybuf[:, c * T:(c + 1) * T],
                                     x1b[:, c * T:(c + 1) * T])
            postln(kb, b, f"fin{b}", ybuf, rowb, "out_f",
                         sqf, dst_off=0)

